# version 10
# baseline (speedup 1.0000x reference)
import contextlib
import numpy as np

import concourse.bass as bass
import concourse.mybir as mybir
from concourse.bass_utils import run_bass_kernel_spmd

F32 = mybir.dt.float32
BF16 = mybir.dt.bfloat16
AF = mybir.ActivationFunctionType
ALU = mybir.AluOpType
AX = mybir.AxisListType

D = 1024
DBG_FFN_IDENT = [False]
DBG_EO_INLINE = [False]
DBG_D1_TP_SYNC = [False]
DBG_D1_NOPING = [False]
DBG_D0_TP_SYNC = [False]
CH = 8
S_ALL = 4096
S_OWN = 2048
EPS = 1e-5
RC = 12582912.0
NEXP = 10

ENGS = ("sync", "scalar", "vector", "tensor", "gpsimd")


class SemT:
    __slots__ = ("h", "n")

    def __init__(self, h):
        self.h = h
        self.n = 0


class Ctx:
    def __init__(self, nc, es):
        self.nc = nc
        self.es = es
        self.plan = {e: [] for e in ENGS}
        self.lastwait = {}

    def sem(self, name):
        return SemT(self.es.enter_context(self.nc.semaphore(name)))

    def emit(self, eng, fn):
        self.plan[eng].append(fn)

    def wait(self, eng, sem, val=None):
        v = sem.n if val is None else val
        if v <= 0:
            return
        key = (eng, id(sem))
        if self.lastwait.get(key, -1) >= v:
            return
        self.lastwait[key] = v
        self.emit(eng, lambda E, s=sem.h, v=v: E.wait_ge(s, v))

    def dma(self, eng, out, in_, sem, transpose=False, slow=False):
        sem.n += 16

        def f(E, out=out, in_=in_, t=transpose, s=sem.h, sl=slow):
            E.dma_start(out=out, in_=in_, transpose=t,
                        allow_slow_non_contiguous=sl).then_inc(s, 16)

        self.emit(eng, f)
        return sem.n

    def op(self, eng, builder, sem=None):
        if sem is not None:
            sem.n += 1
        sh = sem.h if sem is not None else None

        def f(E, b=builder, sh=sh):
            i = b(E)
            if sh is not None:
                i.then_inc(sh, 1)

        self.emit(eng, f)
        return sem.n if sem is not None else None


class Bank:
    def __init__(self, ps):
        self.ps = ps
        self.free = None

    def acquire(self, ctx):
        if self.free is not None:
            ctx.wait("tensor", self.free[0], self.free[1])

    def release(self, sem, val):
        self.free = (sem, val)


class Buf:
    def __init__(self, ap):
        self.ap = ap
        self.free = None
        self.ready = None

    @staticmethod
    def _w(ctx, eng, deps):
        if deps is None:
            return
        if isinstance(deps, list):
            for s, v in deps:
                ctx.wait(eng, s, v)
        else:
            ctx.wait(eng, deps[0], deps[1])

    def wait_free(self, ctx, eng):
        self._w(ctx, eng, self.free)

    def wait_ready(self, ctx, eng):
        self._w(ctx, eng, self.ready)


def build_program(upto="E", taps=()):
    nc = bass.Bass("TRN2", target_bir_lowering=False)
    taps = list(taps)

    xd = nc.dram_tensor("x", [S_ALL, D], F32, kind="ExternalInput").ap()
    wqd = nc.dram_tensor("wq", [D, D], F32, kind="ExternalInput").ap()
    wkvd = nc.dram_tensor("wkv", [D, 2 * D], F32, kind="ExternalInput").ap()
    wod = nc.dram_tensor("wo", [D, D], F32, kind="ExternalInput").ap()
    w1d = nc.dram_tensor("w1", [D, 4096], F32, kind="ExternalInput").ap()
    w2d = nc.dram_tensor("w2", [4096, D], F32, kind="ExternalInput").ap()
    gwd = nc.dram_tensor("gw", [D, 8], F32, kind="ExternalInput").ap()
    W1xd = nc.dram_tensor("W1x", [NEXP, D, 1536], F32, kind="ExternalInput").ap()
    W2xd = nc.dram_tensor("W2x", [NEXP, 768, D], F32, kind="ExternalInput").ap()
    eyed = nc.dram_tensor("eye", [128, 128], F32, kind="ExternalInput").ap()
    onehd = nc.dram_tensor("oneh", [16, 1024], F32, kind="ExternalInput").ap()
    g1d = nc.dram_tensor("g1", [D], F32, kind="ExternalInput").ap()
    b1d = nc.dram_tensor("b1", [D], F32, kind="ExternalInput").ap()
    g2d = nc.dram_tensor("g2", [D], F32, kind="ExternalInput").ap()
    b2d = nc.dram_tensor("b2", [D], F32, kind="ExternalInput").ap()
    outd = nc.dram_tensor("out", [S_OWN, D], F32, kind="ExternalOutput").ap()
    vbd = nc.dram_tensor("vbd", [1, 8192], F32).ap()
    recd = nc.dram_tensor("recd", [16, 512], F32).ap()

    es = contextlib.ExitStack()
    with es:
        ctx = Ctx(nc, es)
        sb = lambda name, shape, dt: es.enter_context(
            nc.sbuf_tensor(name, shape, dt)).ap()

        dqx = ctx.sem("dqx")
        dqt = ctx.sem("dqt")
        dqt2 = ctx.sem("dqt2")
        dqw = ctx.sem("dqw")
        pe = ctx.sem("pe")
        ve = ctx.sem("ve")
        se = ctx.sem("se")
        ge = ctx.sem("ge")
        dqo = ctx.sem("dqo")

        RES = sb("RES", [128, 16 * D], F32)
        FM = sb("FM", [128, CH * S_OWN], BF16)
        g1c = sb("g1c", [128, 8], F32)
        b1c = sb("b1c", [128, 8], F32)
        g2c = sb("g2c", [128, 8], F32)
        b2c = sb("b2c", [128, 8], F32)
        b1cb = sb("b1cb", [128, 8], BF16)
        b2cb = sb("b2cb", [128, 8], BF16)
        vbqc = sb("vbqc", [128, 8], F32)
        vbw1c = sb("vbw1c", [128, 32], F32)
        vbw1s = sb("vbw1s", [128, 32], F32)
        ones1 = sb("ones1", [128, 1], BF16)
        eyeb = sb("eyeb", [128, 128], BF16)
        onehb = sb("onehb", [16, 1024], BF16)
        rechi = sb("rechi", [16, 512], BF16)
        reclo = sb("reclo", [16, 512], BF16)
        fstg2 = sb("fstg2", [128, 2048], F32)
        gwsb = sb("gwsb", [128, 64], BF16)
        kvsb = sb("kvsb", [128, 8 * 132], F32)
        kvs = sb("kvs", [128, 8 * 128], BF16)
        zmat = sb("zmat", [128, 8 * 16], BF16)
        wdsb = sb("wdsb", [128, 16 * 8], F32)
        colA = sb("colA", [128, 48], F32)
        colB = sb("colB", [128, 48], F32)
        recsb = sb("recsb", [16, 512], F32)
        scsb = sb("scsb", [128, 16], F32)
        sccol = sb("sccol", [128, 8], F32)

        WA = sb("WA", [128, 16384], BF16)
        WB = sb("WB", [128, 16384], BF16)
        SCR = sb("SCR", [128, 12288], BF16)

        wqS = WB[:, 0:8192]
        vbkvr = WB[0:1, 8192:10240]
        onesr = WB[0:1, 14336:14464]
        VBKV = WB[:, 8192:12288].bitcast(F32)
        tmpA = WB[:, 12288:13312].bitcast(F32)
        tmpB = WB[:, 13312:14336].bitcast(F32)
        woS = WB[:, 8192:16384]
        w2G = WB
        W1xp_aps = [WA[:, 0:12288], WB[:, 0:12288]]
        u_tm = WB[:, 12288:13824]
        u_tm = WB[:, 12288:13056]
        u_fm = WB[:, 13056:13824]
        silu_sb = WB[:, 13824:15360].bitcast(F32)

        x0p = [SCR[:, 0:2048].bitcast(F32), SCR[:, 2048:4096].bitcast(F32)]
        astp = [SCR[:, 4096:5120], SCR[:, 5120:6144]]
        afsp = [SCR[:, 6144:7168], SCR[:, 7168:8192]]
        phik = SCR[:, 8192:10240]
        vtm = SCR[:, 10240:12288]
        stg = [SCR[:, 0:4096].bitcast(F32), SCR[:, 4096:8192].bitcast(F32)]
        plainb = SCR[:, 8192:10240]
        phiq = SCR[:, 0:4096]
        attnf = SCR[:, 4096:8192]
        recbc = SCR[:, 8192:9216].bitcast(F32)
        ctmpA = SCR[:, 9216:10240].bitcast(F32)
        ctmpB = SCR[:, 10240:11264].bitcast(F32)
        hst = SCR[:, 11264:12288]
        g1G = SCR[:, 0:8192]
        fstg = SCR[:, 8192:12288].bitcast(F32)
        t0p = [SCR[:, 0:1024], SCR[:, 1024:2048]]
        gws = SCR[:, 2048:2112]
        W2xp_aps = [SCR[:, 0:6144], SCR[:, 6144:12288]]

        pt0 = es.enter_context(nc.psum_tensor("pt0", [128, 1536], F32)).ap()
        pt1 = es.enter_context(nc.psum_tensor("pt1", [128, 1536], F32)).ap()
        pb6 = es.enter_context(nc.psum_tensor("pb6", [128, 512], F32)).ap()
        pb7 = es.enter_context(nc.psum_tensor("pb7", [128, 512], F32)).ap()
        banks = [Bank(pt0[:, 0:512]), Bank(pt0[:, 512:1024]), Bank(pt0[:, 1024:1536]),
                 Bank(pt1[:, 0:512]), Bank(pt1[:, 512:1024]), Bank(pt1[:, 1024:1536]),
                 Bank(pb6), Bank(pb7)]
        trios = [pt0, pt1]

        def mm(out_ap, lhsT, rhs, start, stop, inc=False):
            return ctx.op("tensor",
                          lambda E, o=out_ap, l=lhsT, r=rhs, st=start, sp=stop:
                          E.matmul(o, l, r, start=st, stop=sp),
                          pe if inc else None)

        def ts1(eng, out, in0, s1, op0, sem=None):
            return ctx.op(eng, lambda E, o=out, i=in0, a=s1, p=op0:
                          E.tensor_single_scalar(o, i, a, p), sem)

        def ts2(eng, out, in0, s1, s2, op0, op1, sem=None):
            return ctx.op(eng, lambda E, o=out, i=in0, a=s1, b=s2, p=op0, q=op1:
                          E.tensor_scalar(o, i, a, b, p, q), sem)

        def tt(eng, out, in0, in1, op, sem=None):
            return ctx.op(eng, lambda E, o=out, a=in0, b=in1, p=op:
                          E.tensor_tensor(o, a, b, p), sem)

        def stt(eng, out, in0, scal, in1, op0, op1, sem=None):
            return ctx.op(eng, lambda E, o=out, a=in0, s=scal, b=in1, p=op0, q=op1:
                          E.scalar_tensor_tensor(o, a, s, b, p, q), sem)

        def act(out, in_, func, bias=0.0, scale=1.0, sem=None):
            return ctx.op("scalar", lambda E, o=out, i=in_, f=func, b=bias, s=scale:
                          E.activation(o, i, f, bias=b, scale=s), sem)

        def vcopy(out, in_, sem=None):
            return ctx.op("vector", lambda E, o=out, i=in_: E.tensor_copy(o, i), sem)

        def vred(out, in_, op, sem=None):
            return ctx.op("vector", lambda E, o=out, i=in_, p=op:
                          E.tensor_reduce(o, i, AX.X, p), sem)

        def tp_split(pairs, split=False):
            for i, (dst, srcp) in enumerate(pairs):
                if split and i % 2 == 1:
                    ctx.dma("scalar", dst, srcp, dqt2, transpose=True)
                else:
                    ctx.dma("sync", dst, srcp, dqt, transpose=True)
            return [(dqt, dqt.n), (dqt2, dqt2.n)]

        for dst, src in ((g1c, g1d), (b1c, b1d), (g2c, g2d), (b2c, b2d)):
            ctx.dma("scalar", dst[:, :], src.rearrange("(c p) -> p c", p=128), dqx, slow=True)
        ctx.wait("vector", dqx, dqx.n)
        vcopy(b1cb[:, :], b1c[:, :])
        vcopy(b2cb[:, :], b2c[:, :])
        ctx.op("vector", lambda E: E.memset(ones1[:, :], 1.0))
        ctx.op("vector", lambda E: E.memset(onesr[:, :], 1.0))
        ctx.op("vector", lambda E: E.memset(kvsb[:, :], 0.0))
        ctx.op("vector", lambda E: E.memset(kvs[:, :], 0.0))
        ctx.op("vector", lambda E: E.memset(zmat[:, :], 0.0), ve)

        ctx.dma("gpsimd", eyeb[:, :], eyed[:, :], dqw)
        ctx.dma("gpsimd", onehb[:, :], onehd[:, :], dqw)
        stgb = [Buf(stg[0]), Buf(stg[1])]
        vb_banks_q = [banks[0], banks[1]]
        vb_banks_kv = [banks[2], banks[3], banks[4], banks[5]]

        def prep_fold(wsrc, ncols, dst_ap, gcol, vb_banks, bvec):
            for c in range(CH):
                s = stgb[c % 2]
                s.wait_free(ctx, "gpsimd")
                ctx.dma("gpsimd", s.ap[:, 0:ncols], wsrc[c * 128:(c + 1) * 128, :], dqw)
                ctx.wait("vector", dqw, dqw.n)
                ctx.wait("vector", pe)
                ts1("vector", dst_ap[:, c * ncols:(c + 1) * ncols], s.ap[:, 0:ncols],
                    gcol[:, c:c + 1], ALU.mult)
                vcopy(plainb[:, 0:ncols], s.ap[:, 0:ncols], ve)
                s.free = (ve, ve.n)
                ctx.wait("tensor", ve, ve.n)
                for g in range(ncols // 512):
                    b = vb_banks[g]
                    if c == 0:
                        b.acquire(ctx)
                    mm(b.ps[0:1, :], bvec[:, c:c + 1], plainb[:, g * 512:(g + 1) * 512],
                       start=(c == 0), stop=(c == CH - 1), inc=(c == CH - 1))

        prep_fold(wqd, D, wqS, g1c, vb_banks_q, b1cb)
        prep_fold(wkvd, 2 * D, WA, g1c, vb_banks_kv, b1cb)

        ctx.wait("vector", pe)
        for i, b in enumerate(vb_banks_q + vb_banks_kv):
            if i > 0:
                ctx.wait("vector", dqt)
            vcopy(tmpA[0:1, 0:512], b.ps[0:1, :], ve)
            b.release(ve, ve.n)
            ctx.wait("sync", ve, ve.n)
            ctx.dma("sync", vbd[0:1, i * 512:(i + 1) * 512], tmpA[0:1, 0:512], dqt)
        vb_evac = ve.n
        ctx.wait("scalar", dqt, dqt.n)
        ctx.dma("scalar", vbqc[:, :],
                vbd[0:1, 0:1024].rearrange("a (c p) -> (a p) c", p=128), dqx, slow=True)
        ctx.wait("gpsimd", dqt, dqt.n)
        ctx.dma("gpsimd", vbkvr[:, :], vbd[0:1, 1024:3072], dqw)
        prep_loads = dqx.n
        prep_loads_w = dqw.n
        prep_pe = pe.n
        prep_ve = ve.n

        NTC = 2
        ABLK = 16
        x0b = [Buf(x0p[0]), Buf(x0p[1]),
               Buf(fstg2[:, 0:1024]), Buf(fstg2[:, 1024:2048])]
        astb = [Buf(astp[0]), Buf(astp[1])]
        afsb = [Buf(afsp[0]), Buf(afsp[1])]
        for bb in x0b + astb + afsb:
            bb.free = (pe, prep_pe)
        kv_banks = [banks[2], banks[3], banks[4], banks[5]]
        pj_banks = [banks[0], banks[1]]
        first_scr = [True]

        for blk in range(ABLK):
            own = blk < 8
            mvc = colA if blk % 2 == 0 else colB
            for tc in range(NTC):
                jj = blk * NTC + tc
                xb = x0b[jj % 4]
                xb.wait_free(ctx, "scalar")
                if first_scr[0]:
                    ctx.wait("scalar", ve, prep_ve)
                ctx.dma("scalar", xb.ap[:, :], xd[jj * 128:(jj + 1) * 128, :], dqx)
                xb.ready = (dqx, dqx.n)
                xb.wait_ready(ctx, "vector")
                if blk >= 2:
                    ctx.wait("vector", se)
                bnc = 4 + tc * 12
                ctx.op("vector", lambda E, x=xb.ap, o=mvc, b=bnc:
                       E.bn_stats(o[:, b:b + 6], x[:, 0:512]))
                ctx.op("vector", lambda E, x=xb.ap, o=mvc, b=bnc:
                       E.bn_stats(o[:, b + 6:b + 12], x[:, 512:1024]), ve)
                ctx.wait("vector", ve)
                ctx.op("vector", lambda E, o=mvc, tc=tc, b=bnc:
                       E.bn_aggr(o[:, tc * 2:tc * 2 + 2], o[:, b:b + 12]), ve)
                ctx.wait("vector", ve)
            m_ = mvc[:, 0:2 * NTC:2]
            v_ = mvc[:, 1:2 * NTC:2]
            t4_ = mvc[:, 28:28 + NTC]
            t1_ = mvc[:, 28 + NTC:28 + 2 * NTC]
            s01_ = mvc[:, 32:32 + NTC]
            rs1_ = mvc[:, 32 + NTC:32 + 2 * NTC]
            nms_ = mvc[:, 36:36 + NTC]
            nm1_ = mvc[:, 38:38 + NTC]
            ts2("vector", t4_, v_, 1.0 + EPS, EPS * EPS, ALU.mult, ALU.add)
            ts1("vector", t1_, v_, EPS, ALU.add, sem=ve)
            ctx.wait("scalar", ve)
            act(mvc[:, 28:28 + 2 * NTC], mvc[:, 28:28 + 2 * NTC], AF.Ln, sem=se)
            ctx.wait("scalar", se)
            act(mvc[:, 32:32 + 2 * NTC], mvc[:, 28:28 + 2 * NTC], AF.Exp,
                scale=-0.5, sem=se)
            ctx.wait("vector", se)
            stt("vector", nms_, m_, -1.0, s01_, ALU.mult, ALU.mult, sem=ve)
            if own:
                stt("vector", nm1_, m_, -1.0, rs1_, ALU.mult, ALU.mult, sem=ve)
            cols_ve = ve.n

            for tc in range(NTC):
                jj = blk * NTC + tc
                xb = x0b[jj % 4]
                ab = astb[jj % 2]
                ab.wait_free(ctx, "scalar")
                ctx.wait("scalar", ve, cols_ve)
                act(ab.ap[:, :], xb.ap[:, :], AF.Identity,
                    bias=nms_[:, tc:tc + 1], scale=s01_[:, tc:tc + 1], sem=se)
                ab.ready = (se, se.n)
                if own:
                    act(RES[:, jj * D:(jj + 1) * D], xb.ap[:, :], AF.Identity,
                        bias=nm1_[:, tc:tc + 1], scale=rs1_[:, tc:tc + 1], sem=se)
                xb.free = (se, se.n)
                if own:
                    def fmdst(c, jj=jj):
                        return FM[:, c * S_OWN + jj * 128:c * S_OWN + (jj + 1) * 128]
                    dst = fmdst
                else:
                    fs = afsb[jj % 2]
                    fs.wait_free(ctx, "sync")

                    def fmdst(c, fs=fs):
                        return fs.ap[:, c * 128:(c + 1) * 128]
                    dst = fmdst
                tb = banks[6 + jj % 2]
                tb.acquire(ctx)
                ab.wait_ready(ctx, "tensor")
                tb_bf = tb.ps.bitcast(BF16)
                for c in range(CH):
                    ctx.op("tensor",
                           lambda E, o=tb_bf[:, c * 128:(c + 1) * 128],
                           i=ab.ap[:, c * 128:(c + 1) * 128]:
                           E.transpose(o, i, eyeb[:, :]), pe if c == CH - 1 else None)
                tpv = pe.n
                ab.free = (pe, tpv)
                ctx.wait("vector", pe, tpv)
                if own:
                    dview = FM.rearrange("p (c t) -> p c t", c=CH)[:, :, jj * 128:(jj + 1) * 128]
                else:
                    dview = afsb[jj % 2].ap.rearrange("p (c t) -> p c t", c=CH)
                vcopy(dview, tb_bf[:, 0:1024].rearrange("p (c t) -> p c t", c=CH), ve)
                tb.release(ve, ve.n)
                tdep = (ve, ve.n)
                if not own:
                    afsb[jj % 2].ready = tdep

                ctx.wait("tensor", tdep[0], tdep[1])
                if first_scr[0]:
                    ctx.wait("tensor", dqx, prep_loads)
                    first_scr[0] = False
                for grp in range(4):
                    b = pj_banks[(tc * 4 + grp) % 2]
                    b.acquire(ctx)
                    if jj == 0 and grp == 0:
                        ctx.wait("tensor", dqw, prep_loads_w)
                    for c in range(CH):
                        mm(b.ps[:, :], dst(c),
                           WA[:, c * 2048 + grp * 512:c * 2048 + (grp + 1) * 512],
                           start=(c == 0), stop=False)
                    mm(b.ps[:, :], onesr[:, :],
                       vbkvr[0:1, grp * 512:(grp + 1) * 512],
                       start=False, stop=True, inc=True)
                    mmv = pe.n
                    ctx.wait("vector", pe, mmv)
                    if grp < 2:
                        ts1("vector", tmpA[:, :], b.ps[:, :], 0.0, ALU.min, sem=ve)
                        ctx.wait("scalar", ve)
                        ctx.wait("scalar", pe, mmv)
                        act(tmpB[:, :], b.ps[:, :], AF.Relu, sem=se)
                        act(tmpA[:, :], tmpA[:, :], AF.Exp, sem=se)
                        ctx.wait("vector", se)
                        tt("vector",
                           phik[:, tc * 1024 + grp * 512:tc * 1024 + (grp + 1) * 512],
                           tmpA[:, :], tmpB[:, :], ALU.add, sem=ve)
                    else:
                        g2_ = grp - 2
                        vcopy(vtm[:, tc * 1024 + g2_ * 512:tc * 1024 + (g2_ + 1) * 512],
                              b.ps[:, :], ve)
                    b.release(ve, ve.n)
            phiv_done = ve.n

            ctx.wait("tensor", ve, phiv_done)
            for wave in range(2):
                for pi in range(4):
                    pair = wave * 4 + pi
                    b = kv_banks[pi]
                    b.acquire(ctx)
                    for tc in range(NTC):
                        mm(b.ps[:, 0:128],
                           phik[:, tc * 1024 + pair * 128:tc * 1024 + (pair + 1) * 128],
                           vtm[:, tc * 1024 + pair * 128:tc * 1024 + (pair + 1) * 128],
                           start=(tc == 0), stop=False)
                        mm(b.ps[:, 128:129],
                           phik[:, tc * 1024 + pair * 128:tc * 1024 + (pair + 1) * 128],
                           ones1[:, :], start=False, stop=(tc == NTC - 1),
                           inc=(tc == NTC - 1))
                    ctx.wait("vector", pe, pe.n)
                    tt("vector", kvsb[:, pair * 132:pair * 132 + 129],
                       kvsb[:, pair * 132:pair * 132 + 129], b.ps[:, 0:129],
                       ALU.add, sem=ve)
                    b.release(ve, ve.n)
        phaseA_pe = pe.n
        phaseA_ve = ve.n
        phaseA_se = se.n
        phaseA_tp = dqt.n

        ctx.wait("vector", ve, phaseA_ve)
        for pair in range(8):
            vcopy(kvs[0:64, pair * 128:pair * 128 + 64],
                  kvsb[0:64, pair * 132:pair * 132 + 64])
            vcopy(kvs[64:128, pair * 128 + 64:pair * 128 + 128],
                  kvsb[64:128, pair * 132 + 64:pair * 132 + 128])
            vcopy(zmat[0:64, pair * 16 + 2 * pair:pair * 16 + 2 * pair + 1],
                  kvsb[0:64, pair * 132 + 128:pair * 132 + 129])
            vcopy(zmat[64:128, pair * 16 + 2 * pair + 1:pair * 16 + 2 * pair + 2],
                  kvsb[64:128, pair * 132 + 128:pair * 132 + 129],
                  ve if pair == 7 else None)
        kv_final = ve.n

        if upto == "A":
            _finish(ctx, locals())
            _replay(ctx, nc)
            return nc

        ctx.wait("gpsimd", ve, phaseA_ve)
        ctx.wait("gpsimd", se, phaseA_se)
        ctx.dma("gpsimd", woS.rearrange("p (c n) -> p c n", c=CH),
                wod.rearrange("(c p) n -> p c n", p=128), dqw)
        wo_ld = dqw.n

        recb = Buf(recbc)
        hsb = Buf(hst)
        recb.hs_se = 0
        hsb.hs_se = 0
        pend_hs = [None]
        c_first = [True]
        c_state = {}

        def stage_q(cb):
            tok0 = cb * 512
            for m in range(CH):
                b = pj_banks[m % 2]
                b.acquire(ctx)
                for c in range(CH):
                    mm(b.ps[:, :], wqS[:, c * 1024 + m * 128:c * 1024 + (m + 1) * 128],
                       FM[:, c * S_OWN + tok0:c * S_OWN + tok0 + 512],
                       start=(c == 0), stop=(c == CH - 1), inc=(c == CH - 1))
                mmv = pe.n
                ctx.wait("vector", pe, mmv)
                if c_first[0]:
                    ctx.wait("vector", pe, phaseA_pe)
                    ctx.wait("scalar", pe, phaseA_pe)
                    ctx.wait("vector", dqt, phaseA_tp)
                    c_first[0] = False
                ts2("vector", ctmpA[:, :], b.ps[:, :], vbqc[:, m:m + 1], 0.0,
                    ALU.add, ALU.min, sem=ve)
                ctx.wait("scalar", ve)
                ctx.wait("scalar", pe, mmv)
                act(ctmpB[:, :], b.ps[:, :], AF.Relu, bias=vbqc[:, m:m + 1], sem=se)
                act(ctmpA[:, :], ctmpA[:, :], AF.Exp, sem=se)
                ctx.wait("vector", se)
                tt("vector", phiq[:, m * 512:(m + 1) * 512], ctmpA[:, :], ctmpB[:, :],
                   ALU.add, sem=ve)
                b.release(ve, ve.n)
                bden = banks[6]
                if m == 0:
                    bden.acquire(ctx)
                    ctx.wait("tensor", ve, kv_final)
                ctx.wait("tensor", ve, ve.n)
                mm(bden.ps[0:16, :], zmat[:, m * 16:(m + 1) * 16],
                   phiq[:, m * 512:(m + 1) * 512],
                   start=(m == 0), stop=(m == CH - 1), inc=(m == CH - 1))
            phiq_done = ve.n
            c_state[cb] = phiq_done

        def stage_attn(cb):
            phiq_done = c_state[cb]
            bden = banks[6]

            denv = pe.n
            num_banks = [banks[7], banks[0], banks[1], banks[2], banks[3],
                         banks[4], banks[5], banks[7]]
            for pair in range(7):
                bnum = num_banks[pair]
                bnum.acquire(ctx)
                mm(bnum.ps[:, :], kvs[:, pair * 128:(pair + 1) * 128],
                   phiq[:, pair * 512:(pair + 1) * 512], start=True, stop=True,
                   inc=(pair == 6))
            numv = pe.n
            ctx.wait("vector", pe, denv)
            ts1("vector", recsb[:, :], bden.ps[0:16, :], 1e-6, ALU.add, sem=ve)
            ctx.wait("vector", ve)
            recb.wait_free(ctx, "vector")
            ctx.op("vector", lambda E: E.reciprocal(recbc[0:16, :], recsb[:, :]), ve)
            ctx.wait("vector", ve)
            vcopy(rechi[:, :], recbc[0:16, :])
            tt("vector", reclo[:, :], recbc[0:16, :], rechi[:, :], ALU.subtract, sem=ve)
            recv = ve.n
            bden.release(ve, recv)
            ctx.wait("tensor", ve, recv)
            for pair in range(8):
                brc = bden
                brc.acquire(ctx)
                mm(brc.ps[:, :], onehb[:, pair * 128:(pair + 1) * 128], rechi[:, :],
                   start=True, stop=False)
                mm(brc.ps[:, :], onehb[:, pair * 128:(pair + 1) * 128], reclo[:, :],
                   start=False, stop=True, inc=True)
                rbv = pe.n
                bnum = num_banks[pair]
                if pair == 7:
                    bnum.acquire(ctx)
                    mm(bnum.ps[:, :], kvs[:, pair * 128:(pair + 1) * 128],
                       phiq[:, pair * 512:(pair + 1) * 512], start=True, stop=True,
                       inc=True)
                    ctx.wait("vector", pe, pe.n)
                else:
                    ctx.wait("vector", pe, rbv)
                recb.wait_free(ctx, "vector")
                vcopy(recbc[:, :], brc.ps[:, :], ve)
                brc.release(ve, ve.n)
                tt("vector", attnf[:, pair * 512:(pair + 1) * 512],
                   bnum.ps[:, :], recbc[:, :], ALU.mult, sem=ve)
                bnum.release(ve, ve.n)
                recb.free = (ve, ve.n)
            attn_done = ve.n
            c_state[cb] = attn_done

        def stage_wo(cb):
            attn_done = c_state[cb]
            ctx.wait("tensor", dqw, wo_ld)
            ctx.wait("tensor", ve, attn_done)
            for tc in range(4):
                jj = cb * 4 + tc
                if pend_hs[0] is not None:
                    pj, pbuf = pend_hs[0]
                    tbk = banks[2 + pj % 2]
                    tbk.acquire(ctx)
                    ctx.wait("tensor", se, pbuf.hs_se)
                    tb_bf = tbk.ps.bitcast(BF16)
                    pap = pbuf.ap.bitcast(BF16) if pbuf is recb else pbuf.ap
                    for c in range(CH):
                        ctx.op("tensor",
                               lambda E, o=tb_bf[:, c * 128:(c + 1) * 128],
                               i=pap[:, c * 128:(c + 1) * 128]:
                               E.transpose(o, i, eyeb[:, :]), pe if c == CH - 1 else None)
                    tpv = pe.n
                    pbuf.free = (pe, tpv)
                    ctx.wait("vector", pe, tpv)
                    dview = FM.rearrange("p (c t) -> p c t", c=CH)[:, :, pj * 128:(pj + 1) * 128]
                    vcopy(dview, tb_bf[:, 0:1024].rearrange("p (c t) -> p c t", c=CH), ve)
                    tbk.release(ve, ve.n)
                    pend_hs[0] = None
                for grp in range(2):
                    b = pj_banks[grp]
                    b.acquire(ctx)
                    for c in range(CH):
                        mm(b.ps[:, :],
                           attnf[:, c * 512 + tc * 128:c * 512 + (tc + 1) * 128],
                           woS[:, c * 1024 + grp * 512:c * 1024 + (grp + 1) * 512],
                           start=(c == 0), stop=(c == CH - 1), inc=(c == CH - 1))
                    mmv = pe.n
                    ctx.wait("vector", pe, mmv)
                    rsl = RES[:, jj * D + grp * 512:jj * D + (grp + 1) * 512]
                    tt("vector", rsl, b.ps[:, :], rsl, ALU.add, sem=ve)
                    b.release(ve, ve.n)
                mvc = colA if jj % 2 == 0 else colB
                rtile = RES[:, jj * D:(jj + 1) * D]
                ctx.wait("vector", ve)
                ctx.wait("vector", se)
                ctx.op("vector", lambda E, o=mvc, r=rtile: E.bn_stats(o[:, 4:10], r[:, 0:512]))
                ctx.op("vector", lambda E, o=mvc, r=rtile: E.bn_stats(o[:, 10:16], r[:, 512:1024]), ve)
                ctx.wait("vector", ve)
                ctx.op("vector", lambda E, o=mvc: E.bn_aggr(o[:, 0:2], o[:, 4:16]), ve)
                ctx.wait("vector", ve)
                ts1("vector", mvc[:, 2:3], mvc[:, 1:2], EPS, ALU.add, sem=ve)
                ctx.wait("scalar", ve)
                act(mvc[:, 2:3], mvc[:, 2:3], AF.Ln, sem=se)
                ctx.wait("scalar", se)
                act(mvc[:, 3:4], mvc[:, 2:3], AF.Exp, scale=-0.5, sem=se)
                ctx.wait("vector", se)
                stt("vector", mvc[:, 2:3], mvc[:, 0:1], -1.0, mvc[:, 3:4],
                    ALU.mult, ALU.mult, sem=ve)
                hbuf = hsb if jj % 2 == 0 else recb
                hbuf.wait_free(ctx, "scalar")
                ctx.wait("scalar", ve)
                hap = hbuf.ap.bitcast(BF16) if hbuf is recb else hbuf.ap
                act(hap[:, :], rtile, AF.Identity, bias=mvc[:, 2:3],
                    scale=mvc[:, 3:4], sem=se)
                hbuf.hs_se = se.n
                pend_hs[0] = (jj, hbuf)
            pj, pbuf = pend_hs[0]
            tbk = banks[2 + pj % 2]
            tbk.acquire(ctx)
            ctx.wait("tensor", se, pbuf.hs_se)
            tb_bf = tbk.ps.bitcast(BF16)
            pap = pbuf.ap.bitcast(BF16) if pbuf is recb else pbuf.ap
            for c in range(CH):
                ctx.op("tensor",
                       lambda E, o=tb_bf[:, c * 128:(c + 1) * 128],
                       i=pap[:, c * 128:(c + 1) * 128]:
                       E.transpose(o, i, eyeb[:, :]), pe if c == CH - 1 else None)
            tpv = pe.n
            pbuf.free = (pe, tpv)
            ctx.wait("vector", pe, tpv)
            dview = FM.rearrange("p (c t) -> p c t", c=CH)[:, :, pj * 128:(pj + 1) * 128]
            vcopy(dview, tb_bf[:, 0:1024].rearrange("p (c t) -> p c t", c=CH), ve)
            tbk.release(ve, ve.n)
            pend_hs[0] = None

        stage_q(0)
        stage_attn(0)
        for cb in range(4):
            if cb < 3:
                stage_q(cb + 1)
            stage_wo(cb)
            if cb < 3:
                stage_attn(cb + 1)

        hs_tp = ve.n
        hs_tp2 = 0
        phaseC_pe = pe.n
        phaseC_ve = ve.n
        phaseC_se = se.n

        if upto == "C":
            _finish(ctx, locals())
            _replay(ctx, nc)
            return nc

        fstgb = Buf(fstg)
        fstgb.free = (ve, phaseC_ve)
        vb1_banks = [banks[4], banks[5], banks[6], banks[7]]
        f_first = [True]

        def fold_w1G(G):
            ctx.wait("gpsimd", se, se.n)
            ctx.wait("gpsimd", ve, ve.n)
            for c in range(CH):
                fstgb.wait_free(ctx, "gpsimd")
                if f_first[0]:
                    ctx.wait("gpsimd", se, phaseC_se)
                    ctx.wait("gpsimd", ve, hs_tp)
                ctx.dma("gpsimd", fstg[:, :],
                        w1d[c * 128:(c + 1) * 128, G * 2048:(G + 1) * 2048], dqw)
                ctx.wait("vector", dqw, dqw.n)
                if f_first[0]:
                    ctx.wait("vector", pe, phaseC_pe)
                    ctx.wait("vector", ve, hs_tp)
                    f_first[0] = False
                ctx.wait("vector", pe)
                ts1("vector", WA[:, c * 2048:(c + 1) * 2048], fstg[:, :],
                    g2c[:, c:c + 1], ALU.mult)
                vcopy(g1G[:, 0:2048], fstg[:, :], ve)
                fstgb.free = (ve, ve.n)
                ctx.wait("tensor", ve, ve.n)
                for g in range(4):
                    b = vb1_banks[g]
                    if c == 0:
                        b.acquire(ctx)
                    mm(b.ps[0:1, :], b2cb[:, c:c + 1], g1G[:, g * 512:(g + 1) * 512],
                       start=(c == 0), stop=(c == CH - 1), inc=(c == CH - 1))
            ctx.wait("vector", pe)
            for g in range(4):
                if g > 0:
                    ctx.wait("vector", dqt)
                vcopy(recsb[0:1, 0:512], vb1_banks[g].ps[0:1, :], ve)
                vb1_banks[g].release(ve, ve.n)
                ctx.wait("sync", ve, ve.n)
                ctx.dma("sync",
                        vbd[0:1, 3072 + G * 2048 + g * 512:3072 + G * 2048 + (g + 1) * 512],
                        recsb[0:1, 0:512], dqt)
            ctx.wait("scalar", dqt, dqt.n)
            ctx.dma("scalar", vbw1c[:, G * 16:(G + 1) * 16],
                    vbd[0:1, 3072 + G * 2048:3072 + (G + 1) * 2048]
                    .rearrange("a (c p) -> (a p) c", p=128), dqx, slow=True)
            ctx.wait("vector", dqx, dqx.n)
            ts1("vector", vbw1s[:, G * 16:(G + 1) * 16], vbw1c[:, G * 16:(G + 1) * 16],
                10.0, ALU.mult, sem=ve)
            return ve.n

        ftmp = [fstg[:, 0:512], fstg[:, 512:1024]]

        x0db = [Buf(fstg2[:, 0:1024]), Buf(fstg2[:, 1024:2048])]
        for bb in x0db:
            bb.free = [(se, se.n), (ve, ve.n)]
        gw_ld = [0]
        d0_pend = []

        def d0_tile(jj):
            xb = x0db[jj % 2]
            xb.wait_free(ctx, "scalar")
            ctx.wait("scalar", ve, ve.n)
            ctx.dma("scalar", xb.ap[:, :], xd[jj * 128:(jj + 1) * 128, :], dqx)
            xv = dqx.n
            ctx.wait("vector", dqx, xv)
            rtile = RES[:, jj * D:(jj + 1) * D]
            tt("vector", rtile, rtile, xb.ap[:, :], ALU.add, sem=ve)
            mvc = colA if jj % 2 == 0 else colB
            ctx.wait("vector", se)
            ctx.op("vector", lambda E, o=mvc, r=rtile: E.bn_stats(o[:, 4:10], r[:, 0:512]))
            ctx.op("vector", lambda E, o=mvc, r=rtile: E.bn_stats(o[:, 10:16], r[:, 512:1024]), ve)
            ctx.wait("vector", ve)
            ctx.op("vector", lambda E, o=mvc: E.bn_aggr(o[:, 0:2], o[:, 4:16]), ve)
            ctx.wait("vector", ve)
            ts1("vector", mvc[:, 2:3], mvc[:, 1:2], EPS, ALU.add, sem=ve)
            ctx.wait("scalar", ve)
            act(mvc[:, 2:3], mvc[:, 2:3], AF.Ln, sem=se)
            ctx.wait("scalar", se)
            act(mvc[:, 3:4], mvc[:, 2:3], AF.Exp, scale=-0.5, sem=se)
            ctx.wait("vector", se)
            stt("vector", mvc[:, 2:3], mvc[:, 0:1], -1.0, mvc[:, 3:4],
                ALU.mult, ALU.mult, sem=ve)
            t0ap = xb.ap.bitcast(BF16)[:, 0:1024]
            ctx.wait("scalar", ve)
            act(t0ap, rtile, AF.Identity, bias=mvc[:, 2:3], scale=mvc[:, 3:4], sem=se)
            d0_pend.append((jj, xb, t0ap, se.n))

        def d0_pe():
            jj, xb, t0ap, t0se = d0_pend.pop(0)
            tbk = banks[6 + jj % 2]
            tbk.acquire(ctx)
            ctx.wait("tensor", se, t0se)
            tb_bf = tbk.ps.bitcast(BF16)
            for c in range(CH):
                ctx.op("tensor",
                       lambda E, o=tb_bf[:, c * 128:(c + 1) * 128],
                       i=t0ap[:, c * 128:(c + 1) * 128]:
                       E.transpose(o, i, eyeb[:, :]), pe if c == CH - 1 else None)
            tpv = pe.n
            xb.free = (pe, tpv)
            ctx.wait("vector", pe, tpv)
            dview = FM.rearrange("p (c t) -> p c t", c=CH)[:, :, jj * 128:(jj + 1) * 128]
            vcopy(dview, tb_bf[:, 0:1024].rearrange("p (c t) -> p c t", c=CH), ve)
            t0fm_ve = ve.n
            tbk.release(ve, t0fm_ve)
            bg = banks[6 + jj % 2]
            bg.acquire(ctx)
            ctx.wait("tensor", ve, t0fm_ve)
            ctx.wait("tensor", dqw, gw_ld[0])
            for c in range(CH):
                mm(bg.ps[:, 0:8],
                   FM[:, c * S_OWN + jj * 128:c * S_OWN + (jj + 1) * 128],
                   gwsb[:, c * 8:(c + 1) * 8],
                   start=(c == 0), stop=(c == CH - 1), inc=(c == CH - 1))
            ctx.wait("vector", pe, pe.n)
            ctx.wait("vector", se)
            vcopy(scsb[:, 0:8], bg.ps[:, 0:8], ve)
            bg.release(ve, ve.n)
            ctx.wait("vector", ve)
            vred(sccol[:, 0:1], scsb[:, 0:8], ALU.max, sem=ve)
            ctx.wait("vector", ve)
            ts1("vector", sccol[:, 1:2], sccol[:, 0:1], -1.0, ALU.mult, sem=ve)
            ctx.wait("scalar", ve)
            act(scsb[:, 8:16], scsb[:, 0:8], AF.Exp, bias=sccol[:, 1:2], sem=se)
            ctx.wait("vector", se)
            vred(sccol[:, 2:3], scsb[:, 8:16], ALU.add, sem=ve)
            ctx.wait("vector", ve)
            ctx.op("vector", lambda E: E.reciprocal(sccol[:, 3:4], sccol[:, 2:3]), ve)
            ctx.wait("vector", ve)
            ts1("vector", scsb[:, 0:8], scsb[:, 8:16], sccol[:, 3:4], ALU.mult, sem=ve)
            ctx.wait("vector", ve)
            vred(sccol[:, 4:5], scsb[:, 0:8], ALU.max, sem=ve)
            ctx.wait("vector", ve)
            ts1("vector", scsb[:, 8:16], scsb[:, 0:8], sccol[:, 4:5], ALU.is_ge, sem=ve)
            ctx.wait("vector", ve)
            stt("vector", scsb[:, 8:16], scsb[:, 8:16], -1e9, scsb[:, 0:8],
                ALU.mult, ALU.add, sem=ve)
            ctx.wait("vector", ve)
            vred(sccol[:, 5:6], scsb[:, 8:16], ALU.max, sem=ve)
            ctx.wait("vector", ve)
            ts1("vector", scsb[:, 8:16], scsb[:, 0:8], sccol[:, 5:6], ALU.is_ge, sem=ve)
            ctx.wait("vector", ve)
            tt("vector", wdsb[:, jj * 8:(jj + 1) * 8], scsb[:, 8:16], scsb[:, 0:8],
               ALU.mult, sem=ve)

        for G in range(2):
            vbv = fold_w1G(G)
            if G == 0:
                ctx.wait("gpsimd", pe, phaseC_pe)
            ctx.dma("gpsimd", w2G[:, :].rearrange("p (c n) -> p c n", c=16),
                    w2d[G * 2048:(G + 1) * 2048, :].rearrange("(c p) n -> p c n", p=128),
                    dqw)
            w2v = dqw.n
            if G == 1:
                ctx.dma("gpsimd", gwsb[:, :].rearrange("p (c n) -> p c n", c=CH),
                        gwd.rearrange("(c p) n -> p c n", p=128), dqw)
                gw_ld[0] = dqw.n
            for fb in range(4):
                tok0 = fb * 512
                w1_banks = [banks[0], banks[1], banks[4]]
                for cc in range(16):
                    b = w1_banks[cc % 3]
                    b.acquire(ctx)
                    ctx.wait("tensor", ve, vbv)
                    if G == 0 and fb == 0 and cc == 0:
                        ctx.wait("tensor", ve, hs_tp)
                    for c in range(CH):
                        mm(b.ps[:, :],
                           WA[:, c * 2048 + cc * 128:c * 2048 + (cc + 1) * 128],
                           FM[:, c * S_OWN + tok0:c * S_OWN + tok0 + 512],
                           start=(c == 0), stop=(c == CH - 1), inc=(c == CH - 1))
                    mmv = pe.n
                    tmp = ftmp[cc % 2]
                    ctx.wait("scalar", pe, mmv)
                    ctx.wait("scalar", ve)
                    act(tmp, b.ps[:, :], AF.Identity, scale=10.0,
                        bias=vbw1s[:, G * 16 + cc:G * 16 + cc + 1], sem=se)
                    b.release(se, se.n)
                    ctx.wait("vector", se)
                    ts2("vector", tmp, tmp, RC, -RC, ALU.add, ALU.add, sem=ve)
                    ctx.wait("vector", ve)
                    ts2("vector", tmp, tmp, 127.0, -128.0, ALU.min, ALU.max, sem=ve)
                    ctx.wait("scalar", ve)
                    act(g1G[:, cc * 512:(cc + 1) * 512], tmp,
                        AF.Identity if DBG_FFN_IDENT[0] else AF.Gelu_apprx_tanh,
                        scale=0.1, sem=se)
                g1_done = se.n
                ctx.wait("tensor", se, g1_done)
                ctx.wait("tensor", dqw, w2v)
                for tc in range(4):
                    jj = fb * 4 + tc
                    if d0_pend:
                        d0_pe()
                    for grp in range(2):
                        b = [banks[2], banks[3], banks[5]][(tc * 2 + grp) % 3]
                        b.acquire(ctx)
                        for cc in range(16):
                            mm(b.ps[:, :],
                               g1G[:, cc * 512 + tc * 128:cc * 512 + (tc + 1) * 128],
                               w2G[:, cc * 1024 + grp * 512:cc * 1024 + (grp + 1) * 512],
                               start=(cc == 0), stop=(cc == 15), inc=(cc == 15))
                        ctx.wait("vector", pe, pe.n)
                        rsl = RES[:, jj * D + grp * 512:jj * D + (grp + 1) * 512]
                        tt("vector", rsl, b.ps[:, :], rsl, ALU.add, sem=ve)
                        b.release(ve, ve.n)
                    if G == 1:
                        d0_tile(jj)
        ffn_pe = pe.n
        ffn_ve = ve.n

        while d0_pend:
            d0_pe()
        wd_ve = ve.n
        x1_ve = ve.n

        if upto == "F":
            _finish(ctx, locals())
            _replay(ctx, nc)
            return nc

        u_tms = [WB[:, 12288:13056], WB[:, 13824:14592]]
        u_fms = [WB[:, 13056:13824], WB[:, 14592:15360]]
        silu_bf = WB[:, 15360:16128]
        W1b = [Buf(W1xp_aps[0]), Buf(W1xp_aps[1])]
        W2b = [Buf(W2xp_aps[0]), Buf(W2xp_aps[1])]
        W1b[0].free = (pe, ffn_pe)
        W1b[1].free = [(ve, x1_ve), (pe, ffn_pe)]
        W2b[0].free = (ve, wd_ve)
        W2b[1].free = (ve, wd_ve)
        u_tmbs = [Buf(u_tms[0]), Buf(u_tms[1])]
        u_fmbs = [Buf(u_fms[0]), Buf(u_fms[1])]
        for bb in u_tmbs + u_fmbs:
            bb.free = [(ve, x1_ve), (pe, ffn_pe)]
        h_banks = [[banks[0], banks[1], banks[2]], [banks[3], banks[4], banks[5]]]
        eo_banks = [banks[6], banks[7]]

        for e in range(NEXP):
            wb1 = W1b[e % 2]
            wb1.wait_free(ctx, "gpsimd")
            ctx.dma("gpsimd", wb1.ap.rearrange("p (c n) -> p c n", c=CH),
                    W1xd[e].rearrange("(c p) n -> p c n", p=128), dqw)
            wb1.ready = (dqw, dqw.n)
            wb2 = W2b[e % 2]
            wb2.wait_free(ctx, "gpsimd")
            ctx.dma("gpsimd", wb2.ap.rearrange("p (c n) -> p c n", c=6),
                    W2xd[e].rearrange("(c p) n -> p c n", p=128), dqw)
            wb2.ready = (dqw, dqw.n)
            wb1.wait_ready(ctx, "tensor")
            ctx.wait("tensor", ve, wd_ve)
            last_h = [None]
            last_eo = [None]
            pend = []

            def emit_eo(tca, ui):
                fb_ = u_fmbs[ui]
                fb_.wait_ready(ctx, "tensor")
                wb2.wait_ready(ctx, "tensor")
                for grp in range(2):
                    b = eo_banks[grp]
                    b.acquire(ctx)
                    for cc in range(6):
                        mm(b.ps[:, :], u_fms[ui][:, cc * 128:(cc + 1) * 128],
                           wb2.ap[:, cc * 1024 + grp * 512:cc * 1024 + (grp + 1) * 512],
                           start=(cc == 0), stop=(cc == 5), inc=(cc == 5))
                    mmv2 = pe.n
                    last_eo[0] = mmv2
                    ctx.wait("vector", pe, mmv2)
                    rsl = RES[:, tca * D + grp * 512:tca * D + (grp + 1) * 512]
                    tt("vector", rsl, b.ps[:, :], rsl, ALU.add, sem=ve)
                    b.release(ve, ve.n)
                fb_.free = (pe, last_eo[0])

            for tca in range(16):
                upi = tca % 2
                hb = h_banks[tca % 2]
                trio = trios[tca % 2]
                for ng in range(3):
                    b = hb[ng]
                    b.acquire(ctx)
                    for c in range(CH):
                        mm(b.ps[:, :],
                           FM[:, c * S_OWN + tca * 128:c * S_OWN + (tca + 1) * 128],
                           wb1.ap[:, c * 1536 + ng * 512:c * 1536 + (ng + 1) * 512],
                           start=(c == 0), stop=(c == CH - 1), inc=(c == CH - 1))
                mmv = pe.n
                last_h[0] = mmv
                if len(pend) >= 2:
                    tca_, ui_ = pend.pop(0)
                    emit_eo(tca_, ui_)
                ctx.wait("scalar", pe, mmv)
                ctx.wait("scalar", ve)
                act(silu_bf[:, 0:768], trio[:, 768:1536], AF.Silu, sem=se)
                sv = se.n
                ub = u_tmbs[upi]
                ub.wait_free(ctx, "vector")
                ctx.wait("vector", se, sv)
                ctx.wait("vector", pe, mmv)
                ut = u_tms[upi]
                if e < 8:
                    wdcol = wdsb[:, tca * 8 + e:tca * 8 + e + 1]
                    stt("vector", ut[:, 0:768], trio[:, 0:768], wdcol,
                        silu_bf[:, 0:768], ALU.mult, ALU.mult, sem=ve)
                else:
                    tt("vector", ut[:, 0:768], trio[:, 0:768],
                       silu_bf[:, 0:768], ALU.mult, sem=ve)
                uv = ve.n
                ub.ready = (ve, uv)
                hb[1].release(ve, uv)
                hb[2].release(ve, uv)
                fb_ = u_fmbs[upi]
                ctx.wait("tensor", ve, uv)
                tb_bf = hb[0].ps.bitcast(BF16)
                for c in range(6):
                    ctx.op("tensor",
                           lambda E, o=tb_bf[:, c * 128:(c + 1) * 128],
                           i=ut[:, c * 128:(c + 1) * 128]:
                           E.transpose(o, i, eyeb[:, :]), pe if c == 5 else None)
                tpv = pe.n
                ub.free = (pe, tpv)
                fb_.wait_free(ctx, "scalar")
                ctx.wait("scalar", pe, tpv)
                act(u_fms[upi][:, 0:768], tb_bf[:, 0:768], AF.Identity, sem=se)
                hb[0].release(se, se.n)
                fb_.ready = (se, se.n)
                pend.append((tca, upi))
            while pend:
                tca_, ui_ = pend.pop(0)
                emit_eo(tca_, ui_)
            wb1.free = (pe, last_h[0])
            wb2.free = (pe, last_eo[0])
        _finish(ctx, locals())
        _replay(ctx, nc)
        return nc


def _finish(ctx, env):
    pe, ve, se = env["pe"], env["ve"], env["se"]
    dqo, dqt, dqx, dqw = env["dqo"], env["dqt"], env["dqx"], env["dqw"]
    RES, outd = env["RES"], env["outd"]
    for s in (pe, ve, se, dqt, dqx, dqw):
        ctx.wait("scalar", s)
    tap_aps = {
        "as_fm": env["FM"], "hsfm": env["FM"], "t0fm": env["FM"],
        "kvsb": env["kvsb"], "res": env["RES"], "phik": env["phik"],
        "vtm": env["vtm"], "phiq": env["phiq"], "attn": env["attnf"],
        "rec": env["recsb"], "wd": env["wdsb"], "vbq": env["vbqc"],
        "vbkv": env["VBKV"], "wkvs": env["WA"], "wqs": env["wqS"],
        "g1g": env["g1G"], "zmat": env["zmat"], "kvsq": env["kvs"],
    }
    nc = ctx.nc
    for t in env["taps"]:
        src = tap_aps[t]
        shp = [src.shape[0], src.shape[1]]
        dt = src.dtype
        td = nc.dram_tensor("tap_" + t, shp, dt, kind="ExternalOutput").ap()
        ctx.dma("scalar", td[:, :], src, dqo)
    for jj in range(16):
        ctx.dma("scalar", outd[jj * 128:(jj + 1) * 128, :],
                RES[:, jj * 1024:(jj + 1) * 1024], dqo)
    ctx.wait("scalar", dqo)
    ctx.wait("sync", dqo)


def _replay(ctx, nc):
    with nc.Block() as block:
        block.sync(lambda E: [f(E) for f in ctx.plan["sync"]])
        block.scalar(lambda E: [f(E) for f in ctx.plan["scalar"]])
        block.vector(lambda E: [f(E) for f in ctx.plan["vector"]])
        block.tensor(lambda E: [f(E) for f in ctx.plan["tensor"]])
        block.gpsimd(lambda E: [f(E) for f in ctx.plan["gpsimd"]])


def _host_layout(inputs):
    x = np.asarray(inputs["L0_attn_x"], np.float32)
    wq = np.ascontiguousarray(np.asarray(inputs["L0_attn_wq"], np.float32))
    wk = np.asarray(inputs["L0_attn_wk"], np.float32)
    wv = np.asarray(inputs["L0_attn_wv"], np.float32)
    wkv = np.ascontiguousarray(np.concatenate([wk, wv], axis=1))
    wo = np.ascontiguousarray(np.asarray(inputs["L0_attn_wo"], np.float32))
    w1 = np.ascontiguousarray(np.asarray(inputs["L0_attn_w1"], np.float32))
    w2 = np.ascontiguousarray(np.asarray(inputs["L0_attn_w2"], np.float32))
    gw = np.ascontiguousarray(np.asarray(inputs["L0_ffn_gate_w"], np.float32).T)
    W1 = np.asarray(inputs["L0_ffn_W1"], np.float32)
    W2 = np.asarray(inputs["L0_ffn_W2"], np.float32)
    Ws1 = np.asarray(inputs["L0_ffn_Ws1"], np.float32)
    Ws2 = np.asarray(inputs["L0_ffn_Ws2"], np.float32)
    W1t = W1.transpose(0, 2, 1)
    W2t = W2.transpose(0, 2, 1)
    Ws1t = Ws1.T
    Ws2t = Ws2.T
    W1x = np.zeros((NEXP, D, 1536), np.float32)
    W2x = np.zeros((NEXP, 768, D), np.float32)
    W1x[:8] = W1t
    W2x[:8] = W2t
    W1x[8] = np.concatenate([Ws1t[:, 0:768], Ws1t[:, 1536:2304]], axis=1)
    W1x[9] = np.concatenate([Ws1t[:, 768:1536], Ws1t[:, 2304:3072]], axis=1)
    W2x[8] = Ws2t[0:768]
    W2x[9] = Ws2t[768:1536]
    oneh = np.zeros((16, 1024), np.float32)
    for p_ in range(8):
        oneh[2 * p_, p_ * 128:p_ * 128 + 64] = 1.0
        oneh[2 * p_ + 1, p_ * 128 + 64:p_ * 128 + 128] = 1.0
    shared = dict(
        eye=np.eye(128, dtype=np.float32),
        oneh=oneh,
        wq=wq, wkv=wkv, wo=wo, w1=w1, w2=w2, gw=gw,
        W1x=np.ascontiguousarray(W1x), W2x=np.ascontiguousarray(W2x),
        g1=np.ascontiguousarray(np.asarray(inputs["L0_attn_ln1_g"], np.float32)),
        b1=np.ascontiguousarray(np.asarray(inputs["L0_attn_ln1_b"], np.float32)),
        g2=np.ascontiguousarray(np.asarray(inputs["L0_attn_ln2_g"], np.float32)),
        b2=np.ascontiguousarray(np.asarray(inputs["L0_attn_ln2_b"], np.float32)),
    )
    in_maps = []
    for c in range(8):
        b, h = c // 2, c % 2
        own = x[b, h * 2048:(h + 1) * 2048]
        oth = x[b, (1 - h) * 2048:(1 - h) * 2048 + 2048]
        m = dict(shared)
        m["x"] = np.ascontiguousarray(np.concatenate([own, oth], axis=0))
        in_maps.append(m)
    return in_maps


def kernel(**inputs):
    in_maps = _host_layout(inputs)
    nc = build_program()
    res = run_bass_kernel_spmd(nc, in_maps, core_ids=list(range(8)))
    out = np.zeros((4, 4096, 1024), np.float32)
    for c in range(8):
        b, h = c // 2, c % 2
        out[b, h * 2048:(h + 1) * 2048] = res.results[c]["out"]
    return out


# version 11
# speedup vs baseline: 1.0249x; 1.0249x over previous
import contextlib
import numpy as np

import concourse.bass as bass
import concourse.mybir as mybir
from concourse.bass_utils import run_bass_kernel_spmd

F32 = mybir.dt.float32
BF16 = mybir.dt.bfloat16
AF = mybir.ActivationFunctionType
ALU = mybir.AluOpType
AX = mybir.AxisListType

D = 1024
DBG_FFN_IDENT = [False]
PHASE_MARKS = {}
DBG_EO_INLINE = [False]
DBG_D1_TP_SYNC = [False]
DBG_D1_NOPING = [False]
DBG_D0_TP_SYNC = [False]
CH = 8
S_ALL = 4096
S_OWN = 2048
EPS = 1e-5
RC = 12582912.0
NEXP = 10

ENGS = ("sync", "scalar", "vector", "tensor", "gpsimd")


class SemT:
    __slots__ = ("h", "n")

    def __init__(self, h):
        self.h = h
        self.n = 0


class Ctx:
    def __init__(self, nc, es):
        self.nc = nc
        self.es = es
        self.plan = {e: [] for e in ENGS}
        self.lastwait = {}

    def sem(self, name):
        return SemT(self.es.enter_context(self.nc.semaphore(name)))

    def emit(self, eng, fn):
        self.plan[eng].append(fn)

    def wait(self, eng, sem, val=None):
        v = sem.n if val is None else val
        if v <= 0:
            return
        key = (eng, id(sem))
        if self.lastwait.get(key, -1) >= v:
            return
        self.lastwait[key] = v
        self.emit(eng, lambda E, s=sem.h, v=v: E.wait_ge(s, v))

    def dma(self, eng, out, in_, sem, transpose=False, slow=False):
        sem.n += 16

        def f(E, out=out, in_=in_, t=transpose, s=sem.h, sl=slow):
            E.dma_start(out=out, in_=in_, transpose=t,
                        allow_slow_non_contiguous=sl).then_inc(s, 16)

        self.emit(eng, f)
        return sem.n

    def op(self, eng, builder, sem=None):
        if sem is not None:
            sem.n += 1
        sh = sem.h if sem is not None else None

        def f(E, b=builder, sh=sh):
            i = b(E)
            if sh is not None:
                i.then_inc(sh, 1)

        self.emit(eng, f)
        return sem.n if sem is not None else None


class Bank:
    def __init__(self, ps):
        self.ps = ps
        self.free = None

    def acquire(self, ctx):
        if self.free is not None:
            ctx.wait("tensor", self.free[0], self.free[1])

    def release(self, sem, val):
        self.free = (sem, val)


class Buf:
    def __init__(self, ap):
        self.ap = ap
        self.free = None
        self.ready = None

    @staticmethod
    def _w(ctx, eng, deps):
        if deps is None:
            return
        if isinstance(deps, list):
            for s, v in deps:
                ctx.wait(eng, s, v)
        else:
            ctx.wait(eng, deps[0], deps[1])

    def wait_free(self, ctx, eng):
        self._w(ctx, eng, self.free)

    def wait_ready(self, ctx, eng):
        self._w(ctx, eng, self.ready)


def build_program(upto="E", taps=()):
    nc = bass.Bass("TRN2", target_bir_lowering=False)
    taps = list(taps)

    xd = nc.dram_tensor("x", [S_ALL, D], F32, kind="ExternalInput").ap()
    wqd = nc.dram_tensor("wq", [D, D], F32, kind="ExternalInput").ap()
    wkvd = nc.dram_tensor("wkv", [D, 2 * D], F32, kind="ExternalInput").ap()
    wod = nc.dram_tensor("wo", [D, D], F32, kind="ExternalInput").ap()
    w1d = nc.dram_tensor("w1", [D, 4096], F32, kind="ExternalInput").ap()
    w2d = nc.dram_tensor("w2", [4096, D], F32, kind="ExternalInput").ap()
    gwd = nc.dram_tensor("gw", [D, 8], F32, kind="ExternalInput").ap()
    W1xd = nc.dram_tensor("W1x", [NEXP, D, 1536], F32, kind="ExternalInput").ap()
    W2xd = nc.dram_tensor("W2x", [NEXP, 768, D], F32, kind="ExternalInput").ap()
    eyed = nc.dram_tensor("eye", [128, 128], F32, kind="ExternalInput").ap()
    onehd = nc.dram_tensor("oneh", [16, 1024], F32, kind="ExternalInput").ap()
    g1d = nc.dram_tensor("g1", [D], F32, kind="ExternalInput").ap()
    b1d = nc.dram_tensor("b1", [D], F32, kind="ExternalInput").ap()
    g2d = nc.dram_tensor("g2", [D], F32, kind="ExternalInput").ap()
    b2d = nc.dram_tensor("b2", [D], F32, kind="ExternalInput").ap()
    outd = nc.dram_tensor("out", [S_OWN, D], F32, kind="ExternalOutput").ap()
    vbd = nc.dram_tensor("vbd", [1, 8192], F32).ap()
    recd = nc.dram_tensor("recd", [16, 512], F32).ap()

    es = contextlib.ExitStack()
    with es:
        ctx = Ctx(nc, es)
        sb = lambda name, shape, dt: es.enter_context(
            nc.sbuf_tensor(name, shape, dt)).ap()

        dqx = ctx.sem("dqx")
        dqt = ctx.sem("dqt")
        dqt2 = ctx.sem("dqt2")
        dqw = ctx.sem("dqw")
        pe = ctx.sem("pe")
        ve = ctx.sem("ve")
        se = ctx.sem("se")
        ge = ctx.sem("ge")
        dqo = ctx.sem("dqo")

        RES = sb("RES", [128, 16 * D], F32)
        FM = sb("FM", [128, CH * S_OWN], BF16)
        g1c = sb("g1c", [128, 8], F32)
        b1c = sb("b1c", [128, 8], F32)
        g2c = sb("g2c", [128, 8], F32)
        b2c = sb("b2c", [128, 8], F32)
        b1cb = sb("b1cb", [128, 8], BF16)
        b2cb = sb("b2cb", [128, 8], BF16)
        vbqc = sb("vbqc", [128, 8], F32)
        vbw1c = sb("vbw1c", [128, 32], F32)
        vbw1s = sb("vbw1s", [128, 32], F32)
        ones1 = sb("ones1", [128, 1], BF16)
        eyeb = sb("eyeb", [128, 128], BF16)
        onehb = sb("onehb", [16, 1024], BF16)
        rechi = sb("rechi", [16, 512], BF16)
        reclo = sb("reclo", [16, 512], BF16)
        fstg2 = sb("fstg2", [128, 2048], F32)
        gwsb = sb("gwsb", [128, 64], BF16)
        kvsb = sb("kvsb", [128, 8 * 132], F32)
        kvs = sb("kvs", [128, 8 * 128], BF16)
        zmat = sb("zmat", [128, 8 * 16], BF16)
        wdsb = sb("wdsb", [128, 16 * 8], F32)
        colA = sb("colA", [128, 48], F32)
        colB = sb("colB", [128, 48], F32)
        recsb = sb("recsb", [16, 512], F32)
        scsb = sb("scsb", [128, 16], F32)
        sccol = sb("sccol", [128, 8], F32)

        WA = sb("WA", [128, 16384], BF16)
        WB = sb("WB", [128, 16384], BF16)
        SCR = sb("SCR", [128, 12288], BF16)

        wqS = WB[:, 0:8192]
        vbkvr = WB[0:1, 8192:10240]
        onesr = WB[0:1, 14336:14464]
        VBKV = WB[:, 8192:12288].bitcast(F32)
        tmpA = WB[:, 12288:13312].bitcast(F32)
        tmpB = WB[:, 13312:14336].bitcast(F32)
        woS = WB[:, 8192:16384]
        w2G = WB
        W1xp_aps = [WA[:, 0:12288], WB[:, 0:12288]]
        u_tm = WB[:, 12288:13824]
        u_tm = WB[:, 12288:13056]
        u_fm = WB[:, 13056:13824]
        silu_sb = WB[:, 13824:15360].bitcast(F32)

        x0p = [SCR[:, 0:2048].bitcast(F32), SCR[:, 2048:4096].bitcast(F32)]
        astp = [SCR[:, 4096:5120], SCR[:, 5120:6144]]
        afsp = [SCR[:, 6144:7168], SCR[:, 7168:8192]]
        phik = SCR[:, 8192:10240]
        vtm = SCR[:, 10240:12288]
        stg = [SCR[:, 0:4096].bitcast(F32), SCR[:, 4096:8192].bitcast(F32)]
        plainb = SCR[:, 8192:10240]
        phiq = SCR[:, 0:4096]
        attnf = SCR[:, 4096:8192]
        recbc = SCR[:, 8192:9216].bitcast(F32)
        ctmpA = SCR[:, 9216:10240].bitcast(F32)
        ctmpB = SCR[:, 10240:11264].bitcast(F32)
        hst = SCR[:, 11264:12288]
        g1G = SCR[:, 0:8192]
        fstg = SCR[:, 8192:12288].bitcast(F32)
        t0p = [SCR[:, 0:1024], SCR[:, 1024:2048]]
        gws = SCR[:, 2048:2112]
        W2xp_aps = [SCR[:, 0:6144], SCR[:, 6144:12288]]

        pt0 = es.enter_context(nc.psum_tensor("pt0", [128, 1536], F32)).ap()
        pt1 = es.enter_context(nc.psum_tensor("pt1", [128, 1536], F32)).ap()
        pb6 = es.enter_context(nc.psum_tensor("pb6", [128, 512], F32)).ap()
        pb7 = es.enter_context(nc.psum_tensor("pb7", [128, 512], F32)).ap()
        banks = [Bank(pt0[:, 0:512]), Bank(pt0[:, 512:1024]), Bank(pt0[:, 1024:1536]),
                 Bank(pt1[:, 0:512]), Bank(pt1[:, 512:1024]), Bank(pt1[:, 1024:1536]),
                 Bank(pb6), Bank(pb7)]
        trios = [pt0, pt1]

        def mm(out_ap, lhsT, rhs, start, stop, inc=False):
            return ctx.op("tensor",
                          lambda E, o=out_ap, l=lhsT, r=rhs, st=start, sp=stop:
                          E.matmul(o, l, r, start=st, stop=sp),
                          pe if inc else None)

        def ts1(eng, out, in0, s1, op0, sem=None):
            return ctx.op(eng, lambda E, o=out, i=in0, a=s1, p=op0:
                          E.tensor_single_scalar(o, i, a, p), sem)

        def ts2(eng, out, in0, s1, s2, op0, op1, sem=None):
            return ctx.op(eng, lambda E, o=out, i=in0, a=s1, b=s2, p=op0, q=op1:
                          E.tensor_scalar(o, i, a, b, p, q), sem)

        def tt(eng, out, in0, in1, op, sem=None):
            return ctx.op(eng, lambda E, o=out, a=in0, b=in1, p=op:
                          E.tensor_tensor(o, a, b, p), sem)

        def stt(eng, out, in0, scal, in1, op0, op1, sem=None):
            return ctx.op(eng, lambda E, o=out, a=in0, s=scal, b=in1, p=op0, q=op1:
                          E.scalar_tensor_tensor(o, a, s, b, p, q), sem)

        def act(out, in_, func, bias=0.0, scale=1.0, sem=None):
            return ctx.op("scalar", lambda E, o=out, i=in_, f=func, b=bias, s=scale:
                          E.activation(o, i, f, bias=b, scale=s), sem)

        def vcopy(out, in_, sem=None):
            return ctx.op("vector", lambda E, o=out, i=in_: E.tensor_copy(o, i), sem)

        def vred(out, in_, op, sem=None):
            return ctx.op("vector", lambda E, o=out, i=in_, p=op:
                          E.tensor_reduce(o, i, AX.X, p), sem)

        def tp_split(pairs, split=False):
            for i, (dst, srcp) in enumerate(pairs):
                if split and i % 2 == 1:
                    ctx.dma("scalar", dst, srcp, dqt2, transpose=True)
                else:
                    ctx.dma("sync", dst, srcp, dqt, transpose=True)
            return [(dqt, dqt.n), (dqt2, dqt2.n)]

        for dst, src in ((g1c, g1d), (b1c, b1d), (g2c, g2d), (b2c, b2d)):
            ctx.dma("scalar", dst[:, :], src.rearrange("(c p) -> p c", p=128), dqx, slow=True)
        ctx.wait("vector", dqx, dqx.n)
        vcopy(b1cb[:, :], b1c[:, :])
        vcopy(b2cb[:, :], b2c[:, :])
        ctx.op("vector", lambda E: E.memset(ones1[:, :], 1.0))
        ctx.op("vector", lambda E: E.memset(onesr[:, :], 1.0))
        ctx.op("vector", lambda E: E.memset(kvsb[:, :], 0.0))
        ctx.op("vector", lambda E: E.memset(kvs[:, :], 0.0))
        ctx.op("vector", lambda E: E.memset(zmat[:, :], 0.0), ve)

        ctx.dma("gpsimd", eyeb[:, :], eyed[:, :], dqw)
        ctx.dma("gpsimd", onehb[:, :], onehd[:, :], dqw)
        stgb = [Buf(stg[0]), Buf(stg[1])]
        vb_banks_q = [banks[0], banks[1]]
        vb_banks_kv = [banks[2], banks[3], banks[4], banks[5]]

        def prep_fold(wsrc, ncols, dst_ap, gcol, vb_banks, bvec):
            for c in range(CH):
                s = stgb[c % 2]
                s.wait_free(ctx, "gpsimd")
                ctx.dma("gpsimd", s.ap[:, 0:ncols], wsrc[c * 128:(c + 1) * 128, :], dqw)
                ctx.wait("vector", dqw, dqw.n)
                ctx.wait("vector", pe)
                ts1("vector", dst_ap[:, c * ncols:(c + 1) * ncols], s.ap[:, 0:ncols],
                    gcol[:, c:c + 1], ALU.mult)
                vcopy(plainb[:, 0:ncols], s.ap[:, 0:ncols], ve)
                s.free = (ve, ve.n)
                ctx.wait("tensor", ve, ve.n)
                for g in range(ncols // 512):
                    b = vb_banks[g]
                    if c == 0:
                        b.acquire(ctx)
                    mm(b.ps[0:1, :], bvec[:, c:c + 1], plainb[:, g * 512:(g + 1) * 512],
                       start=(c == 0), stop=(c == CH - 1), inc=(c == CH - 1))

        prep_fold(wqd, D, wqS, g1c, vb_banks_q, b1cb)
        prep_fold(wkvd, 2 * D, WA, g1c, vb_banks_kv, b1cb)

        ctx.wait("vector", pe)
        for i, b in enumerate(vb_banks_q + vb_banks_kv):
            if i > 0:
                ctx.wait("vector", dqt)
            vcopy(tmpA[0:1, 0:512], b.ps[0:1, :], ve)
            b.release(ve, ve.n)
            ctx.wait("sync", ve, ve.n)
            ctx.dma("sync", vbd[0:1, i * 512:(i + 1) * 512], tmpA[0:1, 0:512], dqt)
        vb_evac = ve.n
        ctx.wait("scalar", dqt, dqt.n)
        ctx.dma("scalar", vbqc[:, :],
                vbd[0:1, 0:1024].rearrange("a (c p) -> (a p) c", p=128), dqx, slow=True)
        ctx.wait("gpsimd", dqt, dqt.n)
        ctx.dma("gpsimd", vbkvr[:, :], vbd[0:1, 1024:3072], dqw)
        prep_loads = dqx.n
        prep_loads_w = dqw.n
        PHASE_MARKS["prep"] = len(ctx.plan["tensor"])
        prep_pe = pe.n
        prep_ve = ve.n

        NTC = 2
        ABLK = 16
        x0b = [Buf(x0p[0]), Buf(x0p[1]),
               Buf(fstg2[:, 0:1024]), Buf(fstg2[:, 1024:2048])]
        astb = [Buf(astp[0]), Buf(astp[1])]
        afsb = [Buf(afsp[0]), Buf(afsp[1])]
        for bb in x0b + astb + afsb:
            bb.free = (pe, prep_pe)
        kv_banks = [banks[2], banks[3], banks[4], banks[5]]
        pj_banks = [banks[0], banks[1]]
        first_scr = [True]

        for blk in range(ABLK):
            own = blk < 8
            mvc = colA if blk % 2 == 0 else colB
            for tc in range(NTC):
                jj = blk * NTC + tc
                xb = x0b[jj % 4]
                xb.wait_free(ctx, "scalar")
                if first_scr[0]:
                    ctx.wait("scalar", ve, prep_ve)
                ctx.dma("scalar", xb.ap[:, :], xd[jj * 128:(jj + 1) * 128, :], dqx)
                xb.ready = (dqx, dqx.n)
                xb.wait_ready(ctx, "vector")
                if blk >= 2:
                    ctx.wait("vector", se)
                bnc = 4 + tc * 12
                ctx.op("vector", lambda E, x=xb.ap, o=mvc, b=bnc:
                       E.bn_stats(o[:, b:b + 6], x[:, 0:512]))
                ctx.op("vector", lambda E, x=xb.ap, o=mvc, b=bnc:
                       E.bn_stats(o[:, b + 6:b + 12], x[:, 512:1024]), ve)
                ctx.wait("vector", ve)
                ctx.op("vector", lambda E, o=mvc, tc=tc, b=bnc:
                       E.bn_aggr(o[:, tc * 2:tc * 2 + 2], o[:, b:b + 12]), ve)
                ctx.wait("vector", ve)
            m_ = mvc[:, 0:2 * NTC:2]
            v_ = mvc[:, 1:2 * NTC:2]
            t4_ = mvc[:, 28:28 + NTC]
            t1_ = mvc[:, 28 + NTC:28 + 2 * NTC]
            s01_ = mvc[:, 32:32 + NTC]
            rs1_ = mvc[:, 32 + NTC:32 + 2 * NTC]
            nms_ = mvc[:, 36:36 + NTC]
            nm1_ = mvc[:, 38:38 + NTC]
            ts2("vector", t4_, v_, 1.0 + EPS, EPS * EPS, ALU.mult, ALU.add)
            ts1("vector", t1_, v_, EPS, ALU.add, sem=ve)
            ctx.wait("scalar", ve)
            act(mvc[:, 28:28 + 2 * NTC], mvc[:, 28:28 + 2 * NTC], AF.Ln, sem=se)
            ctx.wait("scalar", se)
            act(mvc[:, 32:32 + 2 * NTC], mvc[:, 28:28 + 2 * NTC], AF.Exp,
                scale=-0.5, sem=se)
            ctx.wait("vector", se)
            stt("vector", nms_, m_, -1.0, s01_, ALU.mult, ALU.mult, sem=ve)
            if own:
                stt("vector", nm1_, m_, -1.0, rs1_, ALU.mult, ALU.mult, sem=ve)
            cols_ve = ve.n

            for tc in range(NTC):
                jj = blk * NTC + tc
                xb = x0b[jj % 4]
                ab = astb[jj % 2]
                ab.wait_free(ctx, "scalar")
                ctx.wait("scalar", ve, cols_ve)
                act(ab.ap[:, :], xb.ap[:, :], AF.Identity,
                    bias=nms_[:, tc:tc + 1], scale=s01_[:, tc:tc + 1], sem=se)
                ab.ready = (se, se.n)
                if own:
                    act(RES[:, jj * D:(jj + 1) * D], xb.ap[:, :], AF.Identity,
                        bias=nm1_[:, tc:tc + 1], scale=rs1_[:, tc:tc + 1], sem=se)
                xb.free = (se, se.n)
                if own:
                    def fmdst(c, jj=jj):
                        return FM[:, c * S_OWN + jj * 128:c * S_OWN + (jj + 1) * 128]
                    dst = fmdst
                else:
                    fs = afsb[jj % 2]
                    fs.wait_free(ctx, "sync")

                    def fmdst(c, fs=fs):
                        return fs.ap[:, c * 128:(c + 1) * 128]
                    dst = fmdst
                tb = banks[6 + jj % 2]
                tb.acquire(ctx)
                ab.wait_ready(ctx, "tensor")
                tb_bf = tb.ps.bitcast(BF16)
                for c in range(CH):
                    ctx.op("tensor",
                           lambda E, o=tb_bf[:, c * 128:(c + 1) * 128],
                           i=ab.ap[:, c * 128:(c + 1) * 128]:
                           E.transpose(o, i, eyeb[:, :]), pe if c == CH - 1 else None)
                tpv = pe.n
                ab.free = (pe, tpv)
                ctx.wait("vector", pe, tpv)
                if own:
                    dview = FM.rearrange("p (c t) -> p c t", c=CH)[:, :, jj * 128:(jj + 1) * 128]
                else:
                    dview = afsb[jj % 2].ap.rearrange("p (c t) -> p c t", c=CH)
                vcopy(dview, tb_bf[:, 0:1024].rearrange("p (c t) -> p c t", c=CH), ve)
                tb.release(ve, ve.n)
                tdep = (ve, ve.n)
                if not own:
                    afsb[jj % 2].ready = tdep

                ctx.wait("tensor", tdep[0], tdep[1])
                if first_scr[0]:
                    ctx.wait("tensor", dqx, prep_loads)
                    first_scr[0] = False
                for grp in range(4):
                    b = pj_banks[(tc * 4 + grp) % 2]
                    b.acquire(ctx)
                    if jj == 0 and grp == 0:
                        ctx.wait("tensor", dqw, prep_loads_w)
                    for c in range(CH):
                        mm(b.ps[:, :], dst(c),
                           WA[:, c * 2048 + grp * 512:c * 2048 + (grp + 1) * 512],
                           start=(c == 0), stop=False)
                    mm(b.ps[:, :], onesr[:, :],
                       vbkvr[0:1, grp * 512:(grp + 1) * 512],
                       start=False, stop=True, inc=True)
                    mmv = pe.n
                    ctx.wait("vector", pe, mmv)
                    if grp < 2:
                        ts1("vector", tmpA[:, :], b.ps[:, :], 0.0, ALU.min, sem=ve)
                        ctx.wait("scalar", ve)
                        ctx.wait("scalar", pe, mmv)
                        act(tmpB[:, :], b.ps[:, :], AF.Relu, sem=se)
                        act(tmpA[:, :], tmpA[:, :], AF.Exp, sem=se)
                        ctx.wait("vector", se)
                        tt("vector",
                           phik[:, tc * 1024 + grp * 512:tc * 1024 + (grp + 1) * 512],
                           tmpA[:, :], tmpB[:, :], ALU.add, sem=ve)
                    else:
                        g2_ = grp - 2
                        vcopy(vtm[:, tc * 1024 + g2_ * 512:tc * 1024 + (g2_ + 1) * 512],
                              b.ps[:, :], ve)
                    b.release(ve, ve.n)
            phiv_done = ve.n

            ctx.wait("tensor", ve, phiv_done)
            for wave in range(2):
                for pi in range(4):
                    pair = wave * 4 + pi
                    b = kv_banks[pi]
                    b.acquire(ctx)
                    for tc in range(NTC):
                        mm(b.ps[:, 0:128],
                           phik[:, tc * 1024 + pair * 128:tc * 1024 + (pair + 1) * 128],
                           vtm[:, tc * 1024 + pair * 128:tc * 1024 + (pair + 1) * 128],
                           start=(tc == 0), stop=False)
                        mm(b.ps[:, 128:129],
                           phik[:, tc * 1024 + pair * 128:tc * 1024 + (pair + 1) * 128],
                           ones1[:, :], start=False, stop=(tc == NTC - 1),
                           inc=(tc == NTC - 1))
                    ctx.wait("vector", pe, pe.n)
                    tt("vector", kvsb[:, pair * 132:pair * 132 + 129],
                       kvsb[:, pair * 132:pair * 132 + 129], b.ps[:, 0:129],
                       ALU.add, sem=ve)
                    b.release(ve, ve.n)
        PHASE_MARKS["A"] = len(ctx.plan["tensor"])
        phaseA_pe = pe.n
        phaseA_ve = ve.n
        phaseA_se = se.n
        phaseA_tp = dqt.n

        ctx.wait("vector", ve, phaseA_ve)
        for pair in range(8):
            vcopy(kvs[0:64, pair * 128:pair * 128 + 64],
                  kvsb[0:64, pair * 132:pair * 132 + 64])
            vcopy(kvs[64:128, pair * 128 + 64:pair * 128 + 128],
                  kvsb[64:128, pair * 132 + 64:pair * 132 + 128])
            vcopy(zmat[0:64, pair * 16 + 2 * pair:pair * 16 + 2 * pair + 1],
                  kvsb[0:64, pair * 132 + 128:pair * 132 + 129])
            vcopy(zmat[64:128, pair * 16 + 2 * pair + 1:pair * 16 + 2 * pair + 2],
                  kvsb[64:128, pair * 132 + 128:pair * 132 + 129],
                  ve if pair == 7 else None)
        kv_final = ve.n

        if upto == "A":
            _finish(ctx, locals())
            _replay(ctx, nc)
            return nc

        ctx.wait("gpsimd", ve, phaseA_ve)
        ctx.wait("gpsimd", se, phaseA_se)
        ctx.dma("gpsimd", woS.rearrange("p (c n) -> p c n", c=CH),
                wod.rearrange("(c p) n -> p c n", p=128), dqw)
        wo_ld = dqw.n

        recb = Buf(recbc)
        hsb = Buf(hst)
        recb.hs_se = 0
        hsb.hs_se = 0
        pend_hs = [None]
        c_first = [True]
        c_state = {}

        def stage_q(cb):
            tok0 = cb * 512
            for m in range(CH):
                b = pj_banks[m % 2]
                b.acquire(ctx)
                for c in range(CH):
                    mm(b.ps[:, :], wqS[:, c * 1024 + m * 128:c * 1024 + (m + 1) * 128],
                       FM[:, c * S_OWN + tok0:c * S_OWN + tok0 + 512],
                       start=(c == 0), stop=(c == CH - 1), inc=(c == CH - 1))
                mmv = pe.n
                ctx.wait("vector", pe, mmv)
                if c_first[0]:
                    ctx.wait("vector", pe, phaseA_pe)
                    ctx.wait("scalar", pe, phaseA_pe)
                    ctx.wait("vector", dqt, phaseA_tp)
                    c_first[0] = False
                ts2("vector", ctmpA[:, :], b.ps[:, :], vbqc[:, m:m + 1], 0.0,
                    ALU.add, ALU.min, sem=ve)
                ctx.wait("scalar", ve)
                ctx.wait("scalar", pe, mmv)
                act(ctmpB[:, :], b.ps[:, :], AF.Relu, bias=vbqc[:, m:m + 1], sem=se)
                act(ctmpA[:, :], ctmpA[:, :], AF.Exp, sem=se)
                ctx.wait("vector", se)
                tt("vector", phiq[:, m * 512:(m + 1) * 512], ctmpA[:, :], ctmpB[:, :],
                   ALU.add, sem=ve)
                b.release(ve, ve.n)
                bden = banks[6]
                if m == 0:
                    bden.acquire(ctx)
                    ctx.wait("tensor", ve, kv_final)
                ctx.wait("tensor", ve, ve.n)
                mm(bden.ps[0:16, :], zmat[:, m * 16:(m + 1) * 16],
                   phiq[:, m * 512:(m + 1) * 512],
                   start=(m == 0), stop=(m == CH - 1), inc=(m == CH - 1))
            phiq_done = ve.n
            c_state[cb] = phiq_done

        def stage_attn(cb):
            phiq_done = c_state[cb]
            bden = banks[6]

            denv = pe.n
            num_banks = [banks[7], banks[0], banks[1], banks[2], banks[3],
                         banks[4], banks[5], banks[7]]
            for pair in range(7):
                bnum = num_banks[pair]
                bnum.acquire(ctx)
                mm(bnum.ps[:, :], kvs[:, pair * 128:(pair + 1) * 128],
                   phiq[:, pair * 512:(pair + 1) * 512], start=True, stop=True,
                   inc=(pair == 6))
            numv = pe.n
            ctx.wait("vector", pe, denv)
            ts1("vector", recsb[:, :], bden.ps[0:16, :], 1e-6, ALU.add, sem=ve)
            ctx.wait("vector", ve)
            recb.wait_free(ctx, "vector")
            ctx.op("vector", lambda E: E.reciprocal(recbc[0:16, :], recsb[:, :]), ve)
            ctx.wait("vector", ve)
            vcopy(rechi[:, :], recbc[0:16, :])
            tt("vector", reclo[:, :], recbc[0:16, :], rechi[:, :], ALU.subtract, sem=ve)
            recv = ve.n
            bden.release(ve, recv)
            ctx.wait("tensor", ve, recv)
            for pair in range(8):
                brc = bden
                brc.acquire(ctx)
                mm(brc.ps[:, :], onehb[:, pair * 128:(pair + 1) * 128], rechi[:, :],
                   start=True, stop=False)
                mm(brc.ps[:, :], onehb[:, pair * 128:(pair + 1) * 128], reclo[:, :],
                   start=False, stop=True, inc=True)
                rbv = pe.n
                bnum = num_banks[pair]
                if pair == 7:
                    bnum.acquire(ctx)
                    mm(bnum.ps[:, :], kvs[:, pair * 128:(pair + 1) * 128],
                       phiq[:, pair * 512:(pair + 1) * 512], start=True, stop=True,
                       inc=True)
                    ctx.wait("vector", pe, pe.n)
                else:
                    ctx.wait("vector", pe, rbv)
                recb.wait_free(ctx, "vector")
                vcopy(recbc[:, :], brc.ps[:, :], ve)
                brc.release(ve, ve.n)
                tt("vector", attnf[:, pair * 512:(pair + 1) * 512],
                   bnum.ps[:, :], recbc[:, :], ALU.mult, sem=ve)
                bnum.release(ve, ve.n)
                recb.free = (ve, ve.n)
            attn_done = ve.n
            c_state[cb] = attn_done

        def stage_wo(cb):
            attn_done = c_state[cb]
            ctx.wait("tensor", dqw, wo_ld)
            ctx.wait("tensor", ve, attn_done)
            for tc in range(4):
                jj = cb * 4 + tc
                if pend_hs[0] is not None:
                    pj, pbuf = pend_hs[0]
                    tbk = banks[2 + pj % 2]
                    tbk.acquire(ctx)
                    ctx.wait("tensor", se, pbuf.hs_se)
                    tb_bf = tbk.ps.bitcast(BF16)
                    pap = pbuf.ap.bitcast(BF16) if pbuf is recb else pbuf.ap
                    for c in range(CH):
                        ctx.op("tensor",
                               lambda E, o=tb_bf[:, c * 128:(c + 1) * 128],
                               i=pap[:, c * 128:(c + 1) * 128]:
                               E.transpose(o, i, eyeb[:, :]), pe if c == CH - 1 else None)
                    tpv = pe.n
                    pbuf.free = (pe, tpv)
                    ctx.wait("vector", pe, tpv)
                    dview = FM.rearrange("p (c t) -> p c t", c=CH)[:, :, pj * 128:(pj + 1) * 128]
                    vcopy(dview, tb_bf[:, 0:1024].rearrange("p (c t) -> p c t", c=CH), ve)
                    tbk.release(ve, ve.n)
                    pend_hs[0] = None
                for grp in range(2):
                    b = pj_banks[grp]
                    b.acquire(ctx)
                    for c in range(CH):
                        mm(b.ps[:, :],
                           attnf[:, c * 512 + tc * 128:c * 512 + (tc + 1) * 128],
                           woS[:, c * 1024 + grp * 512:c * 1024 + (grp + 1) * 512],
                           start=(c == 0), stop=(c == CH - 1), inc=(c == CH - 1))
                    mmv = pe.n
                    ctx.wait("vector", pe, mmv)
                    rsl = RES[:, jj * D + grp * 512:jj * D + (grp + 1) * 512]
                    tt("vector", rsl, b.ps[:, :], rsl, ALU.add, sem=ve)
                    b.release(ve, ve.n)
                mvc = colA if jj % 2 == 0 else colB
                rtile = RES[:, jj * D:(jj + 1) * D]
                ctx.wait("vector", ve)
                ctx.wait("vector", se)
                ctx.op("vector", lambda E, o=mvc, r=rtile: E.bn_stats(o[:, 4:10], r[:, 0:512]))
                ctx.op("vector", lambda E, o=mvc, r=rtile: E.bn_stats(o[:, 10:16], r[:, 512:1024]), ve)
                ctx.wait("vector", ve)
                ctx.op("vector", lambda E, o=mvc: E.bn_aggr(o[:, 0:2], o[:, 4:16]), ve)
                ctx.wait("vector", ve)
                ts1("vector", mvc[:, 2:3], mvc[:, 1:2], EPS, ALU.add, sem=ve)
                ctx.wait("scalar", ve)
                act(mvc[:, 2:3], mvc[:, 2:3], AF.Ln, sem=se)
                ctx.wait("scalar", se)
                act(mvc[:, 3:4], mvc[:, 2:3], AF.Exp, scale=-0.5, sem=se)
                ctx.wait("vector", se)
                stt("vector", mvc[:, 2:3], mvc[:, 0:1], -1.0, mvc[:, 3:4],
                    ALU.mult, ALU.mult, sem=ve)
                hbuf = hsb if jj % 2 == 0 else recb
                hbuf.wait_free(ctx, "scalar")
                ctx.wait("scalar", ve)
                hap = hbuf.ap.bitcast(BF16) if hbuf is recb else hbuf.ap
                act(hap[:, :], rtile, AF.Identity, bias=mvc[:, 2:3],
                    scale=mvc[:, 3:4], sem=se)
                hbuf.hs_se = se.n
                pend_hs[0] = (jj, hbuf)
            pj, pbuf = pend_hs[0]
            tbk = banks[2 + pj % 2]
            tbk.acquire(ctx)
            ctx.wait("tensor", se, pbuf.hs_se)
            tb_bf = tbk.ps.bitcast(BF16)
            pap = pbuf.ap.bitcast(BF16) if pbuf is recb else pbuf.ap
            for c in range(CH):
                ctx.op("tensor",
                       lambda E, o=tb_bf[:, c * 128:(c + 1) * 128],
                       i=pap[:, c * 128:(c + 1) * 128]:
                       E.transpose(o, i, eyeb[:, :]), pe if c == CH - 1 else None)
            tpv = pe.n
            pbuf.free = (pe, tpv)
            ctx.wait("vector", pe, tpv)
            dview = FM.rearrange("p (c t) -> p c t", c=CH)[:, :, pj * 128:(pj + 1) * 128]
            vcopy(dview, tb_bf[:, 0:1024].rearrange("p (c t) -> p c t", c=CH), ve)
            tbk.release(ve, ve.n)
            pend_hs[0] = None

        stage_q(0)
        stage_attn(0)
        for cb in range(4):
            if cb < 3:
                stage_q(cb + 1)
            stage_wo(cb)
            if cb < 3:
                stage_attn(cb + 1)

        PHASE_MARKS["C"] = len(ctx.plan["tensor"])
        hs_tp = ve.n
        hs_tp2 = 0
        phaseC_pe = pe.n
        phaseC_ve = ve.n
        phaseC_se = se.n

        if upto == "C":
            _finish(ctx, locals())
            _replay(ctx, nc)
            return nc

        fstgb = Buf(fstg)
        fstgb.free = (ve, phaseC_ve)
        vb1_banks = [banks[4], banks[5], banks[6], banks[7]]
        f_first = [True]

        def fold_w1G(G):
            ctx.wait("gpsimd", se, se.n)
            ctx.wait("gpsimd", ve, ve.n)
            for c in range(CH):
                fstgb.wait_free(ctx, "gpsimd")
                if f_first[0]:
                    ctx.wait("gpsimd", se, phaseC_se)
                    ctx.wait("gpsimd", ve, hs_tp)
                ctx.dma("gpsimd", fstg[:, :],
                        w1d[c * 128:(c + 1) * 128, G * 2048:(G + 1) * 2048], dqw)
                ctx.wait("vector", dqw, dqw.n)
                if f_first[0]:
                    ctx.wait("vector", pe, phaseC_pe)
                    ctx.wait("vector", ve, hs_tp)
                    f_first[0] = False
                ctx.wait("vector", pe)
                ts1("vector", WA[:, c * 2048:(c + 1) * 2048], fstg[:, :],
                    g2c[:, c:c + 1], ALU.mult)
                vcopy(g1G[:, 0:2048], fstg[:, :], ve)
                fstgb.free = (ve, ve.n)
                if c % 2 == 1:
                    qc = c // 2
                    if qc == 0 and G == 0:
                        ctx.wait("gpsimd", pe, phaseC_pe)
                    ctx.dma("gpsimd",
                            w2G[:, qc * 4096:(qc + 1) * 4096].rearrange("p (c n) -> p c n", c=4),
                            w2d[G * 2048 + qc * 512:G * 2048 + (qc + 1) * 512, :]
                            .rearrange("(c p) n -> p c n", p=128), dqw)
                ctx.wait("tensor", ve, ve.n)
                for g in range(4):
                    b = vb1_banks[g]
                    if c == 0:
                        b.acquire(ctx)
                    mm(b.ps[0:1, :], b2cb[:, c:c + 1], g1G[:, g * 512:(g + 1) * 512],
                       start=(c == 0), stop=(c == CH - 1), inc=(c == CH - 1))
            ctx.wait("vector", pe)
            for g in range(4):
                if g > 0:
                    ctx.wait("vector", dqt)
                vcopy(recsb[0:1, 0:512], vb1_banks[g].ps[0:1, :], ve)
                vb1_banks[g].release(ve, ve.n)
                ctx.wait("sync", ve, ve.n)
                ctx.dma("sync",
                        vbd[0:1, 3072 + G * 2048 + g * 512:3072 + G * 2048 + (g + 1) * 512],
                        recsb[0:1, 0:512], dqt)
            ctx.wait("scalar", dqt, dqt.n)
            ctx.dma("scalar", vbw1c[:, G * 16:(G + 1) * 16],
                    vbd[0:1, 3072 + G * 2048:3072 + (G + 1) * 2048]
                    .rearrange("a (c p) -> (a p) c", p=128), dqx, slow=True)
            ctx.wait("vector", dqx, dqx.n)
            ts1("vector", vbw1s[:, G * 16:(G + 1) * 16], vbw1c[:, G * 16:(G + 1) * 16],
                10.0, ALU.mult, sem=ve)
            return ve.n

        ftmp = [fstg[:, 0:512], fstg[:, 512:1024]]

        x0db = [Buf(fstg2[:, 0:1024]), Buf(fstg2[:, 1024:2048])]
        for bb in x0db:
            bb.free = [(se, se.n), (ve, ve.n)]
        gw_ld = [0]
        d0_pend = []

        def d0_tile(jj):
            xb = x0db[jj % 2]
            xb.wait_free(ctx, "scalar")
            ctx.wait("scalar", ve, ve.n)
            ctx.dma("scalar", xb.ap[:, :], xd[jj * 128:(jj + 1) * 128, :], dqx)
            xv = dqx.n
            ctx.wait("vector", dqx, xv)
            rtile = RES[:, jj * D:(jj + 1) * D]
            tt("vector", rtile, rtile, xb.ap[:, :], ALU.add, sem=ve)
            mvc = colA if jj % 2 == 0 else colB
            ctx.wait("vector", se)
            ctx.op("vector", lambda E, o=mvc, r=rtile: E.bn_stats(o[:, 4:10], r[:, 0:512]))
            ctx.op("vector", lambda E, o=mvc, r=rtile: E.bn_stats(o[:, 10:16], r[:, 512:1024]), ve)
            ctx.wait("vector", ve)
            ctx.op("vector", lambda E, o=mvc: E.bn_aggr(o[:, 0:2], o[:, 4:16]), ve)
            ctx.wait("vector", ve)
            ts1("vector", mvc[:, 2:3], mvc[:, 1:2], EPS, ALU.add, sem=ve)
            ctx.wait("scalar", ve)
            act(mvc[:, 2:3], mvc[:, 2:3], AF.Ln, sem=se)
            ctx.wait("scalar", se)
            act(mvc[:, 3:4], mvc[:, 2:3], AF.Exp, scale=-0.5, sem=se)
            ctx.wait("vector", se)
            stt("vector", mvc[:, 2:3], mvc[:, 0:1], -1.0, mvc[:, 3:4],
                ALU.mult, ALU.mult, sem=ve)
            t0ap = xb.ap.bitcast(BF16)[:, 0:1024]
            ctx.wait("scalar", ve)
            act(t0ap, rtile, AF.Identity, bias=mvc[:, 2:3], scale=mvc[:, 3:4], sem=se)
            d0_pend.append((jj, xb, t0ap, se.n))

        def d0_pe():
            jj, xb, t0ap, t0se = d0_pend.pop(0)
            tbk = banks[6 + jj % 2]
            tbk.acquire(ctx)
            ctx.wait("tensor", se, t0se)
            tb_bf = tbk.ps.bitcast(BF16)
            for c in range(CH):
                ctx.op("tensor",
                       lambda E, o=tb_bf[:, c * 128:(c + 1) * 128],
                       i=t0ap[:, c * 128:(c + 1) * 128]:
                       E.transpose(o, i, eyeb[:, :]), pe if c == CH - 1 else None)
            tpv = pe.n
            xb.free = (pe, tpv)
            ctx.wait("vector", pe, tpv)
            dview = FM.rearrange("p (c t) -> p c t", c=CH)[:, :, jj * 128:(jj + 1) * 128]
            vcopy(dview, tb_bf[:, 0:1024].rearrange("p (c t) -> p c t", c=CH), ve)
            t0fm_ve = ve.n
            tbk.release(ve, t0fm_ve)
            bg = banks[6 + jj % 2]
            bg.acquire(ctx)
            ctx.wait("tensor", ve, t0fm_ve)
            ctx.wait("tensor", dqw, gw_ld[0])
            for c in range(CH):
                mm(bg.ps[:, 0:8],
                   FM[:, c * S_OWN + jj * 128:c * S_OWN + (jj + 1) * 128],
                   gwsb[:, c * 8:(c + 1) * 8],
                   start=(c == 0), stop=(c == CH - 1), inc=(c == CH - 1))
            ctx.wait("vector", pe, pe.n)
            ctx.wait("vector", se)
            vcopy(scsb[:, 0:8], bg.ps[:, 0:8], ve)
            bg.release(ve, ve.n)
            ctx.wait("vector", ve)
            vred(sccol[:, 0:1], scsb[:, 0:8], ALU.max, sem=ve)
            ctx.wait("vector", ve)
            ts1("vector", sccol[:, 1:2], sccol[:, 0:1], -1.0, ALU.mult, sem=ve)
            ctx.wait("scalar", ve)
            act(scsb[:, 8:16], scsb[:, 0:8], AF.Exp, bias=sccol[:, 1:2], sem=se)
            ctx.wait("vector", se)
            vred(sccol[:, 2:3], scsb[:, 8:16], ALU.add, sem=ve)
            ctx.wait("vector", ve)
            ctx.op("vector", lambda E: E.reciprocal(sccol[:, 3:4], sccol[:, 2:3]), ve)
            ctx.wait("vector", ve)
            ts1("vector", scsb[:, 0:8], scsb[:, 8:16], sccol[:, 3:4], ALU.mult, sem=ve)
            ctx.wait("vector", ve)
            vred(sccol[:, 4:5], scsb[:, 0:8], ALU.max, sem=ve)
            ctx.wait("vector", ve)
            ts1("vector", scsb[:, 8:16], scsb[:, 0:8], sccol[:, 4:5], ALU.is_ge, sem=ve)
            ctx.wait("vector", ve)
            stt("vector", scsb[:, 8:16], scsb[:, 8:16], -1e9, scsb[:, 0:8],
                ALU.mult, ALU.add, sem=ve)
            ctx.wait("vector", ve)
            vred(sccol[:, 5:6], scsb[:, 8:16], ALU.max, sem=ve)
            ctx.wait("vector", ve)
            ts1("vector", scsb[:, 8:16], scsb[:, 0:8], sccol[:, 5:6], ALU.is_ge, sem=ve)
            ctx.wait("vector", ve)
            tt("vector", wdsb[:, jj * 8:(jj + 1) * 8], scsb[:, 8:16], scsb[:, 0:8],
               ALU.mult, sem=ve)

        for G in range(2):
            vbv = fold_w1G(G)
            w2v = dqw.n
            if G == 1:
                ctx.dma("gpsimd", gwsb[:, :].rearrange("p (c n) -> p c n", c=CH),
                        gwd.rearrange("(c p) n -> p c n", p=128), dqw)
                gw_ld[0] = dqw.n
            for fb in range(4):
                tok0 = fb * 512
                w1_banks = [banks[0], banks[1], banks[4]]
                for cc in range(16):
                    b = w1_banks[cc % 3]
                    b.acquire(ctx)
                    ctx.wait("tensor", ve, vbv)
                    if G == 0 and fb == 0 and cc == 0:
                        ctx.wait("tensor", ve, hs_tp)
                    for c in range(CH):
                        mm(b.ps[:, :],
                           WA[:, c * 2048 + cc * 128:c * 2048 + (cc + 1) * 128],
                           FM[:, c * S_OWN + tok0:c * S_OWN + tok0 + 512],
                           start=(c == 0), stop=(c == CH - 1), inc=(c == CH - 1))
                    mmv = pe.n
                    tmp = ftmp[cc % 2]
                    ctx.wait("scalar", pe, mmv)
                    ctx.wait("scalar", ve)
                    act(tmp, b.ps[:, :], AF.Identity, scale=10.0,
                        bias=vbw1s[:, G * 16 + cc:G * 16 + cc + 1], sem=se)
                    b.release(se, se.n)
                    ctx.wait("vector", se)
                    ts2("vector", tmp, tmp, RC, -RC, ALU.add, ALU.add, sem=ve)
                    ctx.wait("vector", ve)
                    ts2("vector", tmp, tmp, 127.0, -128.0, ALU.min, ALU.max, sem=ve)
                    ctx.wait("scalar", ve)
                    act(g1G[:, cc * 512:(cc + 1) * 512], tmp,
                        AF.Identity if DBG_FFN_IDENT[0] else AF.Gelu_apprx_tanh,
                        scale=0.1, sem=se)
                g1_done = se.n
                ctx.wait("tensor", se, g1_done)
                ctx.wait("tensor", dqw, w2v)
                for tc in range(4):
                    jj = fb * 4 + tc
                    if d0_pend:
                        d0_pe()
                    for grp in range(2):
                        b = [banks[2], banks[3], banks[5]][(tc * 2 + grp) % 3]
                        b.acquire(ctx)
                        for cc in range(16):
                            mm(b.ps[:, :],
                               g1G[:, cc * 512 + tc * 128:cc * 512 + (tc + 1) * 128],
                               w2G[:, cc * 1024 + grp * 512:cc * 1024 + (grp + 1) * 512],
                               start=(cc == 0), stop=(cc == 15), inc=(cc == 15))
                        ctx.wait("vector", pe, pe.n)
                        rsl = RES[:, jj * D + grp * 512:jj * D + (grp + 1) * 512]
                        tt("vector", rsl, b.ps[:, :], rsl, ALU.add, sem=ve)
                        b.release(ve, ve.n)
                    if G == 1:
                        d0_tile(jj)
        ffn_pe = pe.n
        ffn_ve = ve.n

        PHASE_MARKS["F"] = len(ctx.plan["tensor"])
        while d0_pend:
            d0_pe()
        wd_ve = ve.n
        x1_ve = ve.n

        if upto == "F":
            _finish(ctx, locals())
            _replay(ctx, nc)
            return nc

        u_tms = [WB[:, 12288:13056], WB[:, 13824:14592]]
        u_fms = [WB[:, 13056:13824], WB[:, 14592:15360]]
        silu_bf = WB[:, 15360:16128]
        W1b = [Buf(W1xp_aps[0]), Buf(W1xp_aps[1])]
        W2b = [Buf(W2xp_aps[0]), Buf(W2xp_aps[1])]
        W1b[0].free = (pe, ffn_pe)
        W1b[1].free = [(ve, x1_ve), (pe, ffn_pe)]
        W2b[0].free = (ve, wd_ve)
        W2b[1].free = (ve, wd_ve)
        u_tmbs = [Buf(u_tms[0]), Buf(u_tms[1])]
        u_fmbs = [Buf(u_fms[0]), Buf(u_fms[1])]
        for bb in u_tmbs + u_fmbs:
            bb.free = [(ve, x1_ve), (pe, ffn_pe)]
        h_banks = [[banks[0], banks[1], banks[2]], [banks[3], banks[4], banks[5]]]
        eo_banks = [banks[6], banks[7]]

        for e0, wbx, dram, nch in ((0, W1b[0], W1xd, CH), (0, W2b[0], W2xd, 6)):
            wbx.wait_free(ctx, "gpsimd")
            ctx.dma("gpsimd", wbx.ap.rearrange("p (c n) -> p c n", c=nch),
                    dram[0].rearrange("(c p) n -> p c n", p=128), dqw)
            wbx.ready = (dqw, dqw.n)

        def prefetch_chunk(e1, which, c):
            if e1 >= NEXP:
                return
            wbx = (W1b if which == 1 else W2b)[e1 % 2]
            dram = W1xd if which == 1 else W2xd
            ncols = 1536 if which == 1 else 1024
            if c == 0:
                wbx.wait_free(ctx, "gpsimd")
            ctx.wait("gpsimd", pe, pe.n)
            ctx.dma("gpsimd", wbx.ap[:, c * ncols:(c + 1) * ncols],
                    dram[e1, c * 128:(c + 1) * 128, :], dqw)
            nmax = CH if which == 1 else 6
            if c == nmax - 1:
                wbx.ready = (dqw, dqw.n)

        for e in range(NEXP):
            wb1 = W1b[e % 2]
            wb2 = W2b[e % 2]
            wb1.wait_ready(ctx, "tensor")
            ctx.wait("tensor", ve, wd_ve)
            last_h = [None]
            last_eo = [None]
            pend = []

            def emit_eo(tca, ui):
                fb_ = u_fmbs[ui]
                fb_.wait_ready(ctx, "tensor")
                wb2.wait_ready(ctx, "tensor")
                for grp in range(2):
                    b = eo_banks[grp]
                    b.acquire(ctx)
                    for cc in range(6):
                        mm(b.ps[:, :], u_fms[ui][:, cc * 128:(cc + 1) * 128],
                           wb2.ap[:, cc * 1024 + grp * 512:cc * 1024 + (grp + 1) * 512],
                           start=(cc == 0), stop=(cc == 5), inc=(cc == 5))
                    mmv2 = pe.n
                    last_eo[0] = mmv2
                    ctx.wait("vector", pe, mmv2)
                    rsl = RES[:, tca * D + grp * 512:tca * D + (grp + 1) * 512]
                    tt("vector", rsl, b.ps[:, :], rsl, ALU.add, sem=ve)
                    b.release(ve, ve.n)
                fb_.free = (pe, last_eo[0])

            for tca in range(16):
                upi = tca % 2
                hb = h_banks[tca % 2]
                trio = trios[tca % 2]
                for ng in range(3):
                    b = hb[ng]
                    b.acquire(ctx)
                    for c in range(CH):
                        mm(b.ps[:, :],
                           FM[:, c * S_OWN + tca * 128:c * S_OWN + (tca + 1) * 128],
                           wb1.ap[:, c * 1536 + ng * 512:c * 1536 + (ng + 1) * 512],
                           start=(c == 0), stop=(c == CH - 1), inc=(c == CH - 1))
                mmv = pe.n
                last_h[0] = mmv
                if len(pend) >= 2:
                    tca_, ui_ = pend.pop(0)
                    emit_eo(tca_, ui_)
                ctx.wait("scalar", pe, mmv)
                ctx.wait("scalar", ve)
                act(silu_bf[:, 0:768], trio[:, 768:1536], AF.Silu, sem=se)
                sv = se.n
                ub = u_tmbs[upi]
                ub.wait_free(ctx, "vector")
                ctx.wait("vector", se, sv)
                ctx.wait("vector", pe, mmv)
                ut = u_tms[upi]
                if e < 8:
                    wdcol = wdsb[:, tca * 8 + e:tca * 8 + e + 1]
                    stt("vector", ut[:, 0:768], trio[:, 0:768], wdcol,
                        silu_bf[:, 0:768], ALU.mult, ALU.mult, sem=ve)
                else:
                    tt("vector", ut[:, 0:768], trio[:, 0:768],
                       silu_bf[:, 0:768], ALU.mult, sem=ve)
                uv = ve.n
                ub.ready = (ve, uv)
                hb[1].release(ve, uv)
                hb[2].release(ve, uv)
                fb_ = u_fmbs[upi]
                ctx.wait("tensor", ve, uv)
                tb_bf = hb[0].ps.bitcast(BF16)
                for c in range(6):
                    ctx.op("tensor",
                           lambda E, o=tb_bf[:, c * 128:(c + 1) * 128],
                           i=ut[:, c * 128:(c + 1) * 128]:
                           E.transpose(o, i, eyeb[:, :]), pe if c == 5 else None)
                tpv = pe.n
                ub.free = (pe, tpv)
                fb_.wait_free(ctx, "scalar")
                ctx.wait("scalar", pe, tpv)
                act(u_fms[upi][:, 0:768], tb_bf[:, 0:768], AF.Identity, sem=se)
                hb[0].release(se, se.n)
                fb_.ready = (se, se.n)
                pend.append((tca, upi))
                if tca % 2 == 1:
                    prefetch_chunk(e + 1, 1, tca // 2)
                elif 2 <= tca <= 12:
                    prefetch_chunk(e + 1, 2, tca // 2 - 1)
            while pend:
                tca_, ui_ = pend.pop(0)
                emit_eo(tca_, ui_)
            wb1.free = (pe, last_h[0])
            wb2.free = (pe, last_eo[0])
        _finish(ctx, locals())
        _replay(ctx, nc)
        return nc


def _finish(ctx, env):
    pe, ve, se = env["pe"], env["ve"], env["se"]
    dqo, dqt, dqx, dqw = env["dqo"], env["dqt"], env["dqx"], env["dqw"]
    RES, outd = env["RES"], env["outd"]
    for s in (pe, ve, se, dqt, dqx, dqw):
        ctx.wait("scalar", s)
    tap_aps = {
        "as_fm": env["FM"], "hsfm": env["FM"], "t0fm": env["FM"],
        "kvsb": env["kvsb"], "res": env["RES"], "phik": env["phik"],
        "vtm": env["vtm"], "phiq": env["phiq"], "attn": env["attnf"],
        "rec": env["recsb"], "wd": env["wdsb"], "vbq": env["vbqc"],
        "vbkv": env["VBKV"], "wkvs": env["WA"], "wqs": env["wqS"],
        "g1g": env["g1G"], "zmat": env["zmat"], "kvsq": env["kvs"],
    }
    nc = ctx.nc
    for t in env["taps"]:
        src = tap_aps[t]
        shp = [src.shape[0], src.shape[1]]
        dt = src.dtype
        td = nc.dram_tensor("tap_" + t, shp, dt, kind="ExternalOutput").ap()
        ctx.dma("scalar", td[:, :], src, dqo)
    for jj in range(16):
        ctx.dma("scalar", outd[jj * 128:(jj + 1) * 128, :],
                RES[:, jj * 1024:(jj + 1) * 1024], dqo)
    ctx.wait("scalar", dqo)
    ctx.wait("sync", dqo)


def _replay(ctx, nc):
    with nc.Block() as block:
        block.sync(lambda E: [f(E) for f in ctx.plan["sync"]])
        block.scalar(lambda E: [f(E) for f in ctx.plan["scalar"]])
        block.vector(lambda E: [f(E) for f in ctx.plan["vector"]])
        block.tensor(lambda E: [f(E) for f in ctx.plan["tensor"]])
        block.gpsimd(lambda E: [f(E) for f in ctx.plan["gpsimd"]])


def _host_layout(inputs):
    x = np.asarray(inputs["L0_attn_x"], np.float32)
    wq = np.ascontiguousarray(np.asarray(inputs["L0_attn_wq"], np.float32))
    wk = np.asarray(inputs["L0_attn_wk"], np.float32)
    wv = np.asarray(inputs["L0_attn_wv"], np.float32)
    wkv = np.ascontiguousarray(np.concatenate([wk, wv], axis=1))
    wo = np.ascontiguousarray(np.asarray(inputs["L0_attn_wo"], np.float32))
    w1 = np.ascontiguousarray(np.asarray(inputs["L0_attn_w1"], np.float32))
    w2 = np.ascontiguousarray(np.asarray(inputs["L0_attn_w2"], np.float32))
    gw = np.ascontiguousarray(np.asarray(inputs["L0_ffn_gate_w"], np.float32).T)
    W1 = np.asarray(inputs["L0_ffn_W1"], np.float32)
    W2 = np.asarray(inputs["L0_ffn_W2"], np.float32)
    Ws1 = np.asarray(inputs["L0_ffn_Ws1"], np.float32)
    Ws2 = np.asarray(inputs["L0_ffn_Ws2"], np.float32)
    W1t = W1.transpose(0, 2, 1)
    W2t = W2.transpose(0, 2, 1)
    Ws1t = Ws1.T
    Ws2t = Ws2.T
    W1x = np.zeros((NEXP, D, 1536), np.float32)
    W2x = np.zeros((NEXP, 768, D), np.float32)
    W1x[:8] = W1t
    W2x[:8] = W2t
    W1x[8] = np.concatenate([Ws1t[:, 0:768], Ws1t[:, 1536:2304]], axis=1)
    W1x[9] = np.concatenate([Ws1t[:, 768:1536], Ws1t[:, 2304:3072]], axis=1)
    W2x[8] = Ws2t[0:768]
    W2x[9] = Ws2t[768:1536]
    oneh = np.zeros((16, 1024), np.float32)
    for p_ in range(8):
        oneh[2 * p_, p_ * 128:p_ * 128 + 64] = 1.0
        oneh[2 * p_ + 1, p_ * 128 + 64:p_ * 128 + 128] = 1.0
    shared = dict(
        eye=np.eye(128, dtype=np.float32),
        oneh=oneh,
        wq=wq, wkv=wkv, wo=wo, w1=w1, w2=w2, gw=gw,
        W1x=np.ascontiguousarray(W1x), W2x=np.ascontiguousarray(W2x),
        g1=np.ascontiguousarray(np.asarray(inputs["L0_attn_ln1_g"], np.float32)),
        b1=np.ascontiguousarray(np.asarray(inputs["L0_attn_ln1_b"], np.float32)),
        g2=np.ascontiguousarray(np.asarray(inputs["L0_attn_ln2_g"], np.float32)),
        b2=np.ascontiguousarray(np.asarray(inputs["L0_attn_ln2_b"], np.float32)),
    )
    in_maps = []
    for c in range(8):
        b, h = c // 2, c % 2
        own = x[b, h * 2048:(h + 1) * 2048]
        oth = x[b, (1 - h) * 2048:(1 - h) * 2048 + 2048]
        m = dict(shared)
        m["x"] = np.ascontiguousarray(np.concatenate([own, oth], axis=0))
        in_maps.append(m)
    return in_maps


def kernel(**inputs):
    in_maps = _host_layout(inputs)
    nc = build_program()
    res = run_bass_kernel_spmd(nc, in_maps, core_ids=list(range(8)))
    out = np.zeros((4, 4096, 1024), np.float32)
    for c in range(8):
        b, h = c // 2, c % 2
        out[b, h * 2048:(h + 1) * 2048] = res.results[c]["out"]
    return out


# version 13
# speedup vs baseline: 1.1043x; 1.0774x over previous
import contextlib
import numpy as np

import concourse.bass as bass
import concourse.mybir as mybir
from concourse.bass_utils import run_bass_kernel_spmd

F32 = mybir.dt.float32
BF16 = mybir.dt.bfloat16
AF = mybir.ActivationFunctionType
ALU = mybir.AluOpType
AX = mybir.AxisListType

D = 1024
DBG_FFN_IDENT = [False]
PHASE_MARKS = {}
DBG_EO_INLINE = [False]
DBG_D1_TP_SYNC = [False]
DBG_D1_NOPING = [False]
DBG_D0_TP_SYNC = [False]
CH = 8
S_ALL = 4096
S_OWN = 2048
EPS = 1e-5
RC = 12582912.0
NEXP = 10

ENGS = ("sync", "scalar", "vector", "tensor", "gpsimd")


class SemT:
    __slots__ = ("h", "n")

    def __init__(self, h):
        self.h = h
        self.n = 0


class Ctx:
    def __init__(self, nc, es):
        self.nc = nc
        self.es = es
        self.plan = {e: [] for e in ENGS}
        self.lastwait = {}

    def sem(self, name):
        return SemT(self.es.enter_context(self.nc.semaphore(name)))

    def emit(self, eng, fn):
        self.plan[eng].append(fn)

    def wait(self, eng, sem, val=None):
        v = sem.n if val is None else val
        if v <= 0:
            return
        key = (eng, id(sem))
        if self.lastwait.get(key, -1) >= v:
            return
        self.lastwait[key] = v
        self.emit(eng, lambda E, s=sem.h, v=v: E.wait_ge(s, v))

    def dma(self, eng, out, in_, sem, transpose=False, slow=False):
        sem.n += 16

        def f(E, out=out, in_=in_, t=transpose, s=sem.h, sl=slow):
            E.dma_start(out=out, in_=in_, transpose=t,
                        allow_slow_non_contiguous=sl).then_inc(s, 16)

        self.emit(eng, f)
        return sem.n

    def op(self, eng, builder, sem=None):
        if sem is not None:
            sem.n += 1
        sh = sem.h if sem is not None else None

        def f(E, b=builder, sh=sh):
            i = b(E)
            if sh is not None:
                i.then_inc(sh, 1)

        self.emit(eng, f)
        return sem.n if sem is not None else None


class Bank:
    def __init__(self, ps):
        self.ps = ps
        self.free = None

    def acquire(self, ctx):
        if self.free is not None:
            ctx.wait("tensor", self.free[0], self.free[1])

    def release(self, sem, val):
        self.free = (sem, val)


class Buf:
    def __init__(self, ap):
        self.ap = ap
        self.free = None
        self.ready = None

    @staticmethod
    def _w(ctx, eng, deps):
        if deps is None:
            return
        if isinstance(deps, list):
            for s, v in deps:
                ctx.wait(eng, s, v)
        else:
            ctx.wait(eng, deps[0], deps[1])

    def wait_free(self, ctx, eng):
        self._w(ctx, eng, self.free)

    def wait_ready(self, ctx, eng):
        self._w(ctx, eng, self.ready)


def build_program(upto="E", taps=()):
    nc = bass.Bass("TRN2", target_bir_lowering=False)
    taps = list(taps)

    xd = nc.dram_tensor("x", [S_ALL, D], F32, kind="ExternalInput").ap()
    wqd = nc.dram_tensor("wq", [D, D], F32, kind="ExternalInput").ap()
    wkvd = nc.dram_tensor("wkv", [D, 2 * D], F32, kind="ExternalInput").ap()
    wod = nc.dram_tensor("wo", [D, D], F32, kind="ExternalInput").ap()
    w1d = nc.dram_tensor("w1", [D, 4096], F32, kind="ExternalInput").ap()
    w2d = nc.dram_tensor("w2", [4096, D], F32, kind="ExternalInput").ap()
    gwd = nc.dram_tensor("gw", [D, 8], F32, kind="ExternalInput").ap()
    W1xd = nc.dram_tensor("W1x", [NEXP, D, 1536], F32, kind="ExternalInput").ap()
    W2xd = nc.dram_tensor("W2x", [NEXP, 768, D], F32, kind="ExternalInput").ap()
    eyed = nc.dram_tensor("eye", [128, 128], F32, kind="ExternalInput").ap()
    onehd = nc.dram_tensor("oneh", [16, 1024], F32, kind="ExternalInput").ap()
    g1d = nc.dram_tensor("g1", [D], F32, kind="ExternalInput").ap()
    b1d = nc.dram_tensor("b1", [D], F32, kind="ExternalInput").ap()
    g2d = nc.dram_tensor("g2", [D], F32, kind="ExternalInput").ap()
    b2d = nc.dram_tensor("b2", [D], F32, kind="ExternalInput").ap()
    outd = nc.dram_tensor("out", [S_OWN, D], F32, kind="ExternalOutput").ap()
    vbd = nc.dram_tensor("vbd", [1, 8192], F32).ap()
    recd = nc.dram_tensor("recd", [16, 512], F32).ap()

    es = contextlib.ExitStack()
    with es:
        ctx = Ctx(nc, es)
        sb = lambda name, shape, dt: es.enter_context(
            nc.sbuf_tensor(name, shape, dt)).ap()

        dqx = ctx.sem("dqx")
        dqt = ctx.sem("dqt")
        dqt2 = ctx.sem("dqt2")
        dqw = ctx.sem("dqw")
        pe = ctx.sem("pe")
        ve = ctx.sem("ve")
        se = ctx.sem("se")
        ge = ctx.sem("ge")
        dqo = ctx.sem("dqo")

        RES = sb("RES", [128, 16 * D], F32)
        FM = sb("FM", [128, CH * S_OWN], BF16)
        g1c = sb("g1c", [128, 8], F32)
        b1c = sb("b1c", [128, 8], F32)
        g2c = sb("g2c", [128, 8], F32)
        b2c = sb("b2c", [128, 8], F32)
        b1cb = sb("b1cb", [128, 8], BF16)
        b2cb = sb("b2cb", [128, 8], BF16)
        vbqc = sb("vbqc", [128, 8], F32)
        vbw1c = sb("vbw1c", [128, 32], F32)
        vbw1s = sb("vbw1s", [128, 32], F32)
        ones1 = sb("ones1", [128, 1], BF16)
        eyeb = sb("eyeb", [128, 128], BF16)
        onehb = sb("onehb", [16, 1024], BF16)
        rechi = sb("rechi", [16, 512], BF16)
        reclo = sb("reclo", [16, 512], BF16)
        fstg2 = sb("fstg2", [128, 2048], F32)
        gwsb = sb("gwsb", [128, 64], BF16)
        kvsb = sb("kvsb", [128, 8 * 132], F32)
        kvs = sb("kvs", [128, 8 * 128], BF16)
        zmat = sb("zmat", [128, 8 * 16], BF16)
        wdsb = sb("wdsb", [128, 16 * 8], F32)
        colA = sb("colA", [128, 48], F32)
        colB = sb("colB", [128, 48], F32)
        recsb = sb("recsb", [16, 512], F32)
        scsb = sb("scsb", [128, 16], F32)
        sccol = sb("sccol", [128, 8], F32)

        WA = sb("WA", [128, 16384], BF16)
        WB = sb("WB", [128, 16384], BF16)
        SCR = sb("SCR", [128, 12288], BF16)

        wqS = WB[:, 0:8192]
        vbkvr = WB[0:1, 8192:10240]
        onesr = WB[0:1, 14336:14464]
        VBKV = WB[:, 8192:12288].bitcast(F32)
        tmpA = WB[:, 12288:13312].bitcast(F32)
        tmpB = WB[:, 13312:14336].bitcast(F32)
        woS = WB[:, 8192:16384]
        w2G = WB
        W1xp_aps = [WA[:, 0:12288], WB[:, 0:12288]]
        u_tm = WB[:, 12288:13824]
        u_tm = WB[:, 12288:13056]
        u_fm = WB[:, 13056:13824]
        silu_sb = WB[:, 13824:15360].bitcast(F32)

        x0p = [SCR[:, 0:2048].bitcast(F32), SCR[:, 2048:4096].bitcast(F32)]
        astp = [SCR[:, 4096:5120], SCR[:, 5120:6144]]
        afsp = [SCR[:, 6144:7168], SCR[:, 7168:8192]]
        phik = SCR[:, 8192:10240]
        vtm = SCR[:, 10240:12288]
        stg = [SCR[:, 0:4096].bitcast(F32), SCR[:, 4096:8192].bitcast(F32)]
        plainb = SCR[:, 8192:10240]
        phiq = SCR[:, 0:4096]
        attnf = SCR[:, 4096:8192]
        recbc = SCR[:, 8192:9216].bitcast(F32)
        ctmpA = SCR[:, 9216:10240].bitcast(F32)
        ctmpB = SCR[:, 10240:11264].bitcast(F32)
        hst = SCR[:, 11264:12288]
        g1G = SCR[:, 0:8192]
        fstg = SCR[:, 8192:12288].bitcast(F32)
        t0p = [SCR[:, 0:1024], SCR[:, 1024:2048]]
        gws = SCR[:, 2048:2112]
        W2xp_aps = [SCR[:, 0:6144], SCR[:, 6144:12288]]

        pt0 = es.enter_context(nc.psum_tensor("pt0", [128, 1536], F32)).ap()
        pt1 = es.enter_context(nc.psum_tensor("pt1", [128, 1536], F32)).ap()
        pb6 = es.enter_context(nc.psum_tensor("pb6", [128, 512], F32)).ap()
        pb7 = es.enter_context(nc.psum_tensor("pb7", [128, 512], F32)).ap()
        banks = [Bank(pt0[:, 0:512]), Bank(pt0[:, 512:1024]), Bank(pt0[:, 1024:1536]),
                 Bank(pt1[:, 0:512]), Bank(pt1[:, 512:1024]), Bank(pt1[:, 1024:1536]),
                 Bank(pb6), Bank(pb7)]
        trios = [pt0, pt1]

        def mm(out_ap, lhsT, rhs, start, stop, inc=False):
            return ctx.op("tensor",
                          lambda E, o=out_ap, l=lhsT, r=rhs, st=start, sp=stop:
                          E.matmul(o, l, r, start=st, stop=sp),
                          pe if inc else None)

        def ts1(eng, out, in0, s1, op0, sem=None):
            return ctx.op(eng, lambda E, o=out, i=in0, a=s1, p=op0:
                          E.tensor_single_scalar(o, i, a, p), sem)

        def ts2(eng, out, in0, s1, s2, op0, op1, sem=None):
            return ctx.op(eng, lambda E, o=out, i=in0, a=s1, b=s2, p=op0, q=op1:
                          E.tensor_scalar(o, i, a, b, p, q), sem)

        def tt(eng, out, in0, in1, op, sem=None):
            return ctx.op(eng, lambda E, o=out, a=in0, b=in1, p=op:
                          E.tensor_tensor(o, a, b, p), sem)

        def stt(eng, out, in0, scal, in1, op0, op1, sem=None):
            return ctx.op(eng, lambda E, o=out, a=in0, s=scal, b=in1, p=op0, q=op1:
                          E.scalar_tensor_tensor(o, a, s, b, p, q), sem)

        def act(out, in_, func, bias=0.0, scale=1.0, sem=None):
            return ctx.op("scalar", lambda E, o=out, i=in_, f=func, b=bias, s=scale:
                          E.activation(o, i, f, bias=b, scale=s), sem)

        def vcopy(out, in_, sem=None):
            return ctx.op("vector", lambda E, o=out, i=in_: E.tensor_copy(o, i), sem)

        def vred(out, in_, op, sem=None):
            return ctx.op("vector", lambda E, o=out, i=in_, p=op:
                          E.tensor_reduce(o, i, AX.X, p), sem)

        def tp_split(pairs, split=False):
            for i, (dst, srcp) in enumerate(pairs):
                if split and i % 2 == 1:
                    ctx.dma("scalar", dst, srcp, dqt2, transpose=True)
                else:
                    ctx.dma("sync", dst, srcp, dqt, transpose=True)
            return [(dqt, dqt.n), (dqt2, dqt2.n)]

        for dst, src in ((g1c, g1d), (b1c, b1d), (g2c, g2d), (b2c, b2d)):
            ctx.dma("scalar", dst[:, :], src.rearrange("(c p) -> p c", p=128), dqx, slow=True)
        ctx.wait("vector", dqx, dqx.n)
        vcopy(b1cb[:, :], b1c[:, :])
        vcopy(b2cb[:, :], b2c[:, :])
        ctx.op("vector", lambda E: E.memset(ones1[:, :], 1.0))
        ctx.op("vector", lambda E: E.memset(onesr[:, :], 1.0))
        ctx.op("vector", lambda E: E.memset(kvsb[:, :], 0.0))
        ctx.op("vector", lambda E: E.memset(kvs[:, :], 0.0))
        ctx.op("vector", lambda E: E.memset(zmat[:, :], 0.0), ve)

        ctx.dma("gpsimd", eyeb[:, :], eyed[:, :], dqw)
        ctx.dma("gpsimd", onehb[:, :], onehd[:, :], dqw)
        stgb = [Buf(stg[0]), Buf(stg[1])]
        vb_banks_q = [banks[0], banks[1]]
        vb_banks_kv = [banks[2], banks[3], banks[4], banks[5]]

        def prep_fold(wsrc, ncols, dst_ap, gcol, vb_banks, bvec):
            for c in range(CH):
                s = stgb[c % 2]
                s.wait_free(ctx, "gpsimd")
                ctx.dma("gpsimd", s.ap[:, 0:ncols], wsrc[c * 128:(c + 1) * 128, :], dqw)
                ctx.wait("vector", dqw, dqw.n)
                ctx.wait("vector", pe)
                ts1("vector", dst_ap[:, c * ncols:(c + 1) * ncols], s.ap[:, 0:ncols],
                    gcol[:, c:c + 1], ALU.mult)
                vcopy(plainb[:, 0:ncols], s.ap[:, 0:ncols], ve)
                s.free = (ve, ve.n)
                ctx.wait("tensor", ve, ve.n)
                for g in range(ncols // 512):
                    b = vb_banks[g]
                    if c == 0:
                        b.acquire(ctx)
                    mm(b.ps[0:1, :], bvec[:, c:c + 1], plainb[:, g * 512:(g + 1) * 512],
                       start=(c == 0), stop=(c == CH - 1), inc=(c == CH - 1))

        prep_fold(wqd, D, wqS, g1c, vb_banks_q, b1cb)
        prep_fold(wkvd, 2 * D, WA, g1c, vb_banks_kv, b1cb)

        ctx.wait("vector", pe)
        for i, b in enumerate(vb_banks_q + vb_banks_kv):
            if i > 0:
                ctx.wait("vector", dqt)
            vcopy(tmpA[0:1, 0:512], b.ps[0:1, :], ve)
            b.release(ve, ve.n)
            ctx.wait("sync", ve, ve.n)
            ctx.dma("sync", vbd[0:1, i * 512:(i + 1) * 512], tmpA[0:1, 0:512], dqt)
        vb_evac = ve.n
        ctx.wait("scalar", dqt, dqt.n)
        ctx.dma("scalar", vbqc[:, :],
                vbd[0:1, 0:1024].rearrange("a (c p) -> (a p) c", p=128), dqx, slow=True)
        ctx.wait("gpsimd", dqt, dqt.n)
        ctx.dma("gpsimd", vbkvr[:, :], vbd[0:1, 1024:3072], dqw)
        prep_loads = dqx.n
        prep_loads_w = dqw.n
        PHASE_MARKS["prep"] = len(ctx.plan["tensor"])
        prep_pe = pe.n
        prep_ve = ve.n

        NTC = 2
        ABLK = 16
        x0b = [Buf(x0p[0]), Buf(x0p[1]),
               Buf(fstg2[:, 0:1024]), Buf(fstg2[:, 1024:2048])]
        astb = [Buf(astp[0]), Buf(astp[1])]
        afsb = [Buf(afsp[0]), Buf(afsp[1])]
        for bb in x0b + astb + afsb:
            bb.free = (pe, prep_pe)
        kv_banks = [banks[3], banks[4], banks[5]]
        pj_banks = [banks[0], banks[1], banks[2]]
        first_scr = [True]

        for blk in range(ABLK):
            own = blk < 8
            mvc = colA if blk % 2 == 0 else colB
            for tc in range(NTC):
                jj = blk * NTC + tc
                xb = x0b[jj % 4]
                xb.wait_free(ctx, "scalar")
                if first_scr[0]:
                    ctx.wait("scalar", ve, prep_ve)
                ctx.dma("scalar", xb.ap[:, :], xd[jj * 128:(jj + 1) * 128, :], dqx)
                xb.ready = (dqx, dqx.n)
                xb.wait_ready(ctx, "vector")
                if blk >= 2:
                    ctx.wait("vector", se)
                bnc = 4 + tc * 12
                ctx.op("vector", lambda E, x=xb.ap, o=mvc, b=bnc:
                       E.bn_stats(o[:, b:b + 6], x[:, 0:512]))
                ctx.op("vector", lambda E, x=xb.ap, o=mvc, b=bnc:
                       E.bn_stats(o[:, b + 6:b + 12], x[:, 512:1024]), ve)
                ctx.wait("vector", ve)
                ctx.op("vector", lambda E, o=mvc, tc=tc, b=bnc:
                       E.bn_aggr(o[:, tc * 2:tc * 2 + 2], o[:, b:b + 12]), ve)
                ctx.wait("vector", ve)
            m_ = mvc[:, 0:2 * NTC:2]
            v_ = mvc[:, 1:2 * NTC:2]
            t4_ = mvc[:, 28:28 + NTC]
            t1_ = mvc[:, 28 + NTC:28 + 2 * NTC]
            s01_ = mvc[:, 32:32 + NTC]
            rs1_ = mvc[:, 32 + NTC:32 + 2 * NTC]
            nms_ = mvc[:, 36:36 + NTC]
            nm1_ = mvc[:, 38:38 + NTC]
            ts2("vector", t4_, v_, 1.0 + EPS, EPS * EPS, ALU.mult, ALU.add)
            ts1("vector", t1_, v_, EPS, ALU.add, sem=ve)
            ctx.wait("scalar", ve)
            act(mvc[:, 28:28 + 2 * NTC], mvc[:, 28:28 + 2 * NTC], AF.Ln, sem=se)
            ctx.wait("scalar", se)
            act(mvc[:, 32:32 + 2 * NTC], mvc[:, 28:28 + 2 * NTC], AF.Exp,
                scale=-0.5, sem=se)
            ctx.wait("vector", se)
            stt("vector", nms_, m_, -1.0, s01_, ALU.mult, ALU.mult, sem=ve)
            if own:
                stt("vector", nm1_, m_, -1.0, rs1_, ALU.mult, ALU.mult, sem=ve)
            cols_ve = ve.n

            for tc in range(NTC):
                jj = blk * NTC + tc
                xb = x0b[jj % 4]
                ab = astb[jj % 2]
                ab.wait_free(ctx, "scalar")
                ctx.wait("scalar", ve, cols_ve)
                act(ab.ap[:, :], xb.ap[:, :], AF.Identity,
                    bias=nms_[:, tc:tc + 1], scale=s01_[:, tc:tc + 1], sem=se)
                ab.ready = (se, se.n)
                if own:
                    act(RES[:, jj * D:(jj + 1) * D], xb.ap[:, :], AF.Identity,
                        bias=nm1_[:, tc:tc + 1], scale=rs1_[:, tc:tc + 1], sem=se)
                xb.free = (se, se.n)
                if own:
                    def fmdst(c, jj=jj):
                        return FM[:, c * S_OWN + jj * 128:c * S_OWN + (jj + 1) * 128]
                    dst = fmdst
                else:
                    fs = afsb[jj % 2]
                    fs.wait_free(ctx, "sync")

                    def fmdst(c, fs=fs):
                        return fs.ap[:, c * 128:(c + 1) * 128]
                    dst = fmdst
                tb = banks[6 + jj % 2]
                tb.acquire(ctx)
                ab.wait_ready(ctx, "tensor")
                tb_bf = tb.ps.bitcast(BF16)
                for c in range(CH):
                    ctx.op("tensor",
                           lambda E, o=tb_bf[:, c * 128:(c + 1) * 128],
                           i=ab.ap[:, c * 128:(c + 1) * 128]:
                           E.transpose(o, i, eyeb[:, :]), pe if c == CH - 1 else None)
                tpv = pe.n
                ab.free = (pe, tpv)
                ctx.wait("vector", pe, tpv)
                if own:
                    dview = FM.rearrange("p (c t) -> p c t", c=CH)[:, :, jj * 128:(jj + 1) * 128]
                else:
                    dview = afsb[jj % 2].ap.rearrange("p (c t) -> p c t", c=CH)
                vcopy(dview, tb_bf[:, 0:1024].rearrange("p (c t) -> p c t", c=CH), ve)
                tb.release(ve, ve.n)
                tdep = (ve, ve.n)
                if not own:
                    afsb[jj % 2].ready = tdep

                ctx.wait("tensor", tdep[0], tdep[1])
                if first_scr[0]:
                    ctx.wait("tensor", dqx, prep_loads)
                    first_scr[0] = False
                for grp in range(4):
                    b = pj_banks[(tc * 4 + grp) % 3]
                    b.acquire(ctx)
                    if jj == 0 and grp == 0:
                        ctx.wait("tensor", dqw, prep_loads_w)
                    for c in range(CH):
                        mm(b.ps[:, :], dst(c),
                           WA[:, c * 2048 + grp * 512:c * 2048 + (grp + 1) * 512],
                           start=(c == 0), stop=False)
                    mm(b.ps[:, :], onesr[:, :],
                       vbkvr[0:1, grp * 512:(grp + 1) * 512],
                       start=False, stop=True, inc=True)
                    mmv = pe.n
                    ctx.wait("vector", pe, mmv)
                    if grp < 2:
                        ts1("vector", tmpA[:, :], b.ps[:, :], 0.0, ALU.min, sem=ve)
                        ctx.wait("scalar", ve)
                        ctx.wait("scalar", pe, mmv)
                        act(tmpB[:, :], b.ps[:, :], AF.Relu, sem=se)
                        act(tmpA[:, :], tmpA[:, :], AF.Exp, sem=se)
                        ctx.wait("vector", se)
                        tt("vector",
                           phik[:, tc * 1024 + grp * 512:tc * 1024 + (grp + 1) * 512],
                           tmpA[:, :], tmpB[:, :], ALU.add, sem=ve)
                    else:
                        g2_ = grp - 2
                        vcopy(vtm[:, tc * 1024 + g2_ * 512:tc * 1024 + (g2_ + 1) * 512],
                              b.ps[:, :], ve)
                    b.release(ve, ve.n)
            phiv_done = ve.n

            ctx.wait("tensor", ve, phiv_done)
            for pair in range(8):
                    b = kv_banks[pair % 3]
                    b.acquire(ctx)
                    for tc in range(NTC):
                        mm(b.ps[:, 0:128],
                           phik[:, tc * 1024 + pair * 128:tc * 1024 + (pair + 1) * 128],
                           vtm[:, tc * 1024 + pair * 128:tc * 1024 + (pair + 1) * 128],
                           start=(tc == 0), stop=False)
                        mm(b.ps[:, 128:129],
                           phik[:, tc * 1024 + pair * 128:tc * 1024 + (pair + 1) * 128],
                           ones1[:, :], start=False, stop=(tc == NTC - 1),
                           inc=(tc == NTC - 1))
                    ctx.wait("vector", pe, pe.n)
                    tt("vector", kvsb[:, pair * 132:pair * 132 + 129],
                       kvsb[:, pair * 132:pair * 132 + 129], b.ps[:, 0:129],
                       ALU.add, sem=ve)
                    b.release(ve, ve.n)
        PHASE_MARKS["A"] = len(ctx.plan["tensor"])
        phaseA_pe = pe.n
        phaseA_ve = ve.n
        phaseA_se = se.n
        phaseA_tp = dqt.n

        ctx.wait("vector", ve, phaseA_ve)
        for pair in range(8):
            vcopy(kvs[0:64, pair * 128:pair * 128 + 64],
                  kvsb[0:64, pair * 132:pair * 132 + 64])
            vcopy(kvs[64:128, pair * 128 + 64:pair * 128 + 128],
                  kvsb[64:128, pair * 132 + 64:pair * 132 + 128])
            vcopy(zmat[0:64, pair * 16 + 2 * pair:pair * 16 + 2 * pair + 1],
                  kvsb[0:64, pair * 132 + 128:pair * 132 + 129])
            vcopy(zmat[64:128, pair * 16 + 2 * pair + 1:pair * 16 + 2 * pair + 2],
                  kvsb[64:128, pair * 132 + 128:pair * 132 + 129],
                  ve if pair == 7 else None)
        kv_final = ve.n

        if upto == "A":
            _finish(ctx, locals())
            _replay(ctx, nc)
            return nc

        ctx.wait("gpsimd", ve, phaseA_ve)
        ctx.wait("gpsimd", se, phaseA_se)
        ctx.dma("gpsimd", woS.rearrange("p (c n) -> p c n", c=CH),
                wod.rearrange("(c p) n -> p c n", p=128), dqw)
        wo_ld = dqw.n

        recb = Buf(recbc)
        hsb = Buf(hst)
        recb.hs_se = 0
        hsb.hs_se = 0
        pend_hs = [None]
        c_first = [True]
        c_state = {}

        def stage_q(cb):
            tok0 = cb * 512
            for m in range(CH):
                b = banks[m % 3]
                b.acquire(ctx)
                for c in range(CH):
                    mm(b.ps[:, :], wqS[:, c * 1024 + m * 128:c * 1024 + (m + 1) * 128],
                       FM[:, c * S_OWN + tok0:c * S_OWN + tok0 + 512],
                       start=(c == 0), stop=(c == CH - 1), inc=(c == CH - 1))
                mmv = pe.n
                ctx.wait("vector", pe, mmv)
                if c_first[0]:
                    ctx.wait("vector", pe, phaseA_pe)
                    ctx.wait("scalar", pe, phaseA_pe)
                    ctx.wait("vector", dqt, phaseA_tp)
                    c_first[0] = False
                ts2("vector", ctmpA[:, :], b.ps[:, :], vbqc[:, m:m + 1], 0.0,
                    ALU.add, ALU.min, sem=ve)
                ctx.wait("scalar", ve)
                ctx.wait("scalar", pe, mmv)
                act(ctmpB[:, :], b.ps[:, :], AF.Relu, bias=vbqc[:, m:m + 1], sem=se)
                act(ctmpA[:, :], ctmpA[:, :], AF.Exp, sem=se)
                ctx.wait("vector", se)
                tt("vector", phiq[:, m * 512:(m + 1) * 512], ctmpA[:, :], ctmpB[:, :],
                   ALU.add, sem=ve)
                b.release(ve, ve.n)
                bden = banks[6]
                if m == 0:
                    bden.acquire(ctx)
                    ctx.wait("tensor", ve, kv_final)
                ctx.wait("tensor", ve, ve.n)
                mm(bden.ps[0:16, :], zmat[:, m * 16:(m + 1) * 16],
                   phiq[:, m * 512:(m + 1) * 512],
                   start=(m == 0), stop=(m == CH - 1), inc=(m == CH - 1))
            phiq_done = ve.n
            c_state[cb] = phiq_done

        def stage_attn(cb):
            phiq_done = c_state[cb]
            bden = banks[6]

            denv = pe.n
            num_banks = [banks[7], banks[0], banks[1], banks[2], banks[3],
                         banks[4], banks[5], banks[7]]
            for pair in range(7):
                bnum = num_banks[pair]
                bnum.acquire(ctx)
                mm(bnum.ps[:, :], kvs[:, pair * 128:(pair + 1) * 128],
                   phiq[:, pair * 512:(pair + 1) * 512], start=True, stop=True,
                   inc=(pair == 6))
            numv = pe.n
            ctx.wait("vector", pe, denv)
            ts1("vector", recsb[:, :], bden.ps[0:16, :], 1e-6, ALU.add, sem=ve)
            ctx.wait("vector", ve)
            recb.wait_free(ctx, "vector")
            ctx.op("vector", lambda E: E.reciprocal(recbc[0:16, :], recsb[:, :]), ve)
            ctx.wait("vector", ve)
            vcopy(rechi[:, :], recbc[0:16, :])
            tt("vector", reclo[:, :], recbc[0:16, :], rechi[:, :], ALU.subtract, sem=ve)
            recv = ve.n
            bden.release(ve, recv)
            ctx.wait("tensor", ve, recv)
            for pair in range(8):
                brc = bden
                brc.acquire(ctx)
                mm(brc.ps[:, :], onehb[:, pair * 128:(pair + 1) * 128], rechi[:, :],
                   start=True, stop=False)
                mm(brc.ps[:, :], onehb[:, pair * 128:(pair + 1) * 128], reclo[:, :],
                   start=False, stop=True, inc=True)
                rbv = pe.n
                bnum = num_banks[pair]
                if pair == 7:
                    bnum.acquire(ctx)
                    mm(bnum.ps[:, :], kvs[:, pair * 128:(pair + 1) * 128],
                       phiq[:, pair * 512:(pair + 1) * 512], start=True, stop=True,
                       inc=True)
                    ctx.wait("vector", pe, pe.n)
                else:
                    ctx.wait("vector", pe, rbv)
                recb.wait_free(ctx, "vector")
                vcopy(recbc[:, :], brc.ps[:, :], ve)
                brc.release(ve, ve.n)
                tt("vector", attnf[:, pair * 512:(pair + 1) * 512],
                   bnum.ps[:, :], recbc[:, :], ALU.mult, sem=ve)
                bnum.release(ve, ve.n)
                recb.free = (ve, ve.n)
            attn_done = ve.n
            c_state[cb] = attn_done

        def stage_wo(cb):
            attn_done = c_state[cb]
            ctx.wait("tensor", dqw, wo_ld)
            ctx.wait("tensor", ve, attn_done)
            for tc in range(4):
                jj = cb * 4 + tc
                if pend_hs[0] is not None:
                    pj, pbuf = pend_hs[0]
                    tbk = banks[2 + pj % 2]
                    tbk.acquire(ctx)
                    ctx.wait("tensor", se, pbuf.hs_se)
                    tb_bf = tbk.ps.bitcast(BF16)
                    pap = pbuf.ap.bitcast(BF16) if pbuf is recb else pbuf.ap
                    for c in range(CH):
                        ctx.op("tensor",
                               lambda E, o=tb_bf[:, c * 128:(c + 1) * 128],
                               i=pap[:, c * 128:(c + 1) * 128]:
                               E.transpose(o, i, eyeb[:, :]), pe if c == CH - 1 else None)
                    tpv = pe.n
                    pbuf.free = (pe, tpv)
                    ctx.wait("vector", pe, tpv)
                    dview = FM.rearrange("p (c t) -> p c t", c=CH)[:, :, pj * 128:(pj + 1) * 128]
                    vcopy(dview, tb_bf[:, 0:1024].rearrange("p (c t) -> p c t", c=CH), ve)
                    tbk.release(ve, ve.n)
                    pend_hs[0] = None
                for grp in range(2):
                    b = pj_banks[grp]
                    b.acquire(ctx)
                    for c in range(CH):
                        mm(b.ps[:, :],
                           attnf[:, c * 512 + tc * 128:c * 512 + (tc + 1) * 128],
                           woS[:, c * 1024 + grp * 512:c * 1024 + (grp + 1) * 512],
                           start=(c == 0), stop=(c == CH - 1), inc=(c == CH - 1))
                    mmv = pe.n
                    ctx.wait("vector", pe, mmv)
                    rsl = RES[:, jj * D + grp * 512:jj * D + (grp + 1) * 512]
                    tt("vector", rsl, b.ps[:, :], rsl, ALU.add, sem=ve)
                    b.release(ve, ve.n)
                mvc = colA if jj % 2 == 0 else colB
                rtile = RES[:, jj * D:(jj + 1) * D]
                ctx.wait("vector", ve)
                ctx.wait("vector", se)
                ctx.op("vector", lambda E, o=mvc, r=rtile: E.bn_stats(o[:, 4:10], r[:, 0:512]))
                ctx.op("vector", lambda E, o=mvc, r=rtile: E.bn_stats(o[:, 10:16], r[:, 512:1024]), ve)
                ctx.wait("vector", ve)
                ctx.op("vector", lambda E, o=mvc: E.bn_aggr(o[:, 0:2], o[:, 4:16]), ve)
                ctx.wait("vector", ve)
                ts1("vector", mvc[:, 2:3], mvc[:, 1:2], EPS, ALU.add, sem=ve)
                ctx.wait("scalar", ve)
                act(mvc[:, 2:3], mvc[:, 2:3], AF.Ln, sem=se)
                ctx.wait("scalar", se)
                act(mvc[:, 3:4], mvc[:, 2:3], AF.Exp, scale=-0.5, sem=se)
                ctx.wait("vector", se)
                stt("vector", mvc[:, 2:3], mvc[:, 0:1], -1.0, mvc[:, 3:4],
                    ALU.mult, ALU.mult, sem=ve)
                hbuf = hsb if jj % 2 == 0 else recb
                hbuf.wait_free(ctx, "scalar")
                ctx.wait("scalar", ve)
                hap = hbuf.ap.bitcast(BF16) if hbuf is recb else hbuf.ap
                act(hap[:, :], rtile, AF.Identity, bias=mvc[:, 2:3],
                    scale=mvc[:, 3:4], sem=se)
                hbuf.hs_se = se.n
                pend_hs[0] = (jj, hbuf)
            pj, pbuf = pend_hs[0]
            tbk = banks[2 + pj % 2]
            tbk.acquire(ctx)
            ctx.wait("tensor", se, pbuf.hs_se)
            tb_bf = tbk.ps.bitcast(BF16)
            pap = pbuf.ap.bitcast(BF16) if pbuf is recb else pbuf.ap
            for c in range(CH):
                ctx.op("tensor",
                       lambda E, o=tb_bf[:, c * 128:(c + 1) * 128],
                       i=pap[:, c * 128:(c + 1) * 128]:
                       E.transpose(o, i, eyeb[:, :]), pe if c == CH - 1 else None)
            tpv = pe.n
            pbuf.free = (pe, tpv)
            ctx.wait("vector", pe, tpv)
            dview = FM.rearrange("p (c t) -> p c t", c=CH)[:, :, pj * 128:(pj + 1) * 128]
            vcopy(dview, tb_bf[:, 0:1024].rearrange("p (c t) -> p c t", c=CH), ve)
            tbk.release(ve, ve.n)
            pend_hs[0] = None

        stage_q(0)
        stage_attn(0)
        for cb in range(4):
            if cb < 3:
                stage_q(cb + 1)
            stage_wo(cb)
            if cb < 3:
                stage_attn(cb + 1)

        PHASE_MARKS["C"] = len(ctx.plan["tensor"])
        hs_tp = ve.n
        hs_tp2 = 0
        phaseC_pe = pe.n
        phaseC_ve = ve.n
        phaseC_se = se.n

        if upto == "C":
            _finish(ctx, locals())
            _replay(ctx, nc)
            return nc

        fstgb = Buf(fstg)
        fstgb.free = (ve, phaseC_ve)
        vb1_banks = [banks[4], banks[5], banks[6], banks[7]]
        f_first = [True]

        def fold_w1G(G):
            ctx.wait("gpsimd", se, se.n)
            ctx.wait("gpsimd", ve, ve.n)
            for c in range(CH):
                fstgb.wait_free(ctx, "gpsimd")
                if f_first[0]:
                    ctx.wait("gpsimd", se, phaseC_se)
                    ctx.wait("gpsimd", ve, hs_tp)
                ctx.dma("gpsimd", fstg[:, :],
                        w1d[c * 128:(c + 1) * 128, G * 2048:(G + 1) * 2048], dqw)
                ctx.wait("vector", dqw, dqw.n)
                if f_first[0]:
                    ctx.wait("vector", pe, phaseC_pe)
                    ctx.wait("vector", ve, hs_tp)
                    f_first[0] = False
                ctx.wait("vector", pe)
                ts1("vector", WA[:, c * 2048:(c + 1) * 2048], fstg[:, :],
                    g2c[:, c:c + 1], ALU.mult)
                vcopy(g1G[:, 0:2048], fstg[:, :], ve)
                fstgb.free = (ve, ve.n)
                if c % 2 == 1:
                    qc = c // 2
                    if qc == 0 and G == 0:
                        ctx.wait("gpsimd", pe, phaseC_pe)
                    ctx.dma("gpsimd",
                            w2G[:, qc * 4096:(qc + 1) * 4096].rearrange("p (c n) -> p c n", c=4),
                            w2d[G * 2048 + qc * 512:G * 2048 + (qc + 1) * 512, :]
                            .rearrange("(c p) n -> p c n", p=128), dqw)
                ctx.wait("tensor", ve, ve.n)
                for g in range(4):
                    b = vb1_banks[g]
                    if c == 0:
                        b.acquire(ctx)
                    mm(b.ps[0:1, :], b2cb[:, c:c + 1], g1G[:, g * 512:(g + 1) * 512],
                       start=(c == 0), stop=(c == CH - 1), inc=(c == CH - 1))
            ctx.wait("vector", pe)
            for g in range(4):
                if g > 0:
                    ctx.wait("vector", dqt)
                vcopy(recsb[0:1, 0:512], vb1_banks[g].ps[0:1, :], ve)
                vb1_banks[g].release(ve, ve.n)
                ctx.wait("sync", ve, ve.n)
                ctx.dma("sync",
                        vbd[0:1, 3072 + G * 2048 + g * 512:3072 + G * 2048 + (g + 1) * 512],
                        recsb[0:1, 0:512], dqt)
            ctx.wait("scalar", dqt, dqt.n)
            ctx.dma("scalar", vbw1c[:, G * 16:(G + 1) * 16],
                    vbd[0:1, 3072 + G * 2048:3072 + (G + 1) * 2048]
                    .rearrange("a (c p) -> (a p) c", p=128), dqx, slow=True)
            ctx.wait("vector", dqx, dqx.n)
            ts1("vector", vbw1s[:, G * 16:(G + 1) * 16], vbw1c[:, G * 16:(G + 1) * 16],
                10.0, ALU.mult, sem=ve)
            return ve.n

        ftmp = [fstg[:, 0:512], fstg[:, 512:1024]]

        x0db = [Buf(fstg2[:, 0:1024]), Buf(fstg2[:, 1024:2048])]
        for bb in x0db:
            bb.free = [(se, se.n), (ve, ve.n)]
        gw_ld = [0]
        d0_pend = []

        def d0_tile(jj):
            xb = x0db[jj % 2]
            xb.wait_free(ctx, "scalar")
            ctx.wait("scalar", ve, ve.n)
            ctx.dma("scalar", xb.ap[:, :], xd[jj * 128:(jj + 1) * 128, :], dqx)
            xv = dqx.n
            ctx.wait("vector", dqx, xv)
            rtile = RES[:, jj * D:(jj + 1) * D]
            tt("vector", rtile, rtile, xb.ap[:, :], ALU.add, sem=ve)
            mvc = colA if jj % 2 == 0 else colB
            ctx.wait("vector", se)
            ctx.op("vector", lambda E, o=mvc, r=rtile: E.bn_stats(o[:, 4:10], r[:, 0:512]))
            ctx.op("vector", lambda E, o=mvc, r=rtile: E.bn_stats(o[:, 10:16], r[:, 512:1024]), ve)
            ctx.wait("vector", ve)
            ctx.op("vector", lambda E, o=mvc: E.bn_aggr(o[:, 0:2], o[:, 4:16]), ve)
            ctx.wait("vector", ve)
            ts1("vector", mvc[:, 2:3], mvc[:, 1:2], EPS, ALU.add, sem=ve)
            ctx.wait("scalar", ve)
            act(mvc[:, 2:3], mvc[:, 2:3], AF.Ln, sem=se)
            ctx.wait("scalar", se)
            act(mvc[:, 3:4], mvc[:, 2:3], AF.Exp, scale=-0.5, sem=se)
            ctx.wait("vector", se)
            stt("vector", mvc[:, 2:3], mvc[:, 0:1], -1.0, mvc[:, 3:4],
                ALU.mult, ALU.mult, sem=ve)
            t0ap = xb.ap.bitcast(BF16)[:, 0:1024]
            ctx.wait("scalar", ve)
            act(t0ap, rtile, AF.Identity, bias=mvc[:, 2:3], scale=mvc[:, 3:4], sem=se)
            d0_pend.append((jj, xb, t0ap, se.n))

        def d0_pe():
            jj, xb, t0ap, t0se = d0_pend.pop(0)
            tbk = banks[6 + jj % 2]
            tbk.acquire(ctx)
            ctx.wait("tensor", se, t0se)
            tb_bf = tbk.ps.bitcast(BF16)
            for c in range(CH):
                ctx.op("tensor",
                       lambda E, o=tb_bf[:, c * 128:(c + 1) * 128],
                       i=t0ap[:, c * 128:(c + 1) * 128]:
                       E.transpose(o, i, eyeb[:, :]), pe if c == CH - 1 else None)
            tpv = pe.n
            xb.free = (pe, tpv)
            ctx.wait("vector", pe, tpv)
            dview = FM.rearrange("p (c t) -> p c t", c=CH)[:, :, jj * 128:(jj + 1) * 128]
            vcopy(dview, tb_bf[:, 0:1024].rearrange("p (c t) -> p c t", c=CH), ve)
            t0fm_ve = ve.n
            tbk.release(ve, t0fm_ve)
            bg = banks[6 + jj % 2]
            bg.acquire(ctx)
            ctx.wait("tensor", ve, t0fm_ve)
            ctx.wait("tensor", dqw, gw_ld[0])
            for c in range(CH):
                mm(bg.ps[:, 0:8],
                   FM[:, c * S_OWN + jj * 128:c * S_OWN + (jj + 1) * 128],
                   gwsb[:, c * 8:(c + 1) * 8],
                   start=(c == 0), stop=(c == CH - 1), inc=(c == CH - 1))
            ctx.wait("vector", pe, pe.n)
            ctx.wait("vector", se)
            vcopy(scsb[:, 0:8], bg.ps[:, 0:8], ve)
            bg.release(ve, ve.n)
            ctx.wait("vector", ve)
            vred(sccol[:, 0:1], scsb[:, 0:8], ALU.max, sem=ve)
            ctx.wait("vector", ve)
            ts1("vector", sccol[:, 1:2], sccol[:, 0:1], -1.0, ALU.mult, sem=ve)
            ctx.wait("scalar", ve)
            act(scsb[:, 8:16], scsb[:, 0:8], AF.Exp, bias=sccol[:, 1:2], sem=se)
            ctx.wait("vector", se)
            vred(sccol[:, 2:3], scsb[:, 8:16], ALU.add, sem=ve)
            ctx.wait("vector", ve)
            ctx.op("vector", lambda E: E.reciprocal(sccol[:, 3:4], sccol[:, 2:3]), ve)
            ctx.wait("vector", ve)
            ts1("vector", scsb[:, 0:8], scsb[:, 8:16], sccol[:, 3:4], ALU.mult, sem=ve)
            ctx.wait("vector", ve)
            vred(sccol[:, 4:5], scsb[:, 0:8], ALU.max, sem=ve)
            ctx.wait("vector", ve)
            ts1("vector", scsb[:, 8:16], scsb[:, 0:8], sccol[:, 4:5], ALU.is_ge, sem=ve)
            ctx.wait("vector", ve)
            stt("vector", scsb[:, 8:16], scsb[:, 8:16], -1e9, scsb[:, 0:8],
                ALU.mult, ALU.add, sem=ve)
            ctx.wait("vector", ve)
            vred(sccol[:, 5:6], scsb[:, 8:16], ALU.max, sem=ve)
            ctx.wait("vector", ve)
            ts1("vector", scsb[:, 8:16], scsb[:, 0:8], sccol[:, 5:6], ALU.is_ge, sem=ve)
            ctx.wait("vector", ve)
            tt("vector", wdsb[:, jj * 8:(jj + 1) * 8], scsb[:, 8:16], scsb[:, 0:8],
               ALU.mult, sem=ve)

        for G in range(2):
            vbv = fold_w1G(G)
            w2v = dqw.n
            if G == 1:
                ctx.dma("gpsimd", gwsb[:, :].rearrange("p (c n) -> p c n", c=CH),
                        gwd.rearrange("(c p) n -> p c n", p=128), dqw)
                gw_ld[0] = dqw.n
            for fb in range(4):
                tok0 = fb * 512
                w1_banks = [banks[0], banks[1], banks[4]]
                for cc in range(16):
                    b = w1_banks[cc % 3]
                    b.acquire(ctx)
                    ctx.wait("tensor", ve, vbv)
                    if G == 0 and fb == 0 and cc == 0:
                        ctx.wait("tensor", ve, hs_tp)
                    for c in range(CH):
                        mm(b.ps[:, :],
                           WA[:, c * 2048 + cc * 128:c * 2048 + (cc + 1) * 128],
                           FM[:, c * S_OWN + tok0:c * S_OWN + tok0 + 512],
                           start=(c == 0), stop=(c == CH - 1), inc=(c == CH - 1))
                    mmv = pe.n
                    tmp = ftmp[cc % 2]
                    ctx.wait("scalar", pe, mmv)
                    ctx.wait("scalar", ve)
                    act(tmp, b.ps[:, :], AF.Identity, scale=10.0,
                        bias=vbw1s[:, G * 16 + cc:G * 16 + cc + 1], sem=se)
                    b.release(se, se.n)
                    ctx.wait("vector", se)
                    ts2("vector", tmp, tmp, RC, -RC, ALU.add, ALU.add, sem=ve)
                    ctx.wait("vector", ve)
                    ts2("vector", tmp, tmp, 127.0, -128.0, ALU.min, ALU.max, sem=ve)
                    ctx.wait("scalar", ve)
                    act(g1G[:, cc * 512:(cc + 1) * 512], tmp,
                        AF.Identity if DBG_FFN_IDENT[0] else AF.Gelu_apprx_tanh,
                        scale=0.1, sem=se)
                g1_done = se.n
                ctx.wait("tensor", se, g1_done)
                ctx.wait("tensor", dqw, w2v)
                for tc in range(4):
                    jj = fb * 4 + tc
                    for grp in range(2):
                        b = [banks[2], banks[3], banks[5]][(tc * 2 + grp) % 3]
                        b.acquire(ctx)
                        for cc in range(16):
                            mm(b.ps[:, :],
                               g1G[:, cc * 512 + tc * 128:cc * 512 + (tc + 1) * 128],
                               w2G[:, cc * 1024 + grp * 512:cc * 1024 + (grp + 1) * 512],
                               start=(cc == 0), stop=(cc == 15), inc=(cc == 15))
                        ctx.wait("vector", pe, pe.n)
                        rsl = RES[:, jj * D + grp * 512:jj * D + (grp + 1) * 512]
                        tt("vector", rsl, b.ps[:, :], rsl, ALU.add, sem=ve)
                        b.release(ve, ve.n)
                    if len(d0_pend) >= 2:
                        d0_pe()
                    if G == 1:
                        d0_tile(jj)
        ffn_pe = pe.n
        ffn_ve = ve.n

        PHASE_MARKS["F"] = len(ctx.plan["tensor"])
        while d0_pend:
            d0_pe()
        wd_ve = ve.n
        x1_ve = ve.n

        if upto == "F":
            _finish(ctx, locals())
            _replay(ctx, nc)
            return nc

        u_tms = [WB[:, 12288:13056], WB[:, 13824:14592]]
        u_fms = [WB[:, 13056:13824], WB[:, 14592:15360]]
        silu_bf = WB[:, 15360:16128]
        W1b = [Buf(W1xp_aps[0]), Buf(W1xp_aps[1])]
        W2b = [Buf(W2xp_aps[0]), Buf(W2xp_aps[1])]
        W1b[0].free = (pe, ffn_pe)
        W1b[1].free = [(ve, x1_ve), (pe, ffn_pe)]
        W2b[0].free = (ve, wd_ve)
        W2b[1].free = (ve, wd_ve)
        u_tmbs = [Buf(u_tms[0]), Buf(u_tms[1])]
        u_fmbs = [Buf(u_fms[0]), Buf(u_fms[1])]
        for bb in u_tmbs + u_fmbs:
            bb.free = [(ve, x1_ve), (pe, ffn_pe)]
        h_banks = [[banks[0], banks[1], banks[2]], [banks[3], banks[4], banks[5]]]
        eo_banks = [banks[6], banks[7]]

        for e0, wbx, dram, nch in ((0, W1b[0], W1xd, CH), (0, W2b[0], W2xd, 6)):
            wbx.wait_free(ctx, "gpsimd")
            ctx.dma("gpsimd", wbx.ap.rearrange("p (c n) -> p c n", c=nch),
                    dram[0].rearrange("(c p) n -> p c n", p=128), dqw)
            wbx.ready = (dqw, dqw.n)

        def prefetch_chunk(e1, which, c):
            if e1 >= NEXP:
                return
            wbx = (W1b if which == 1 else W2b)[e1 % 2]
            dram = W1xd if which == 1 else W2xd
            ncols = 1536 if which == 1 else 1024
            if c == 0:
                wbx.wait_free(ctx, "gpsimd")
            ctx.wait("gpsimd", pe, pe.n)
            ctx.dma("gpsimd", wbx.ap[:, c * ncols:(c + 1) * ncols],
                    dram[e1, c * 128:(c + 1) * 128, :], dqw)
            nmax = CH if which == 1 else 6
            if c == nmax - 1:
                wbx.ready = (dqw, dqw.n)

        for e in range(NEXP):
            wb1 = W1b[e % 2]
            wb2 = W2b[e % 2]
            wb1.wait_ready(ctx, "tensor")
            ctx.wait("tensor", ve, wd_ve)
            last_h = [None]
            last_eo = [None]
            pend = []

            def emit_eo(tca, ui):
                fb_ = u_fmbs[ui]
                fb_.wait_ready(ctx, "tensor")
                wb2.wait_ready(ctx, "tensor")
                for grp in range(2):
                    b = eo_banks[grp]
                    b.acquire(ctx)
                    for cc in range(6):
                        mm(b.ps[:, :], u_fms[ui][:, cc * 128:(cc + 1) * 128],
                           wb2.ap[:, cc * 1024 + grp * 512:cc * 1024 + (grp + 1) * 512],
                           start=(cc == 0), stop=(cc == 5), inc=(cc == 5))
                    mmv2 = pe.n
                    last_eo[0] = mmv2
                    ctx.wait("vector", pe, mmv2)
                    rsl = RES[:, tca * D + grp * 512:tca * D + (grp + 1) * 512]
                    tt("vector", rsl, b.ps[:, :], rsl, ALU.add, sem=ve)
                    b.release(ve, ve.n)
                fb_.free = (pe, last_eo[0])

            for tca in range(16):
                upi = tca % 2
                hb = h_banks[tca % 2]
                trio = trios[tca % 2]
                gate_pe = None
                for ng in (1, 2, 0):
                    b = hb[ng]
                    b.acquire(ctx)
                    for c in range(CH):
                        mm(b.ps[:, :],
                           FM[:, c * S_OWN + tca * 128:c * S_OWN + (tca + 1) * 128],
                           wb1.ap[:, c * 1536 + ng * 512:c * 1536 + (ng + 1) * 512],
                           start=(c == 0), stop=(c == CH - 1), inc=(c == CH - 1))
                    if ng == 2:
                        gate_pe = pe.n
                mmv = pe.n
                last_h[0] = mmv
                if len(pend) >= 2:
                    tca_, ui_ = pend.pop(0)
                    emit_eo(tca_, ui_)
                ctx.wait("scalar", pe, gate_pe)
                ctx.wait("scalar", ve)
                act(silu_bf[:, 0:768], trio[:, 768:1536], AF.Silu, sem=se)
                sv = se.n
                ub = u_tmbs[upi]
                ub.wait_free(ctx, "vector")
                ctx.wait("vector", se, sv)
                ctx.wait("vector", pe, mmv)
                ut = u_tms[upi]
                if e < 8:
                    wdcol = wdsb[:, tca * 8 + e:tca * 8 + e + 1]
                    stt("vector", ut[:, 0:768], trio[:, 0:768], wdcol,
                        silu_bf[:, 0:768], ALU.mult, ALU.mult, sem=ve)
                else:
                    tt("vector", ut[:, 0:768], trio[:, 0:768],
                       silu_bf[:, 0:768], ALU.mult, sem=ve)
                uv = ve.n
                ub.ready = (ve, uv)
                hb[1].release(ve, uv)
                hb[2].release(ve, uv)
                fb_ = u_fmbs[upi]
                ctx.wait("tensor", ve, uv)
                tb_bf = hb[0].ps.bitcast(BF16)
                for c in range(6):
                    ctx.op("tensor",
                           lambda E, o=tb_bf[:, c * 128:(c + 1) * 128],
                           i=ut[:, c * 128:(c + 1) * 128]:
                           E.transpose(o, i, eyeb[:, :]), pe if c == 5 else None)
                tpv = pe.n
                ub.free = (pe, tpv)
                fb_.wait_free(ctx, "scalar")
                ctx.wait("scalar", pe, tpv)
                act(u_fms[upi][:, 0:768], tb_bf[:, 0:768], AF.Identity, sem=se)
                hb[0].release(se, se.n)
                fb_.ready = (se, se.n)
                pend.append((tca, upi))
                if tca % 2 == 1:
                    prefetch_chunk(e + 1, 1, tca // 2)
                elif 2 <= tca <= 12:
                    prefetch_chunk(e + 1, 2, tca // 2 - 1)
            while pend:
                tca_, ui_ = pend.pop(0)
                emit_eo(tca_, ui_)
            wb1.free = (pe, last_h[0])
            wb2.free = (pe, last_eo[0])
        _finish(ctx, locals())
        _replay(ctx, nc)
        return nc


def _finish(ctx, env):
    pe, ve, se = env["pe"], env["ve"], env["se"]
    dqo, dqt, dqx, dqw = env["dqo"], env["dqt"], env["dqx"], env["dqw"]
    RES, outd = env["RES"], env["outd"]
    for s in (pe, ve, se, dqt, dqx, dqw):
        ctx.wait("scalar", s)
    tap_aps = {
        "as_fm": env["FM"], "hsfm": env["FM"], "t0fm": env["FM"],
        "kvsb": env["kvsb"], "res": env["RES"], "phik": env["phik"],
        "vtm": env["vtm"], "phiq": env["phiq"], "attn": env["attnf"],
        "rec": env["recsb"], "wd": env["wdsb"], "vbq": env["vbqc"],
        "vbkv": env["VBKV"], "wkvs": env["WA"], "wqs": env["wqS"],
        "g1g": env["g1G"], "zmat": env["zmat"], "kvsq": env["kvs"],
    }
    nc = ctx.nc
    for t in env["taps"]:
        src = tap_aps[t]
        shp = [src.shape[0], src.shape[1]]
        dt = src.dtype
        td = nc.dram_tensor("tap_" + t, shp, dt, kind="ExternalOutput").ap()
        ctx.dma("scalar", td[:, :], src, dqo)
    for jj in range(16):
        ctx.dma("scalar", outd[jj * 128:(jj + 1) * 128, :],
                RES[:, jj * 1024:(jj + 1) * 1024], dqo)
    ctx.wait("scalar", dqo)
    ctx.wait("sync", dqo)


def _replay(ctx, nc):
    with nc.Block() as block:
        block.sync(lambda E: [f(E) for f in ctx.plan["sync"]])
        block.scalar(lambda E: [f(E) for f in ctx.plan["scalar"]])
        block.vector(lambda E: [f(E) for f in ctx.plan["vector"]])
        block.tensor(lambda E: [f(E) for f in ctx.plan["tensor"]])
        block.gpsimd(lambda E: [f(E) for f in ctx.plan["gpsimd"]])


def _host_layout(inputs):
    x = np.asarray(inputs["L0_attn_x"], np.float32)
    wq = np.ascontiguousarray(np.asarray(inputs["L0_attn_wq"], np.float32))
    wk = np.asarray(inputs["L0_attn_wk"], np.float32)
    wv = np.asarray(inputs["L0_attn_wv"], np.float32)
    wkv = np.ascontiguousarray(np.concatenate([wk, wv], axis=1))
    wo = np.ascontiguousarray(np.asarray(inputs["L0_attn_wo"], np.float32))
    w1 = np.ascontiguousarray(np.asarray(inputs["L0_attn_w1"], np.float32))
    w2 = np.ascontiguousarray(np.asarray(inputs["L0_attn_w2"], np.float32))
    gw = np.ascontiguousarray(np.asarray(inputs["L0_ffn_gate_w"], np.float32).T)
    W1 = np.asarray(inputs["L0_ffn_W1"], np.float32)
    W2 = np.asarray(inputs["L0_ffn_W2"], np.float32)
    Ws1 = np.asarray(inputs["L0_ffn_Ws1"], np.float32)
    Ws2 = np.asarray(inputs["L0_ffn_Ws2"], np.float32)
    W1t = W1.transpose(0, 2, 1)
    W2t = W2.transpose(0, 2, 1)
    Ws1t = Ws1.T
    Ws2t = Ws2.T
    W1x = np.zeros((NEXP, D, 1536), np.float32)
    W2x = np.zeros((NEXP, 768, D), np.float32)
    W1x[:8] = W1t
    W2x[:8] = W2t
    W1x[8] = np.concatenate([Ws1t[:, 0:768], Ws1t[:, 1536:2304]], axis=1)
    W1x[9] = np.concatenate([Ws1t[:, 768:1536], Ws1t[:, 2304:3072]], axis=1)
    W2x[8] = Ws2t[0:768]
    W2x[9] = Ws2t[768:1536]
    oneh = np.zeros((16, 1024), np.float32)
    for p_ in range(8):
        oneh[2 * p_, p_ * 128:p_ * 128 + 64] = 1.0
        oneh[2 * p_ + 1, p_ * 128 + 64:p_ * 128 + 128] = 1.0
    shared = dict(
        eye=np.eye(128, dtype=np.float32),
        oneh=oneh,
        wq=wq, wkv=wkv, wo=wo, w1=w1, w2=w2, gw=gw,
        W1x=np.ascontiguousarray(W1x), W2x=np.ascontiguousarray(W2x),
        g1=np.ascontiguousarray(np.asarray(inputs["L0_attn_ln1_g"], np.float32)),
        b1=np.ascontiguousarray(np.asarray(inputs["L0_attn_ln1_b"], np.float32)),
        g2=np.ascontiguousarray(np.asarray(inputs["L0_attn_ln2_g"], np.float32)),
        b2=np.ascontiguousarray(np.asarray(inputs["L0_attn_ln2_b"], np.float32)),
    )
    in_maps = []
    for c in range(8):
        b, h = c // 2, c % 2
        own = x[b, h * 2048:(h + 1) * 2048]
        oth = x[b, (1 - h) * 2048:(1 - h) * 2048 + 2048]
        m = dict(shared)
        m["x"] = np.ascontiguousarray(np.concatenate([own, oth], axis=0))
        in_maps.append(m)
    return in_maps


def kernel(**inputs):
    in_maps = _host_layout(inputs)
    nc = build_program()
    res = run_bass_kernel_spmd(nc, in_maps, core_ids=list(range(8)))
    out = np.zeros((4, 4096, 1024), np.float32)
    for c in range(8):
        b, h = c // 2, c % 2
        out[b, h * 2048:(h + 1) * 2048] = res.results[c]["out"]
    return out


# version 14
# speedup vs baseline: 1.3048x; 1.1816x over previous
import contextlib
import numpy as np

import concourse.bass as bass
import concourse.mybir as mybir
from concourse.bass_utils import run_bass_kernel_spmd

F32 = mybir.dt.float32
BF16 = mybir.dt.bfloat16
AF = mybir.ActivationFunctionType
ALU = mybir.AluOpType
AX = mybir.AxisListType

D = 1024
DBG_FFN_IDENT = [False]
PHASE_MARKS = {}
DBG_EO_INLINE = [False]
DBG_D1_TP_SYNC = [False]
DBG_D1_NOPING = [False]
DBG_D0_TP_SYNC = [False]
CH = 8
S_ALL = 4096
S_OWN = 2048
EPS = 1e-5
RC = 12582912.0
NEXP = 10

ENGS = ("sync", "scalar", "vector", "tensor", "gpsimd")


class SemT:
    __slots__ = ("h", "n")

    def __init__(self, h):
        self.h = h
        self.n = 0


class Ctx:
    def __init__(self, nc, es):
        self.nc = nc
        self.es = es
        self.plan = {e: [] for e in ENGS}
        self.lastwait = {}

    def sem(self, name):
        return SemT(self.es.enter_context(self.nc.semaphore(name)))

    def emit(self, eng, fn):
        self.plan[eng].append(fn)

    def wait(self, eng, sem, val=None):
        v = sem.n if val is None else val
        if v <= 0:
            return
        key = (eng, id(sem))
        if self.lastwait.get(key, -1) >= v:
            return
        self.lastwait[key] = v
        self.emit(eng, lambda E, s=sem.h, v=v: E.wait_ge(s, v))

    def dma(self, eng, out, in_, sem, transpose=False, slow=False):
        sem.n += 16

        def f(E, out=out, in_=in_, t=transpose, s=sem.h, sl=slow):
            E.dma_start(out=out, in_=in_, transpose=t,
                        allow_slow_non_contiguous=sl).then_inc(s, 16)

        self.emit(eng, f)
        return sem.n

    def op(self, eng, builder, sem=None):
        if sem is not None:
            sem.n += 1
        sh = sem.h if sem is not None else None

        def f(E, b=builder, sh=sh):
            i = b(E)
            if sh is not None:
                i.then_inc(sh, 1)

        self.emit(eng, f)
        return sem.n if sem is not None else None


class Bank:
    def __init__(self, ps):
        self.ps = ps
        self.free = None

    def acquire(self, ctx):
        if self.free is not None:
            ctx.wait("tensor", self.free[0], self.free[1])

    def release(self, sem, val):
        self.free = (sem, val)


class Buf:
    def __init__(self, ap):
        self.ap = ap
        self.free = None
        self.ready = None

    @staticmethod
    def _w(ctx, eng, deps):
        if deps is None:
            return
        if isinstance(deps, list):
            for s, v in deps:
                ctx.wait(eng, s, v)
        else:
            ctx.wait(eng, deps[0], deps[1])

    def wait_free(self, ctx, eng):
        self._w(ctx, eng, self.free)

    def wait_ready(self, ctx, eng):
        self._w(ctx, eng, self.ready)


def build_program(upto="E", taps=()):
    nc = bass.Bass("TRN2", target_bir_lowering=False)
    taps = list(taps)

    xd = nc.dram_tensor("x", [S_ALL, D], F32, kind="ExternalInput").ap()
    wqd = nc.dram_tensor("wq", [D, D], F32, kind="ExternalInput").ap()
    wkvd = nc.dram_tensor("wkv", [D, 2 * D], F32, kind="ExternalInput").ap()
    wod = nc.dram_tensor("wo", [D, D], F32, kind="ExternalInput").ap()
    w1d = nc.dram_tensor("w1", [D, 4096], F32, kind="ExternalInput").ap()
    w2d = nc.dram_tensor("w2", [4096, D], F32, kind="ExternalInput").ap()
    gwd = nc.dram_tensor("gw", [D, 8], F32, kind="ExternalInput").ap()
    W1xd = nc.dram_tensor("W1x", [NEXP, D, 1536], F32, kind="ExternalInput").ap()
    W2xd = nc.dram_tensor("W2x", [NEXP, 768, D], F32, kind="ExternalInput").ap()
    eyed = nc.dram_tensor("eye", [128, 128], F32, kind="ExternalInput").ap()
    onehd = nc.dram_tensor("oneh", [16, 1024], F32, kind="ExternalInput").ap()
    g1d = nc.dram_tensor("g1", [D], F32, kind="ExternalInput").ap()
    b1d = nc.dram_tensor("b1", [D], F32, kind="ExternalInput").ap()
    g2d = nc.dram_tensor("g2", [D], F32, kind="ExternalInput").ap()
    b2d = nc.dram_tensor("b2", [D], F32, kind="ExternalInput").ap()
    outd = nc.dram_tensor("out", [S_OWN, D], F32, kind="ExternalOutput").ap()
    vbd = nc.dram_tensor("vbd", [1, 8192], F32).ap()
    recd = nc.dram_tensor("recd", [16, 512], F32).ap()

    es = contextlib.ExitStack()
    with es:
        ctx = Ctx(nc, es)
        sb = lambda name, shape, dt: es.enter_context(
            nc.sbuf_tensor(name, shape, dt)).ap()

        dqx = ctx.sem("dqx")
        dqt = ctx.sem("dqt")
        dqt2 = ctx.sem("dqt2")
        dqw = ctx.sem("dqw")
        pe = ctx.sem("pe")
        ve = ctx.sem("ve")
        se = ctx.sem("se")
        ge = ctx.sem("ge")
        dqo = ctx.sem("dqo")

        RES = sb("RES", [128, 16 * D], F32)
        FM = sb("FM", [128, CH * S_OWN], BF16)
        g1c = sb("g1c", [128, 8], F32)
        b1c = sb("b1c", [128, 8], F32)
        g2c = sb("g2c", [128, 8], F32)
        b2c = sb("b2c", [128, 8], F32)
        b1cb = sb("b1cb", [128, 8], BF16)
        b2cb = sb("b2cb", [128, 8], BF16)
        vbqc = sb("vbqc", [128, 8], F32)
        vbw1c = sb("vbw1c", [128, 32], F32)
        vbw1s = sb("vbw1s", [128, 32], F32)
        ones1 = sb("ones1", [128, 1], BF16)
        eyeb = sb("eyeb", [128, 128], BF16)
        onehb = sb("onehb", [16, 1024], BF16)
        rechi = sb("rechi", [16, 512], BF16)
        reclo = sb("reclo", [16, 512], BF16)
        fstg2 = sb("fstg2", [128, 2048], F32)
        gwsb = sb("gwsb", [128, 64], BF16)
        kvsb = sb("kvsb", [128, 8 * 132], F32)
        kvs = sb("kvs", [128, 8 * 128], BF16)
        zmat = sb("zmat", [128, 8 * 16], BF16)
        wdsb = sb("wdsb", [128, 16 * 8], F32)
        colA = sb("colA", [128, 48], F32)
        colB = sb("colB", [128, 48], F32)
        recsb = sb("recsb", [16, 512], F32)
        scsb = sb("scsb", [128, 16], F32)
        sccol = sb("sccol", [128, 8], F32)

        WA = sb("WA", [128, 16384], BF16)
        WB = sb("WB", [128, 16384], BF16)
        SCR = sb("SCR", [128, 12288], BF16)

        wqS = WB[:, 0:8192]
        vbkvr = WB[0:1, 8192:10240]
        onesr = WB[0:1, 14336:14464]
        VBKV = WB[:, 8192:12288].bitcast(F32)
        tmpA = WB[:, 12288:13312].bitcast(F32)
        tmpB = WB[:, 13312:14336].bitcast(F32)
        woS = WB[:, 8192:16384]
        w2G = WB
        W1xp_aps = [WA[:, 0:12288], WB[:, 0:12288]]
        u_tm = WB[:, 12288:13824]
        u_tm = WB[:, 12288:13056]
        u_fm = WB[:, 13056:13824]
        silu_sb = WB[:, 13824:15360].bitcast(F32)

        x0p = [SCR[:, 0:2048].bitcast(F32), SCR[:, 2048:4096].bitcast(F32)]
        astp = [SCR[:, 4096:5120], SCR[:, 5120:6144]]
        afsp = [SCR[:, 6144:7168], SCR[:, 7168:8192]]
        phik = SCR[:, 8192:10240]
        vtm = SCR[:, 10240:12288]
        stg = [SCR[:, 0:4096].bitcast(F32), SCR[:, 4096:8192].bitcast(F32)]
        plainb = SCR[:, 8192:10240]
        phiq = SCR[:, 0:4096]
        attnf = SCR[:, 4096:8192]
        recbc = SCR[:, 8192:9216].bitcast(F32)
        ctmpA = SCR[:, 9216:10240].bitcast(F32)
        ctmpB = SCR[:, 10240:11264].bitcast(F32)
        hst = SCR[:, 11264:12288]
        g1G = SCR[:, 0:8192]
        fstg = SCR[:, 8192:12288].bitcast(F32)
        t0p = [SCR[:, 0:1024], SCR[:, 1024:2048]]
        gws = SCR[:, 2048:2112]
        W2xp_aps = [SCR[:, 0:6144], SCR[:, 6144:12288]]

        pt0 = es.enter_context(nc.psum_tensor("pt0", [128, 1536], F32)).ap()
        pt1 = es.enter_context(nc.psum_tensor("pt1", [128, 1536], F32)).ap()
        pb6 = es.enter_context(nc.psum_tensor("pb6", [128, 512], F32)).ap()
        pb7 = es.enter_context(nc.psum_tensor("pb7", [128, 512], F32)).ap()
        banks = [Bank(pt0[:, 0:512]), Bank(pt0[:, 512:1024]), Bank(pt0[:, 1024:1536]),
                 Bank(pt1[:, 0:512]), Bank(pt1[:, 512:1024]), Bank(pt1[:, 1024:1536]),
                 Bank(pb6), Bank(pb7)]
        trios = [pt0, pt1]

        def mm(out_ap, lhsT, rhs, start, stop, inc=False):
            return ctx.op("tensor",
                          lambda E, o=out_ap, l=lhsT, r=rhs, st=start, sp=stop:
                          E.matmul(o, l, r, start=st, stop=sp),
                          pe if inc else None)

        def ts1(eng, out, in0, s1, op0, sem=None):
            return ctx.op(eng, lambda E, o=out, i=in0, a=s1, p=op0:
                          E.tensor_single_scalar(o, i, a, p), sem)

        def ts2(eng, out, in0, s1, s2, op0, op1, sem=None):
            return ctx.op(eng, lambda E, o=out, i=in0, a=s1, b=s2, p=op0, q=op1:
                          E.tensor_scalar(o, i, a, b, p, q), sem)

        def tt(eng, out, in0, in1, op, sem=None):
            return ctx.op(eng, lambda E, o=out, a=in0, b=in1, p=op:
                          E.tensor_tensor(o, a, b, p), sem)

        def stt(eng, out, in0, scal, in1, op0, op1, sem=None):
            return ctx.op(eng, lambda E, o=out, a=in0, s=scal, b=in1, p=op0, q=op1:
                          E.scalar_tensor_tensor(o, a, s, b, p, q), sem)

        def act(out, in_, func, bias=0.0, scale=1.0, sem=None):
            return ctx.op("scalar", lambda E, o=out, i=in_, f=func, b=bias, s=scale:
                          E.activation(o, i, f, bias=b, scale=s), sem)

        def vcopy(out, in_, sem=None):
            return ctx.op("vector", lambda E, o=out, i=in_: E.tensor_copy(o, i), sem)

        def vred(out, in_, op, sem=None):
            return ctx.op("vector", lambda E, o=out, i=in_, p=op:
                          E.tensor_reduce(o, i, AX.X, p), sem)

        def tp_split(pairs, split=False):
            for i, (dst, srcp) in enumerate(pairs):
                if split and i % 2 == 1:
                    ctx.dma("scalar", dst, srcp, dqt2, transpose=True)
                else:
                    ctx.dma("sync", dst, srcp, dqt, transpose=True)
            return [(dqt, dqt.n), (dqt2, dqt2.n)]

        for dst, src in ((g1c, g1d), (b1c, b1d), (g2c, g2d), (b2c, b2d)):
            ctx.dma("scalar", dst[:, :], src.rearrange("(c p) -> p c", p=128), dqx, slow=True)
        ctx.wait("vector", dqx, dqx.n)
        vcopy(b1cb[:, :], b1c[:, :])
        vcopy(b2cb[:, :], b2c[:, :])
        ctx.op("vector", lambda E: E.memset(ones1[:, :], 1.0))
        ctx.op("vector", lambda E: E.memset(onesr[:, :], 1.0))
        ctx.op("vector", lambda E: E.memset(kvsb[:, :], 0.0))
        ctx.op("vector", lambda E: E.memset(kvs[:, :], 0.0))
        ctx.op("vector", lambda E: E.memset(zmat[:, :], 0.0), ve)

        ctx.dma("gpsimd", eyeb[:, :], eyed[:, :], dqw)
        ctx.dma("gpsimd", onehb[:, :], onehd[:, :], dqw)
        stgb = [Buf(stg[0]), Buf(stg[1])]
        vb_banks_q = [banks[0], banks[1]]
        vb_banks_kv = [banks[2], banks[3], banks[4], banks[5]]

        def prep_fold(wsrc, ncols, dst_ap, gcol, vb_banks, bvec):
            for c in range(CH):
                s = stgb[c % 2]
                s.wait_free(ctx, "gpsimd")
                ctx.dma("gpsimd", s.ap[:, 0:ncols], wsrc[c * 128:(c + 1) * 128, :], dqw)
                ctx.wait("vector", dqw, dqw.n)
                ctx.wait("vector", pe)
                ts1("vector", dst_ap[:, c * ncols:(c + 1) * ncols], s.ap[:, 0:ncols],
                    gcol[:, c:c + 1], ALU.mult)
                vcopy(plainb[:, 0:ncols], s.ap[:, 0:ncols], ve)
                s.free = (ve, ve.n)
                ctx.wait("tensor", ve, ve.n)
                for g in range(ncols // 512):
                    b = vb_banks[g]
                    if c == 0:
                        b.acquire(ctx)
                    mm(b.ps[0:1, :], bvec[:, c:c + 1], plainb[:, g * 512:(g + 1) * 512],
                       start=(c == 0), stop=(c == CH - 1), inc=(c == CH - 1))

        prep_fold(wqd, D, wqS, g1c, vb_banks_q, b1cb)
        prep_fold(wkvd, 2 * D, WA, g1c, vb_banks_kv, b1cb)

        ctx.wait("vector", pe)
        for i, b in enumerate(vb_banks_q + vb_banks_kv):
            if i > 0:
                ctx.wait("vector", dqt)
            vcopy(tmpA[0:1, 0:512], b.ps[0:1, :], ve)
            b.release(ve, ve.n)
            ctx.wait("sync", ve, ve.n)
            ctx.dma("sync", vbd[0:1, i * 512:(i + 1) * 512], tmpA[0:1, 0:512], dqt)
        vb_evac = ve.n
        ctx.wait("scalar", dqt, dqt.n)
        ctx.dma("scalar", vbqc[:, :],
                vbd[0:1, 0:1024].rearrange("a (c p) -> (a p) c", p=128), dqx, slow=True)
        ctx.wait("gpsimd", dqt, dqt.n)
        ctx.dma("gpsimd", vbkvr[:, :], vbd[0:1, 1024:3072], dqw)
        prep_loads = dqx.n
        prep_loads_w = dqw.n
        PHASE_MARKS["prep"] = len(ctx.plan["tensor"])
        prep_pe = pe.n
        prep_ve = ve.n

        NTC = 2
        ABLK = 16
        x0b = [Buf(x0p[0]), Buf(x0p[1]),
               Buf(fstg2[:, 0:1024]), Buf(fstg2[:, 1024:2048])]
        astb = [Buf(astp[0]), Buf(astp[1])]
        afsb = [Buf(afsp[0]), Buf(afsp[1])]
        for bb in x0b + astb + afsb:
            bb.free = (pe, prep_pe)
        kv_banks = [banks[3], banks[4], banks[5]]
        pj_banks = [banks[0], banks[1], banks[2]]
        first_scr = [True]

        for blk in range(ABLK):
            own = blk < 8
            mvc = colA if blk % 2 == 0 else colB
            for tc in range(NTC):
                jj = blk * NTC + tc
                xb = x0b[jj % 4]
                xb.wait_free(ctx, "scalar")
                if first_scr[0]:
                    ctx.wait("scalar", ve, prep_ve)
                ctx.dma("scalar", xb.ap[:, :], xd[jj * 128:(jj + 1) * 128, :], dqx)
                xb.ready = (dqx, dqx.n)
                xb.wait_ready(ctx, "vector")
                if blk >= 2:
                    ctx.wait("vector", se)
                bnc = 4 + tc * 12
                ctx.op("vector", lambda E, x=xb.ap, o=mvc, b=bnc:
                       E.bn_stats(o[:, b:b + 6], x[:, 0:512]))
                ctx.op("vector", lambda E, x=xb.ap, o=mvc, b=bnc:
                       E.bn_stats(o[:, b + 6:b + 12], x[:, 512:1024]), ve)
                ctx.wait("vector", ve)
                ctx.op("vector", lambda E, o=mvc, tc=tc, b=bnc:
                       E.bn_aggr(o[:, tc * 2:tc * 2 + 2], o[:, b:b + 12]), ve)
                ctx.wait("vector", ve)
            m_ = mvc[:, 0:2 * NTC:2]
            v_ = mvc[:, 1:2 * NTC:2]
            t4_ = mvc[:, 28:28 + NTC]
            t1_ = mvc[:, 28 + NTC:28 + 2 * NTC]
            s01_ = mvc[:, 32:32 + NTC]
            rs1_ = mvc[:, 32 + NTC:32 + 2 * NTC]
            nms_ = mvc[:, 36:36 + NTC]
            nm1_ = mvc[:, 38:38 + NTC]
            ts2("vector", t4_, v_, 1.0 + EPS, EPS * EPS, ALU.mult, ALU.add)
            ts1("vector", t1_, v_, EPS, ALU.add, sem=ve)
            ctx.wait("scalar", ve)
            act(mvc[:, 28:28 + 2 * NTC], mvc[:, 28:28 + 2 * NTC], AF.Ln, sem=se)
            ctx.wait("scalar", se)
            act(mvc[:, 32:32 + 2 * NTC], mvc[:, 28:28 + 2 * NTC], AF.Exp,
                scale=-0.5, sem=se)
            ctx.wait("vector", se)
            stt("vector", nms_, m_, -1.0, s01_, ALU.mult, ALU.mult, sem=ve)
            if own:
                stt("vector", nm1_, m_, -1.0, rs1_, ALU.mult, ALU.mult, sem=ve)
            cols_ve = ve.n

            for tc in range(NTC):
                jj = blk * NTC + tc
                xb = x0b[jj % 4]
                ab = astb[jj % 2]
                ab.wait_free(ctx, "scalar")
                ctx.wait("scalar", ve, cols_ve)
                act(ab.ap[:, :], xb.ap[:, :], AF.Identity,
                    bias=nms_[:, tc:tc + 1], scale=s01_[:, tc:tc + 1], sem=se)
                ab.ready = (se, se.n)
                if own:
                    act(RES[:, jj * D:(jj + 1) * D], xb.ap[:, :], AF.Identity,
                        bias=nm1_[:, tc:tc + 1], scale=rs1_[:, tc:tc + 1], sem=se)
                xb.free = (se, se.n)
                if own:
                    def fmdst(c, jj=jj):
                        return FM[:, c * S_OWN + jj * 128:c * S_OWN + (jj + 1) * 128]
                    dst = fmdst
                else:
                    fs = afsb[jj % 2]
                    fs.wait_free(ctx, "sync")

                    def fmdst(c, fs=fs):
                        return fs.ap[:, c * 128:(c + 1) * 128]
                    dst = fmdst
                tb = banks[6 + jj % 2]
                tb.acquire(ctx)
                ab.wait_ready(ctx, "tensor")
                tb_bf = tb.ps.bitcast(BF16)
                for c in range(CH):
                    ctx.op("tensor",
                           lambda E, o=tb_bf[:, c * 128:(c + 1) * 128],
                           i=ab.ap[:, c * 128:(c + 1) * 128]:
                           E.transpose(o, i, eyeb[:, :]), pe if c == CH - 1 else None)
                tpv = pe.n
                ab.free = (pe, tpv)
                ctx.wait("vector", pe, tpv)
                if own:
                    dview = FM.rearrange("p (c t) -> p c t", c=CH)[:, :, jj * 128:(jj + 1) * 128]
                else:
                    dview = afsb[jj % 2].ap.rearrange("p (c t) -> p c t", c=CH)
                vcopy(dview, tb_bf[:, 0:1024].rearrange("p (c t) -> p c t", c=CH), ve)
                tb.release(ve, ve.n)
                tdep = (ve, ve.n)
                if not own:
                    afsb[jj % 2].ready = tdep

                ctx.wait("tensor", tdep[0], tdep[1])
                if first_scr[0]:
                    ctx.wait("tensor", dqx, prep_loads)
                    first_scr[0] = False
                for grp in range(4):
                    b = pj_banks[(tc * 4 + grp) % 3]
                    b.acquire(ctx)
                    if jj == 0 and grp == 0:
                        ctx.wait("tensor", dqw, prep_loads_w)
                    for c in range(CH):
                        mm(b.ps[:, :], dst(c),
                           WA[:, c * 2048 + grp * 512:c * 2048 + (grp + 1) * 512],
                           start=(c == 0), stop=False)
                    mm(b.ps[:, :], onesr[:, :],
                       vbkvr[0:1, grp * 512:(grp + 1) * 512],
                       start=False, stop=True, inc=True)
                    mmv = pe.n
                    ctx.wait("vector", pe, mmv)
                    if grp < 2:
                        ts1("vector", tmpA[:, :], b.ps[:, :], 0.0, ALU.min, sem=ve)
                        ctx.wait("scalar", ve)
                        ctx.wait("scalar", pe, mmv)
                        act(tmpB[:, :], b.ps[:, :], AF.Relu, sem=se)
                        act(tmpA[:, :], tmpA[:, :], AF.Exp, sem=se)
                        ctx.wait("vector", se)
                        tt("vector",
                           phik[:, tc * 1024 + grp * 512:tc * 1024 + (grp + 1) * 512],
                           tmpA[:, :], tmpB[:, :], ALU.add, sem=ve)
                    else:
                        g2_ = grp - 2
                        vcopy(vtm[:, tc * 1024 + g2_ * 512:tc * 1024 + (g2_ + 1) * 512],
                              b.ps[:, :], ve)
                    b.release(ve, ve.n)
            phiv_done = ve.n

            ctx.wait("tensor", ve, phiv_done)
            for pair in range(8):
                    b = kv_banks[pair % 3]
                    b.acquire(ctx)
                    for tc in range(NTC):
                        mm(b.ps[:, 0:128],
                           phik[:, tc * 1024 + pair * 128:tc * 1024 + (pair + 1) * 128],
                           vtm[:, tc * 1024 + pair * 128:tc * 1024 + (pair + 1) * 128],
                           start=(tc == 0), stop=False)
                        mm(b.ps[:, 128:129],
                           phik[:, tc * 1024 + pair * 128:tc * 1024 + (pair + 1) * 128],
                           ones1[:, :], start=False, stop=(tc == NTC - 1),
                           inc=(tc == NTC - 1))
                    ctx.wait("vector", pe, pe.n)
                    tt("vector", kvsb[:, pair * 132:pair * 132 + 129],
                       kvsb[:, pair * 132:pair * 132 + 129], b.ps[:, 0:129],
                       ALU.add, sem=ve)
                    b.release(ve, ve.n)
        PHASE_MARKS["A"] = len(ctx.plan["tensor"])
        phaseA_pe = pe.n
        phaseA_ve = ve.n
        phaseA_se = se.n
        phaseA_tp = dqt.n

        ctx.wait("vector", ve, phaseA_ve)
        for pair in range(8):
            vcopy(kvs[0:64, pair * 128:pair * 128 + 64],
                  kvsb[0:64, pair * 132:pair * 132 + 64])
            vcopy(kvs[64:128, pair * 128 + 64:pair * 128 + 128],
                  kvsb[64:128, pair * 132 + 64:pair * 132 + 128])
            vcopy(zmat[0:64, pair * 16 + 2 * pair:pair * 16 + 2 * pair + 1],
                  kvsb[0:64, pair * 132 + 128:pair * 132 + 129])
            vcopy(zmat[64:128, pair * 16 + 2 * pair + 1:pair * 16 + 2 * pair + 2],
                  kvsb[64:128, pair * 132 + 128:pair * 132 + 129],
                  ve if pair == 7 else None)
        kv_final = ve.n

        if upto == "A":
            _finish(ctx, locals())
            _replay(ctx, nc)
            return nc

        ctx.wait("gpsimd", ve, phaseA_ve)
        ctx.wait("gpsimd", se, phaseA_se)
        ctx.dma("gpsimd", woS.rearrange("p (c n) -> p c n", c=CH),
                wod.rearrange("(c p) n -> p c n", p=128), dqw)
        wo_ld = dqw.n

        recb = Buf(recbc)
        hsb = Buf(hst)
        recb.hs_se = 0
        hsb.hs_se = 0
        pend_hs = [None]
        c_first = [True]
        c_state = {}

        def stage_q(cb):
            tok0 = cb * 512
            for m in range(CH):
                b = banks[m % 3]
                b.acquire(ctx)
                for c in range(CH):
                    mm(b.ps[:, :], wqS[:, c * 1024 + m * 128:c * 1024 + (m + 1) * 128],
                       FM[:, c * S_OWN + tok0:c * S_OWN + tok0 + 512],
                       start=(c == 0), stop=(c == CH - 1), inc=(c == CH - 1))
                mmv = pe.n
                ctx.wait("vector", pe, mmv)
                if c_first[0]:
                    ctx.wait("vector", pe, phaseA_pe)
                    ctx.wait("scalar", pe, phaseA_pe)
                    ctx.wait("vector", dqt, phaseA_tp)
                    c_first[0] = False
                ts2("vector", ctmpA[:, :], b.ps[:, :], vbqc[:, m:m + 1], 0.0,
                    ALU.add, ALU.min, sem=ve)
                ctx.wait("scalar", ve)
                ctx.wait("scalar", pe, mmv)
                act(ctmpB[:, :], b.ps[:, :], AF.Relu, bias=vbqc[:, m:m + 1], sem=se)
                act(ctmpA[:, :], ctmpA[:, :], AF.Exp, sem=se)
                ctx.wait("vector", se)
                tt("vector", phiq[:, m * 512:(m + 1) * 512], ctmpA[:, :], ctmpB[:, :],
                   ALU.add, sem=ve)
                b.release(ve, ve.n)
                bden = banks[6]
                if m == 0:
                    bden.acquire(ctx)
                    ctx.wait("tensor", ve, kv_final)
                ctx.wait("tensor", ve, ve.n)
                mm(bden.ps[0:16, :], zmat[:, m * 16:(m + 1) * 16],
                   phiq[:, m * 512:(m + 1) * 512],
                   start=(m == 0), stop=(m == CH - 1), inc=(m == CH - 1))
            phiq_done = ve.n
            c_state[cb] = phiq_done

        def stage_attn(cb):
            phiq_done = c_state[cb]
            bden = banks[6]

            denv = pe.n
            num_banks = [banks[7], banks[0], banks[1], banks[2], banks[3],
                         banks[4], banks[5], banks[7]]
            for pair in range(7):
                bnum = num_banks[pair]
                bnum.acquire(ctx)
                mm(bnum.ps[:, :], kvs[:, pair * 128:(pair + 1) * 128],
                   phiq[:, pair * 512:(pair + 1) * 512], start=True, stop=True,
                   inc=(pair == 6))
            numv = pe.n
            ctx.wait("vector", pe, denv)
            ts1("vector", recsb[:, :], bden.ps[0:16, :], 1e-6, ALU.add, sem=ve)
            ctx.wait("vector", ve)
            recb.wait_free(ctx, "vector")
            ctx.op("vector", lambda E: E.reciprocal(recbc[0:16, :], recsb[:, :]), ve)
            ctx.wait("vector", ve)
            vcopy(rechi[:, :], recbc[0:16, :])
            tt("vector", reclo[:, :], recbc[0:16, :], rechi[:, :], ALU.subtract, sem=ve)
            recv = ve.n
            bden.release(ve, recv)
            ctx.wait("tensor", ve, recv)
            for pair in range(8):
                brc = bden
                brc.acquire(ctx)
                mm(brc.ps[:, :], onehb[:, pair * 128:(pair + 1) * 128], rechi[:, :],
                   start=True, stop=False)
                mm(brc.ps[:, :], onehb[:, pair * 128:(pair + 1) * 128], reclo[:, :],
                   start=False, stop=True, inc=True)
                rbv = pe.n
                bnum = num_banks[pair]
                if pair == 7:
                    bnum.acquire(ctx)
                    mm(bnum.ps[:, :], kvs[:, pair * 128:(pair + 1) * 128],
                       phiq[:, pair * 512:(pair + 1) * 512], start=True, stop=True,
                       inc=True)
                    ctx.wait("vector", pe, pe.n)
                else:
                    ctx.wait("vector", pe, rbv)
                recb.wait_free(ctx, "vector")
                vcopy(recbc[:, :], brc.ps[:, :], ve)
                brc.release(ve, ve.n)
                tt("vector", attnf[:, pair * 512:(pair + 1) * 512],
                   bnum.ps[:, :], recbc[:, :], ALU.mult, sem=ve)
                bnum.release(ve, ve.n)
                recb.free = (ve, ve.n)
            attn_done = ve.n
            c_state[cb] = attn_done

        def stage_wo(cb):
            attn_done = c_state[cb]
            ctx.wait("tensor", dqw, wo_ld)
            ctx.wait("tensor", ve, attn_done)
            for tc in range(4):
                jj = cb * 4 + tc
                if pend_hs[0] is not None:
                    pj, pbuf = pend_hs[0]
                    tbk = banks[2 + pj % 2]
                    tbk.acquire(ctx)
                    ctx.wait("tensor", se, pbuf.hs_se)
                    tb_bf = tbk.ps.bitcast(BF16)
                    pap = pbuf.ap.bitcast(BF16) if pbuf is recb else pbuf.ap
                    for c in range(CH):
                        ctx.op("tensor",
                               lambda E, o=tb_bf[:, c * 128:(c + 1) * 128],
                               i=pap[:, c * 128:(c + 1) * 128]:
                               E.transpose(o, i, eyeb[:, :]), pe if c == CH - 1 else None)
                    tpv = pe.n
                    pbuf.free = (pe, tpv)
                    ctx.wait("vector", pe, tpv)
                    dview = FM.rearrange("p (c t) -> p c t", c=CH)[:, :, pj * 128:(pj + 1) * 128]
                    vcopy(dview, tb_bf[:, 0:1024].rearrange("p (c t) -> p c t", c=CH), ve)
                    tbk.release(ve, ve.n)
                    pend_hs[0] = None
                for grp in range(2):
                    b = pj_banks[grp]
                    b.acquire(ctx)
                    for c in range(CH):
                        mm(b.ps[:, :],
                           attnf[:, c * 512 + tc * 128:c * 512 + (tc + 1) * 128],
                           woS[:, c * 1024 + grp * 512:c * 1024 + (grp + 1) * 512],
                           start=(c == 0), stop=(c == CH - 1), inc=(c == CH - 1))
                    mmv = pe.n
                    ctx.wait("vector", pe, mmv)
                    rsl = RES[:, jj * D + grp * 512:jj * D + (grp + 1) * 512]
                    tt("vector", rsl, b.ps[:, :], rsl, ALU.add, sem=ve)
                    b.release(ve, ve.n)
                mvc = colA if jj % 2 == 0 else colB
                rtile = RES[:, jj * D:(jj + 1) * D]
                ctx.wait("vector", ve)
                ctx.wait("vector", se)
                ctx.op("vector", lambda E, o=mvc, r=rtile: E.bn_stats(o[:, 4:10], r[:, 0:512]))
                ctx.op("vector", lambda E, o=mvc, r=rtile: E.bn_stats(o[:, 10:16], r[:, 512:1024]), ve)
                ctx.wait("vector", ve)
                ctx.op("vector", lambda E, o=mvc: E.bn_aggr(o[:, 0:2], o[:, 4:16]), ve)
                ctx.wait("vector", ve)
                ts1("vector", mvc[:, 2:3], mvc[:, 1:2], EPS, ALU.add, sem=ve)
                ctx.wait("scalar", ve)
                act(mvc[:, 2:3], mvc[:, 2:3], AF.Ln, sem=se)
                ctx.wait("scalar", se)
                act(mvc[:, 3:4], mvc[:, 2:3], AF.Exp, scale=-0.5, sem=se)
                ctx.wait("vector", se)
                stt("vector", mvc[:, 2:3], mvc[:, 0:1], -1.0, mvc[:, 3:4],
                    ALU.mult, ALU.mult, sem=ve)
                hbuf = hsb if jj % 2 == 0 else recb
                hbuf.wait_free(ctx, "scalar")
                ctx.wait("scalar", ve)
                hap = hbuf.ap.bitcast(BF16) if hbuf is recb else hbuf.ap
                act(hap[:, :], rtile, AF.Identity, bias=mvc[:, 2:3],
                    scale=mvc[:, 3:4], sem=se)
                hbuf.hs_se = se.n
                pend_hs[0] = (jj, hbuf)
            pj, pbuf = pend_hs[0]
            tbk = banks[2 + pj % 2]
            tbk.acquire(ctx)
            ctx.wait("tensor", se, pbuf.hs_se)
            tb_bf = tbk.ps.bitcast(BF16)
            pap = pbuf.ap.bitcast(BF16) if pbuf is recb else pbuf.ap
            for c in range(CH):
                ctx.op("tensor",
                       lambda E, o=tb_bf[:, c * 128:(c + 1) * 128],
                       i=pap[:, c * 128:(c + 1) * 128]:
                       E.transpose(o, i, eyeb[:, :]), pe if c == CH - 1 else None)
            tpv = pe.n
            pbuf.free = (pe, tpv)
            ctx.wait("vector", pe, tpv)
            dview = FM.rearrange("p (c t) -> p c t", c=CH)[:, :, pj * 128:(pj + 1) * 128]
            vcopy(dview, tb_bf[:, 0:1024].rearrange("p (c t) -> p c t", c=CH), ve)
            tbk.release(ve, ve.n)
            pend_hs[0] = None

        stage_q(0)
        stage_attn(0)
        for cb in range(4):
            if cb < 3:
                stage_q(cb + 1)
            stage_wo(cb)
            if cb < 3:
                stage_attn(cb + 1)

        PHASE_MARKS["C"] = len(ctx.plan["tensor"])
        hs_tp = ve.n
        hs_tp2 = 0
        phaseC_pe = pe.n
        phaseC_ve = ve.n
        phaseC_se = se.n

        if upto == "C":
            _finish(ctx, locals())
            _replay(ctx, nc)
            return nc

        fstgb = Buf(fstg)
        fstgb.free = (ve, phaseC_ve)
        vb1_banks = [banks[4], banks[5], banks[6], banks[7]]
        f_first = [True]

        def fold_w1G(G):
            ctx.wait("gpsimd", se, se.n)
            ctx.wait("gpsimd", ve, ve.n)
            for c in range(CH):
                fstgb.wait_free(ctx, "gpsimd")
                if f_first[0]:
                    ctx.wait("gpsimd", se, phaseC_se)
                    ctx.wait("gpsimd", ve, hs_tp)
                ctx.dma("gpsimd", fstg[:, :],
                        w1d[c * 128:(c + 1) * 128, G * 2048:(G + 1) * 2048], dqw)
                ctx.wait("vector", dqw, dqw.n)
                if f_first[0]:
                    ctx.wait("vector", pe, phaseC_pe)
                    ctx.wait("vector", ve, hs_tp)
                    f_first[0] = False
                ctx.wait("vector", pe)
                ts1("vector", WA[:, c * 2048:(c + 1) * 2048], fstg[:, :],
                    g2c[:, c:c + 1], ALU.mult)
                vcopy(g1G[:, 0:2048], fstg[:, :], ve)
                fstgb.free = (ve, ve.n)
                if c % 2 == 1:
                    qc = c // 2
                    if qc == 0 and G == 0:
                        ctx.wait("gpsimd", pe, phaseC_pe)
                    ctx.dma("gpsimd",
                            w2G[:, qc * 4096:(qc + 1) * 4096].rearrange("p (c n) -> p c n", c=4),
                            w2d[G * 2048 + qc * 512:G * 2048 + (qc + 1) * 512, :]
                            .rearrange("(c p) n -> p c n", p=128), dqw)
                ctx.wait("tensor", ve, ve.n)
                for g in range(4):
                    b = vb1_banks[g]
                    if c == 0:
                        b.acquire(ctx)
                    mm(b.ps[0:1, :], b2cb[:, c:c + 1], g1G[:, g * 512:(g + 1) * 512],
                       start=(c == 0), stop=(c == CH - 1), inc=(c == CH - 1))
            ctx.wait("vector", pe)
            for g in range(4):
                if g > 0:
                    ctx.wait("vector", dqt)
                vcopy(recsb[0:1, 0:512], vb1_banks[g].ps[0:1, :], ve)
                vb1_banks[g].release(ve, ve.n)
                ctx.wait("sync", ve, ve.n)
                ctx.dma("sync",
                        vbd[0:1, 3072 + G * 2048 + g * 512:3072 + G * 2048 + (g + 1) * 512],
                        recsb[0:1, 0:512], dqt)
            ctx.wait("scalar", dqt, dqt.n)
            ctx.dma("scalar", vbw1c[:, G * 16:(G + 1) * 16],
                    vbd[0:1, 3072 + G * 2048:3072 + (G + 1) * 2048]
                    .rearrange("a (c p) -> (a p) c", p=128), dqx, slow=True)
            ctx.wait("vector", dqx, dqx.n)
            ts1("vector", vbw1s[:, G * 16:(G + 1) * 16], vbw1c[:, G * 16:(G + 1) * 16],
                10.0, ALU.mult, sem=ve)
            return ve.n

        ftmp = [fstg[:, 0:512], fstg[:, 512:1024]]

        x0db = [Buf(fstg2[:, 0:1024]), Buf(fstg2[:, 1024:2048])]
        for bb in x0db:
            bb.free = [(se, se.n), (ve, ve.n)]
        gw_ld = [0]
        d0_pend = []

        def d0_tile(jj):
            xb = x0db[jj % 2]
            xb.wait_free(ctx, "scalar")
            ctx.wait("scalar", ve, ve.n)
            ctx.dma("scalar", xb.ap[:, :], xd[jj * 128:(jj + 1) * 128, :], dqx)
            xv = dqx.n
            ctx.wait("vector", dqx, xv)
            rtile = RES[:, jj * D:(jj + 1) * D]
            tt("vector", rtile, rtile, xb.ap[:, :], ALU.add, sem=ve)
            mvc = colA if jj % 2 == 0 else colB
            ctx.wait("vector", se)
            ctx.op("vector", lambda E, o=mvc, r=rtile: E.bn_stats(o[:, 4:10], r[:, 0:512]))
            ctx.op("vector", lambda E, o=mvc, r=rtile: E.bn_stats(o[:, 10:16], r[:, 512:1024]), ve)
            ctx.wait("vector", ve)
            ctx.op("vector", lambda E, o=mvc: E.bn_aggr(o[:, 0:2], o[:, 4:16]), ve)
            ctx.wait("vector", ve)
            ts1("vector", mvc[:, 2:3], mvc[:, 1:2], EPS, ALU.add, sem=ve)
            ctx.wait("scalar", ve)
            act(mvc[:, 2:3], mvc[:, 2:3], AF.Ln, sem=se)
            ctx.wait("scalar", se)
            act(mvc[:, 3:4], mvc[:, 2:3], AF.Exp, scale=-0.5, sem=se)
            ctx.wait("vector", se)
            stt("vector", mvc[:, 2:3], mvc[:, 0:1], -1.0, mvc[:, 3:4],
                ALU.mult, ALU.mult, sem=ve)
            t0ap = xb.ap.bitcast(BF16)[:, 0:1024]
            ctx.wait("scalar", ve)
            act(t0ap, rtile, AF.Identity, bias=mvc[:, 2:3], scale=mvc[:, 3:4], sem=se)
            d0_pend.append((jj, xb, t0ap, se.n))

        def d0_pe():
            jj, xb, t0ap, t0se = d0_pend.pop(0)
            tbk = banks[6 + jj % 2]
            tbk.acquire(ctx)
            ctx.wait("tensor", se, t0se)
            tb_bf = tbk.ps.bitcast(BF16)
            for c in range(CH):
                ctx.op("tensor",
                       lambda E, o=tb_bf[:, c * 128:(c + 1) * 128],
                       i=t0ap[:, c * 128:(c + 1) * 128]:
                       E.transpose(o, i, eyeb[:, :]), pe if c == CH - 1 else None)
            tpv = pe.n
            xb.free = (pe, tpv)
            ctx.wait("vector", pe, tpv)
            dview = FM.rearrange("p (c t) -> p c t", c=CH)[:, :, jj * 128:(jj + 1) * 128]
            vcopy(dview, tb_bf[:, 0:1024].rearrange("p (c t) -> p c t", c=CH), ve)
            t0fm_ve = ve.n
            tbk.release(ve, t0fm_ve)
            bg = banks[6 + jj % 2]
            bg.acquire(ctx)
            ctx.wait("tensor", ve, t0fm_ve)
            ctx.wait("tensor", dqw, gw_ld[0])
            for c in range(CH):
                mm(bg.ps[:, 0:8],
                   FM[:, c * S_OWN + jj * 128:c * S_OWN + (jj + 1) * 128],
                   gwsb[:, c * 8:(c + 1) * 8],
                   start=(c == 0), stop=(c == CH - 1), inc=(c == CH - 1))
            ctx.wait("vector", pe, pe.n)
            ctx.wait("vector", se)
            vcopy(scsb[:, 0:8], bg.ps[:, 0:8], ve)
            bg.release(ve, ve.n)
            ctx.wait("vector", ve)
            vred(sccol[:, 0:1], scsb[:, 0:8], ALU.max, sem=ve)
            ctx.wait("vector", ve)
            ts1("vector", sccol[:, 1:2], sccol[:, 0:1], -1.0, ALU.mult, sem=ve)
            ctx.wait("scalar", ve)
            act(scsb[:, 8:16], scsb[:, 0:8], AF.Exp, bias=sccol[:, 1:2], sem=se)
            ctx.wait("vector", se)
            vred(sccol[:, 2:3], scsb[:, 8:16], ALU.add, sem=ve)
            ctx.wait("vector", ve)
            ctx.op("vector", lambda E: E.reciprocal(sccol[:, 3:4], sccol[:, 2:3]), ve)
            ctx.wait("vector", ve)
            ts1("vector", scsb[:, 0:8], scsb[:, 8:16], sccol[:, 3:4], ALU.mult, sem=ve)
            ctx.wait("vector", ve)
            vred(sccol[:, 4:5], scsb[:, 0:8], ALU.max, sem=ve)
            ctx.wait("vector", ve)
            ts1("vector", scsb[:, 8:16], scsb[:, 0:8], sccol[:, 4:5], ALU.is_ge, sem=ve)
            ctx.wait("vector", ve)
            stt("vector", scsb[:, 8:16], scsb[:, 8:16], -1e9, scsb[:, 0:8],
                ALU.mult, ALU.add, sem=ve)
            ctx.wait("vector", ve)
            vred(sccol[:, 5:6], scsb[:, 8:16], ALU.max, sem=ve)
            ctx.wait("vector", ve)
            ts1("vector", scsb[:, 8:16], scsb[:, 0:8], sccol[:, 5:6], ALU.is_ge, sem=ve)
            ctx.wait("vector", ve)
            tt("vector", wdsb[:, jj * 8:(jj + 1) * 8], scsb[:, 8:16], scsb[:, 0:8],
               ALU.mult, sem=ve)

        for G in range(2):
            vbv = fold_w1G(G)
            w2v = dqw.n
            if G == 1:
                ctx.dma("gpsimd", gwsb[:, :].rearrange("p (c n) -> p c n", c=CH),
                        gwd.rearrange("(c p) n -> p c n", p=128), dqw)
                gw_ld[0] = dqw.n
            for fb in range(4):
                tok0 = fb * 512
                w1_banks = [banks[0], banks[1], banks[4]]
                for cc in range(16):
                    b = w1_banks[cc % 3]
                    b.acquire(ctx)
                    ctx.wait("tensor", ve, vbv)
                    if G == 0 and fb == 0 and cc == 0:
                        ctx.wait("tensor", ve, hs_tp)
                    for c in range(CH):
                        mm(b.ps[:, :],
                           WA[:, c * 2048 + cc * 128:c * 2048 + (cc + 1) * 128],
                           FM[:, c * S_OWN + tok0:c * S_OWN + tok0 + 512],
                           start=(c == 0), stop=(c == CH - 1), inc=(c == CH - 1))
                    mmv = pe.n
                    tmp = ftmp[cc % 2]
                    ctx.wait("scalar", pe, mmv)
                    ctx.wait("scalar", ve)
                    act(tmp, b.ps[:, :], AF.Identity, scale=10.0,
                        bias=vbw1s[:, G * 16 + cc:G * 16 + cc + 1], sem=se)
                    b.release(se, se.n)
                    ctx.wait("vector", se)
                    ts2("vector", tmp, tmp, RC, -RC, ALU.add, ALU.add, sem=ve)
                    ctx.wait("vector", ve)
                    ts2("vector", tmp, tmp, 127.0, -128.0, ALU.min, ALU.max, sem=ve)
                    ctx.wait("scalar", ve)
                    act(g1G[:, cc * 512:(cc + 1) * 512], tmp,
                        AF.Identity if DBG_FFN_IDENT[0] else AF.Gelu_apprx_tanh,
                        scale=0.1, sem=se)
                g1_done = se.n
                ctx.wait("tensor", se, g1_done)
                ctx.wait("tensor", dqw, w2v)
                for tc in range(4):
                    jj = fb * 4 + tc
                    for grp in range(2):
                        b = [banks[2], banks[3], banks[5]][(tc * 2 + grp) % 3]
                        b.acquire(ctx)
                        for cc in range(16):
                            mm(b.ps[:, :],
                               g1G[:, cc * 512 + tc * 128:cc * 512 + (tc + 1) * 128],
                               w2G[:, cc * 1024 + grp * 512:cc * 1024 + (grp + 1) * 512],
                               start=(cc == 0), stop=(cc == 15), inc=(cc == 15))
                        ctx.wait("vector", pe, pe.n)
                        rsl = RES[:, jj * D + grp * 512:jj * D + (grp + 1) * 512]
                        tt("vector", rsl, b.ps[:, :], rsl, ALU.add, sem=ve)
                        b.release(ve, ve.n)
                    if len(d0_pend) >= 2:
                        d0_pe()
                    if G == 1:
                        d0_tile(jj)
        ffn_pe = pe.n
        ffn_ve = ve.n

        PHASE_MARKS["F"] = len(ctx.plan["tensor"])
        while d0_pend:
            d0_pe()
        wd_ve = ve.n
        x1_ve = ve.n

        if upto == "F":
            _finish(ctx, locals())
            _replay(ctx, nc)
            return nc

        u_tms = [WB[:, 12288:13056], WB[:, 13824:14592]]
        u_fms = [WB[:, 13056:13824], WB[:, 14592:15360]]
        silu_bf = WB[:, 15360:16128]
        W1b = [Buf(W1xp_aps[0]), Buf(W1xp_aps[1])]
        W2b = [Buf(W2xp_aps[0]), Buf(W2xp_aps[1])]
        W1b[0].free = (pe, ffn_pe)
        W1b[1].free = [(ve, x1_ve), (pe, ffn_pe)]
        W2b[0].free = (ve, wd_ve)
        W2b[1].free = (ve, wd_ve)
        u_tmbs = [Buf(u_tms[0]), Buf(u_tms[1])]
        u_fmbs = [Buf(u_fms[0]), Buf(u_fms[1])]
        for bb in u_tmbs + u_fmbs:
            bb.free = [(ve, x1_ve), (pe, ffn_pe)]
        h_banks = [[banks[0], banks[1], banks[2]], [banks[3], banks[4], banks[5]]]
        eo_banks = [banks[6], banks[7]]

        for e0, wbx, dram, nch in ((0, W1b[0], W1xd, CH), (0, W2b[0], W2xd, 6)):
            wbx.wait_free(ctx, "gpsimd")
            ctx.dma("gpsimd", wbx.ap.rearrange("p (c n) -> p c n", c=nch),
                    dram[0].rearrange("(c p) n -> p c n", p=128), dqw)
            wbx.ready = (dqw, dqw.n)

        def prefetch_chunk(e1, which, c):
            if e1 >= NEXP:
                return
            wbx = (W1b if which == 1 else W2b)[e1 % 2]
            dram = W1xd if which == 1 else W2xd
            ncols = 1536 if which == 1 else 1024
            if c == 0:
                wbx.wait_free(ctx, "gpsimd")
            ctx.wait("gpsimd", pe, pe.n)
            ctx.dma("gpsimd", wbx.ap[:, c * ncols:(c + 1) * ncols],
                    dram[e1, c * 128:(c + 1) * 128, :], dqw)
            nmax = CH if which == 1 else 6
            if c == nmax - 1:
                wbx.ready = (dqw, dqw.n)

        for e in range(NEXP):
            wb1 = W1b[e % 2]
            wb2 = W2b[e % 2]
            wb1.wait_ready(ctx, "tensor")
            ctx.wait("tensor", ve, wd_ve)
            last_h = [None]
            last_eo = [None]
            pend = []

            def emit_eo(tca, ui):
                fb_ = u_fmbs[ui]
                fb_.wait_ready(ctx, "tensor")
                wb2.wait_ready(ctx, "tensor")
                for grp in range(2):
                    b = eo_banks[grp]
                    b.acquire(ctx)
                    for cc in range(6):
                        mm(b.ps[:, :], u_fms[ui][:, cc * 128:(cc + 1) * 128],
                           wb2.ap[:, cc * 1024 + grp * 512:cc * 1024 + (grp + 1) * 512],
                           start=(cc == 0), stop=(cc == 5), inc=(cc == 5))
                    mmv2 = pe.n
                    last_eo[0] = mmv2
                    ctx.wait("vector", pe, mmv2)
                    rsl = RES[:, tca * D + grp * 512:tca * D + (grp + 1) * 512]
                    tt("vector", rsl, b.ps[:, :], rsl, ALU.add, sem=ve)
                    b.release(ve, ve.n)
                fb_.free = (pe, last_eo[0])

            for tca in range(16):
                upi = tca % 2
                hb = h_banks[tca % 2]
                trio = trios[tca % 2]
                gate_pe = None
                for ng in (1, 2, 0):
                    b = hb[ng]
                    b.acquire(ctx)
                    for c in range(CH):
                        mm(b.ps[:, :],
                           FM[:, c * S_OWN + tca * 128:c * S_OWN + (tca + 1) * 128],
                           wb1.ap[:, c * 1536 + ng * 512:c * 1536 + (ng + 1) * 512],
                           start=(c == 0), stop=(c == CH - 1), inc=(c == CH - 1))
                    if ng == 2:
                        gate_pe = pe.n
                mmv = pe.n
                last_h[0] = mmv
                ctx.wait("scalar", pe, gate_pe)
                ctx.wait("scalar", ve)
                act(silu_bf[:, 0:768], trio[:, 768:1536], AF.Silu, sem=se)
                sv = se.n
                ub = u_tmbs[upi]
                ub.wait_free(ctx, "vector")
                ctx.wait("vector", se, sv)
                ctx.wait("vector", pe, mmv)
                ut = u_tms[upi]
                if e < 8:
                    wdcol = wdsb[:, tca * 8 + e:tca * 8 + e + 1]
                    stt("vector", ut[:, 0:768], trio[:, 0:768], wdcol,
                        silu_bf[:, 0:768], ALU.mult, ALU.mult, sem=ve)
                else:
                    tt("vector", ut[:, 0:768], trio[:, 0:768],
                       silu_bf[:, 0:768], ALU.mult, sem=ve)
                uv = ve.n
                ub.ready = (ve, uv)
                hb[1].release(ve, uv)
                hb[2].release(ve, uv)
                if len(pend) >= 2:
                    tca_, ui_ = pend.pop(0)
                    emit_eo(tca_, ui_)
                fb_ = u_fmbs[upi]
                ctx.wait("tensor", ve, uv)
                tb_bf = hb[0].ps.bitcast(BF16)
                for c in range(6):
                    ctx.op("tensor",
                           lambda E, o=tb_bf[:, c * 128:(c + 1) * 128],
                           i=ut[:, c * 128:(c + 1) * 128]:
                           E.transpose(o, i, eyeb[:, :]), pe if c == 5 else None)
                tpv = pe.n
                ub.free = (pe, tpv)
                fb_.wait_free(ctx, "scalar")
                ctx.wait("scalar", pe, tpv)
                act(u_fms[upi][:, 0:768], tb_bf[:, 0:768], AF.Identity, sem=se)
                hb[0].release(se, se.n)
                fb_.ready = (se, se.n)
                pend.append((tca, upi))
                if tca % 2 == 1:
                    prefetch_chunk(e + 1, 1, tca // 2)
                elif 2 <= tca <= 12:
                    prefetch_chunk(e + 1, 2, tca // 2 - 1)
            while pend:
                tca_, ui_ = pend.pop(0)
                emit_eo(tca_, ui_)
            wb1.free = (pe, last_h[0])
            wb2.free = (pe, last_eo[0])
        _finish(ctx, locals())
        _replay(ctx, nc)
        return nc


def _finish(ctx, env):
    pe, ve, se = env["pe"], env["ve"], env["se"]
    dqo, dqt, dqx, dqw = env["dqo"], env["dqt"], env["dqx"], env["dqw"]
    RES, outd = env["RES"], env["outd"]
    for s in (pe, ve, se, dqt, dqx, dqw):
        ctx.wait("scalar", s)
    tap_aps = {
        "as_fm": env["FM"], "hsfm": env["FM"], "t0fm": env["FM"],
        "kvsb": env["kvsb"], "res": env["RES"], "phik": env["phik"],
        "vtm": env["vtm"], "phiq": env["phiq"], "attn": env["attnf"],
        "rec": env["recsb"], "wd": env["wdsb"], "vbq": env["vbqc"],
        "vbkv": env["VBKV"], "wkvs": env["WA"], "wqs": env["wqS"],
        "g1g": env["g1G"], "zmat": env["zmat"], "kvsq": env["kvs"],
    }
    nc = ctx.nc
    for t in env["taps"]:
        src = tap_aps[t]
        shp = [src.shape[0], src.shape[1]]
        dt = src.dtype
        td = nc.dram_tensor("tap_" + t, shp, dt, kind="ExternalOutput").ap()
        ctx.dma("scalar", td[:, :], src, dqo)
    for jj in range(16):
        ctx.dma("scalar", outd[jj * 128:(jj + 1) * 128, :],
                RES[:, jj * 1024:(jj + 1) * 1024], dqo)
    ctx.wait("scalar", dqo)
    ctx.wait("sync", dqo)


def _replay(ctx, nc):
    with nc.Block() as block:
        block.sync(lambda E: [f(E) for f in ctx.plan["sync"]])
        block.scalar(lambda E: [f(E) for f in ctx.plan["scalar"]])
        block.vector(lambda E: [f(E) for f in ctx.plan["vector"]])
        block.tensor(lambda E: [f(E) for f in ctx.plan["tensor"]])
        block.gpsimd(lambda E: [f(E) for f in ctx.plan["gpsimd"]])


def _host_layout(inputs):
    x = np.asarray(inputs["L0_attn_x"], np.float32)
    wq = np.ascontiguousarray(np.asarray(inputs["L0_attn_wq"], np.float32))
    wk = np.asarray(inputs["L0_attn_wk"], np.float32)
    wv = np.asarray(inputs["L0_attn_wv"], np.float32)
    wkv = np.ascontiguousarray(np.concatenate([wk, wv], axis=1))
    wo = np.ascontiguousarray(np.asarray(inputs["L0_attn_wo"], np.float32))
    w1 = np.ascontiguousarray(np.asarray(inputs["L0_attn_w1"], np.float32))
    w2 = np.ascontiguousarray(np.asarray(inputs["L0_attn_w2"], np.float32))
    gw = np.ascontiguousarray(np.asarray(inputs["L0_ffn_gate_w"], np.float32).T)
    W1 = np.asarray(inputs["L0_ffn_W1"], np.float32)
    W2 = np.asarray(inputs["L0_ffn_W2"], np.float32)
    Ws1 = np.asarray(inputs["L0_ffn_Ws1"], np.float32)
    Ws2 = np.asarray(inputs["L0_ffn_Ws2"], np.float32)
    W1t = W1.transpose(0, 2, 1)
    W2t = W2.transpose(0, 2, 1)
    Ws1t = Ws1.T
    Ws2t = Ws2.T
    W1x = np.zeros((NEXP, D, 1536), np.float32)
    W2x = np.zeros((NEXP, 768, D), np.float32)
    W1x[:8] = W1t
    W2x[:8] = W2t
    W1x[8] = np.concatenate([Ws1t[:, 0:768], Ws1t[:, 1536:2304]], axis=1)
    W1x[9] = np.concatenate([Ws1t[:, 768:1536], Ws1t[:, 2304:3072]], axis=1)
    W2x[8] = Ws2t[0:768]
    W2x[9] = Ws2t[768:1536]
    oneh = np.zeros((16, 1024), np.float32)
    for p_ in range(8):
        oneh[2 * p_, p_ * 128:p_ * 128 + 64] = 1.0
        oneh[2 * p_ + 1, p_ * 128 + 64:p_ * 128 + 128] = 1.0
    shared = dict(
        eye=np.eye(128, dtype=np.float32),
        oneh=oneh,
        wq=wq, wkv=wkv, wo=wo, w1=w1, w2=w2, gw=gw,
        W1x=np.ascontiguousarray(W1x), W2x=np.ascontiguousarray(W2x),
        g1=np.ascontiguousarray(np.asarray(inputs["L0_attn_ln1_g"], np.float32)),
        b1=np.ascontiguousarray(np.asarray(inputs["L0_attn_ln1_b"], np.float32)),
        g2=np.ascontiguousarray(np.asarray(inputs["L0_attn_ln2_g"], np.float32)),
        b2=np.ascontiguousarray(np.asarray(inputs["L0_attn_ln2_b"], np.float32)),
    )
    in_maps = []
    for c in range(8):
        b, h = c // 2, c % 2
        own = x[b, h * 2048:(h + 1) * 2048]
        oth = x[b, (1 - h) * 2048:(1 - h) * 2048 + 2048]
        m = dict(shared)
        m["x"] = np.ascontiguousarray(np.concatenate([own, oth], axis=0))
        in_maps.append(m)
    return in_maps


def kernel(**inputs):
    in_maps = _host_layout(inputs)
    nc = build_program()
    res = run_bass_kernel_spmd(nc, in_maps, core_ids=list(range(8)))
    out = np.zeros((4, 4096, 1024), np.float32)
    for c in range(8):
        b, h = c // 2, c % 2
        out[b, h * 2048:(h + 1) * 2048] = res.results[c]["out"]
    return out


# version 16
# speedup vs baseline: 1.3101x; 1.0041x over previous
import contextlib
import numpy as np

import concourse.bass as bass
import concourse.mybir as mybir
from concourse.bass_utils import run_bass_kernel_spmd

F32 = mybir.dt.float32
BF16 = mybir.dt.bfloat16
AF = mybir.ActivationFunctionType
ALU = mybir.AluOpType
AX = mybir.AxisListType

D = 1024
DBG_FFN_IDENT = [False]
PHASE_MARKS = {}
DBG_EO_INLINE = [False]
DBG_D1_TP_SYNC = [False]
DBG_D1_NOPING = [False]
DBG_D0_TP_SYNC = [False]
CH = 8
S_ALL = 4096
S_OWN = 2048
EPS = 1e-5
RC = 12582912.0
NEXP = 10

ENGS = ("sync", "scalar", "vector", "tensor", "gpsimd")


class SemT:
    __slots__ = ("h", "n")

    def __init__(self, h):
        self.h = h
        self.n = 0


class Ctx:
    def __init__(self, nc, es):
        self.nc = nc
        self.es = es
        self.plan = {e: [] for e in ENGS}
        self.lastwait = {}

    def sem(self, name):
        return SemT(self.es.enter_context(self.nc.semaphore(name)))

    def emit(self, eng, fn):
        self.plan[eng].append(fn)

    def wait(self, eng, sem, val=None):
        v = sem.n if val is None else val
        if v <= 0:
            return
        key = (eng, id(sem))
        if self.lastwait.get(key, -1) >= v:
            return
        self.lastwait[key] = v
        self.emit(eng, lambda E, s=sem.h, v=v: E.wait_ge(s, v))

    def dma(self, eng, out, in_, sem, transpose=False, slow=False):
        sem.n += 16

        def f(E, out=out, in_=in_, t=transpose, s=sem.h, sl=slow):
            E.dma_start(out=out, in_=in_, transpose=t,
                        allow_slow_non_contiguous=sl).then_inc(s, 16)

        self.emit(eng, f)
        return sem.n

    def op(self, eng, builder, sem=None):
        if sem is not None:
            sem.n += 1
        sh = sem.h if sem is not None else None

        def f(E, b=builder, sh=sh):
            i = b(E)
            if sh is not None:
                i.then_inc(sh, 1)

        self.emit(eng, f)
        return sem.n if sem is not None else None


class Bank:
    def __init__(self, ps):
        self.ps = ps
        self.free = None

    def acquire(self, ctx):
        if self.free is not None:
            ctx.wait("tensor", self.free[0], self.free[1])

    def release(self, sem, val):
        self.free = (sem, val)


class Buf:
    def __init__(self, ap):
        self.ap = ap
        self.free = None
        self.ready = None

    @staticmethod
    def _w(ctx, eng, deps):
        if deps is None:
            return
        if isinstance(deps, list):
            for s, v in deps:
                ctx.wait(eng, s, v)
        else:
            ctx.wait(eng, deps[0], deps[1])

    def wait_free(self, ctx, eng):
        self._w(ctx, eng, self.free)

    def wait_ready(self, ctx, eng):
        self._w(ctx, eng, self.ready)


def build_program(upto="E", taps=()):
    nc = bass.Bass("TRN2", target_bir_lowering=False)
    taps = list(taps)

    xd = nc.dram_tensor("x", [S_ALL, D], F32, kind="ExternalInput").ap()
    wqd = nc.dram_tensor("wq", [D, D], F32, kind="ExternalInput").ap()
    wkvd = nc.dram_tensor("wkv", [D, 2 * D], F32, kind="ExternalInput").ap()
    wod = nc.dram_tensor("wo", [D, D], F32, kind="ExternalInput").ap()
    w1d = nc.dram_tensor("w1", [D, 4096], F32, kind="ExternalInput").ap()
    w2d = nc.dram_tensor("w2", [4096, D], F32, kind="ExternalInput").ap()
    gwd = nc.dram_tensor("gw", [D, 8], F32, kind="ExternalInput").ap()
    W1xd = nc.dram_tensor("W1x", [NEXP, D, 1536], F32, kind="ExternalInput").ap()
    W2xd = nc.dram_tensor("W2x", [NEXP, 768, D], F32, kind="ExternalInput").ap()
    eyed = nc.dram_tensor("eye", [128, 128], F32, kind="ExternalInput").ap()
    onehd = nc.dram_tensor("oneh", [16, 1024], F32, kind="ExternalInput").ap()
    g1d = nc.dram_tensor("g1", [D], F32, kind="ExternalInput").ap()
    b1d = nc.dram_tensor("b1", [D], F32, kind="ExternalInput").ap()
    g2d = nc.dram_tensor("g2", [D], F32, kind="ExternalInput").ap()
    b2d = nc.dram_tensor("b2", [D], F32, kind="ExternalInput").ap()
    outd = nc.dram_tensor("out", [S_OWN, D], F32, kind="ExternalOutput").ap()
    vbd = nc.dram_tensor("vbd", [1, 8192], F32).ap()
    recd = nc.dram_tensor("recd", [16, 512], F32).ap()

    es = contextlib.ExitStack()
    with es:
        ctx = Ctx(nc, es)
        sb = lambda name, shape, dt: es.enter_context(
            nc.sbuf_tensor(name, shape, dt)).ap()

        dqx = ctx.sem("dqx")
        dqt = ctx.sem("dqt")
        dqt2 = ctx.sem("dqt2")
        dqw = ctx.sem("dqw")
        pe = ctx.sem("pe")
        ve = ctx.sem("ve")
        se = ctx.sem("se")
        ge = ctx.sem("ge")
        dqo = ctx.sem("dqo")

        RES = sb("RES", [128, 16 * D], F32)
        FM = sb("FM", [128, CH * S_OWN], BF16)
        g1c = sb("g1c", [128, 8], F32)
        b1c = sb("b1c", [128, 8], F32)
        g2c = sb("g2c", [128, 8], F32)
        b2c = sb("b2c", [128, 8], F32)
        b1cb = sb("b1cb", [128, 8], BF16)
        b2cb = sb("b2cb", [128, 8], BF16)
        vbqc = sb("vbqc", [128, 8], F32)
        vbw1c = sb("vbw1c", [128, 32], F32)
        vbw1s = sb("vbw1s", [128, 32], F32)
        ones1 = sb("ones1", [128, 1], BF16)
        eyeb = sb("eyeb", [128, 128], BF16)
        onehb = sb("onehb", [16, 1024], BF16)
        rechi = sb("rechi", [16, 512], BF16)
        reclo = sb("reclo", [16, 512], BF16)
        fstg2 = sb("fstg2", [128, 2048], F32)
        gwsb = sb("gwsb", [128, 64], BF16)
        kvsb = sb("kvsb", [128, 8 * 132], F32)
        kvs = sb("kvs", [128, 8 * 128], BF16)
        zmat = sb("zmat", [128, 8 * 16], BF16)
        wdsb = sb("wdsb", [128, 16 * 8], F32)
        colA = sb("colA", [128, 48], F32)
        colB = sb("colB", [128, 48], F32)
        recsb = sb("recsb", [16, 512], F32)
        scsb = sb("scsb", [128, 16], F32)
        sccol = sb("sccol", [128, 8], F32)

        WA = sb("WA", [128, 16384], BF16)
        WB = sb("WB", [128, 16384], BF16)
        SCR = sb("SCR", [128, 12288], BF16)

        wqS = WB[:, 0:8192]
        vbkvr = WB[0:1, 8192:10240]
        onesr = WB[0:1, 14336:14464]
        VBKV = WB[:, 8192:12288].bitcast(F32)
        tmpA = WB[:, 12288:13312].bitcast(F32)
        tmpB = WB[:, 13312:14336].bitcast(F32)
        woS = WB[:, 8192:16384]
        w2G = WB
        W1xp_aps = [WA[:, 0:12288], WB[:, 0:12288]]
        u_tm = WB[:, 12288:13824]
        u_tm = WB[:, 12288:13056]
        u_fm = WB[:, 13056:13824]
        silu_sb = WB[:, 13824:15360].bitcast(F32)

        x0p = [SCR[:, 0:2048].bitcast(F32), SCR[:, 2048:4096].bitcast(F32)]
        astp = [SCR[:, 4096:5120], SCR[:, 5120:6144]]
        afsp = [SCR[:, 6144:7168], SCR[:, 7168:8192]]
        phik = SCR[:, 8192:10240]
        vtm = SCR[:, 10240:12288]
        stg = [SCR[:, 0:4096].bitcast(F32), SCR[:, 4096:8192].bitcast(F32)]
        plainb = SCR[:, 8192:10240]
        phiq = SCR[:, 0:4096]
        attnf = SCR[:, 4096:8192]
        recbc = SCR[:, 8192:9216].bitcast(F32)
        ctmpA = SCR[:, 9216:10240].bitcast(F32)
        ctmpB = SCR[:, 10240:11264].bitcast(F32)
        hst = SCR[:, 11264:12288]
        g1G = SCR[:, 0:8192]
        fstg = SCR[:, 8192:12288].bitcast(F32)
        t0p = [SCR[:, 0:1024], SCR[:, 1024:2048]]
        gws = SCR[:, 2048:2112]
        W2xp_aps = [SCR[:, 0:6144], SCR[:, 6144:12288]]

        pt0 = es.enter_context(nc.psum_tensor("pt0", [128, 1536], F32)).ap()
        pt1 = es.enter_context(nc.psum_tensor("pt1", [128, 1536], F32)).ap()
        pb6 = es.enter_context(nc.psum_tensor("pb6", [128, 512], F32)).ap()
        pb7 = es.enter_context(nc.psum_tensor("pb7", [128, 512], F32)).ap()
        banks = [Bank(pt0[:, 0:512]), Bank(pt0[:, 512:1024]), Bank(pt0[:, 1024:1536]),
                 Bank(pt1[:, 0:512]), Bank(pt1[:, 512:1024]), Bank(pt1[:, 1024:1536]),
                 Bank(pb6), Bank(pb7)]
        trios = [pt0, pt1]

        def mm(out_ap, lhsT, rhs, start, stop, inc=False):
            return ctx.op("tensor",
                          lambda E, o=out_ap, l=lhsT, r=rhs, st=start, sp=stop:
                          E.matmul(o, l, r, start=st, stop=sp),
                          pe if inc else None)

        def ts1(eng, out, in0, s1, op0, sem=None):
            return ctx.op(eng, lambda E, o=out, i=in0, a=s1, p=op0:
                          E.tensor_single_scalar(o, i, a, p), sem)

        def ts2(eng, out, in0, s1, s2, op0, op1, sem=None):
            return ctx.op(eng, lambda E, o=out, i=in0, a=s1, b=s2, p=op0, q=op1:
                          E.tensor_scalar(o, i, a, b, p, q), sem)

        def tt(eng, out, in0, in1, op, sem=None):
            return ctx.op(eng, lambda E, o=out, a=in0, b=in1, p=op:
                          E.tensor_tensor(o, a, b, p), sem)

        def stt(eng, out, in0, scal, in1, op0, op1, sem=None):
            return ctx.op(eng, lambda E, o=out, a=in0, s=scal, b=in1, p=op0, q=op1:
                          E.scalar_tensor_tensor(o, a, s, b, p, q), sem)

        def act(out, in_, func, bias=0.0, scale=1.0, sem=None):
            return ctx.op("scalar", lambda E, o=out, i=in_, f=func, b=bias, s=scale:
                          E.activation(o, i, f, bias=b, scale=s), sem)

        def vcopy(out, in_, sem=None):
            return ctx.op("vector", lambda E, o=out, i=in_: E.tensor_copy(o, i), sem)

        def vred(out, in_, op, sem=None):
            return ctx.op("vector", lambda E, o=out, i=in_, p=op:
                          E.tensor_reduce(o, i, AX.X, p), sem)

        def tp_split(pairs, split=False):
            for i, (dst, srcp) in enumerate(pairs):
                if split and i % 2 == 1:
                    ctx.dma("scalar", dst, srcp, dqt2, transpose=True)
                else:
                    ctx.dma("sync", dst, srcp, dqt, transpose=True)
            return [(dqt, dqt.n), (dqt2, dqt2.n)]

        for dst, src in ((g1c, g1d), (b1c, b1d), (g2c, g2d), (b2c, b2d)):
            ctx.dma("sync", dst[:, :], src.rearrange("(c p) -> p c", p=128), dqx, slow=True)
        ctx.wait("vector", dqx, dqx.n)
        vcopy(b1cb[:, :], b1c[:, :])
        vcopy(b2cb[:, :], b2c[:, :])
        ctx.op("vector", lambda E: E.memset(ones1[:, :], 1.0))
        ctx.op("vector", lambda E: E.memset(onesr[:, :], 1.0))
        ctx.op("vector", lambda E: E.memset(kvsb[:, :], 0.0))
        ctx.op("vector", lambda E: E.memset(kvs[:, :], 0.0))
        ctx.op("vector", lambda E: E.memset(zmat[:, :], 0.0), ve)

        ctx.dma("gpsimd", eyeb[:, :], eyed[:, :], dqw)
        ctx.dma("gpsimd", onehb[:, :], onehd[:, :], dqw)
        stgb = [Buf(stg[0]), Buf(stg[1])]
        vb_banks_q = [banks[0], banks[1]]
        vb_banks_kv = [banks[2], banks[3], banks[4], banks[5]]

        def prep_fold(wsrc, ncols, dst_ap, gcol, vb_banks, bvec):
            for c in range(CH):
                s = stgb[c % 2]
                s.wait_free(ctx, "gpsimd")
                ctx.dma("gpsimd", s.ap[:, 0:ncols], wsrc[c * 128:(c + 1) * 128, :], dqw)
                ctx.wait("vector", dqw, dqw.n)
                ctx.wait("vector", pe)
                ts1("vector", dst_ap[:, c * ncols:(c + 1) * ncols], s.ap[:, 0:ncols],
                    gcol[:, c:c + 1], ALU.mult)
                vcopy(plainb[:, 0:ncols], s.ap[:, 0:ncols], ve)
                s.free = (ve, ve.n)
                ctx.wait("tensor", ve, ve.n)
                for g in range(ncols // 512):
                    b = vb_banks[g]
                    if c == 0:
                        b.acquire(ctx)
                    mm(b.ps[0:1, :], bvec[:, c:c + 1], plainb[:, g * 512:(g + 1) * 512],
                       start=(c == 0), stop=(c == CH - 1), inc=(c == CH - 1))

        prep_fold(wqd, D, wqS, g1c, vb_banks_q, b1cb)
        prep_fold(wkvd, 2 * D, WA, g1c, vb_banks_kv, b1cb)

        ctx.wait("vector", pe)
        for i, b in enumerate(vb_banks_q + vb_banks_kv):
            if i > 0:
                ctx.wait("vector", dqt)
            vcopy(tmpA[0:1, 0:512], b.ps[0:1, :], ve)
            b.release(ve, ve.n)
            ctx.wait("sync", ve, ve.n)
            ctx.dma("sync", vbd[0:1, i * 512:(i + 1) * 512], tmpA[0:1, 0:512], dqt)
        vb_evac = ve.n
        ctx.wait("sync", dqt, dqt.n)
        ctx.dma("sync", vbqc[:, :],
                vbd[0:1, 0:1024].rearrange("a (c p) -> (a p) c", p=128), dqx, slow=True)
        ctx.wait("gpsimd", dqt, dqt.n)
        ctx.dma("gpsimd", vbkvr[:, :], vbd[0:1, 1024:3072], dqw)
        prep_loads = dqx.n
        prep_loads_w = dqw.n
        PHASE_MARKS["prep"] = len(ctx.plan["tensor"])
        prep_pe = pe.n
        prep_ve = ve.n

        NTC = 2
        ABLK = 16
        x0b = [Buf(x0p[0]), Buf(x0p[1]),
               Buf(fstg2[:, 0:1024]), Buf(fstg2[:, 1024:2048])]
        astb = [Buf(astp[0]), Buf(astp[1])]
        afsb = [Buf(afsp[0]), Buf(afsp[1])]
        for bb in x0b + astb + afsb:
            bb.free = (pe, prep_pe)
        kv_banks = [banks[3], banks[4], banks[5]]
        pj_banks = [banks[0], banks[1], banks[2]]
        first_scr = [True]

        for blk in range(ABLK):
            own = blk < 8
            mvc = colA if blk % 2 == 0 else colB
            for tc in range(NTC):
                jj = blk * NTC + tc
                xb = x0b[jj % 4]
                xb.wait_free(ctx, "sync")
                if first_scr[0]:
                    ctx.wait("sync", ve, prep_ve)
                ctx.dma("sync", xb.ap[:, :], xd[jj * 128:(jj + 1) * 128, :], dqx)
                xb.ready = (dqx, dqx.n)
                xb.wait_ready(ctx, "vector")
                if blk >= 2:
                    ctx.wait("vector", se)
                bnc = 4 + tc * 12
                ctx.op("vector", lambda E, x=xb.ap, o=mvc, b=bnc:
                       E.bn_stats(o[:, b:b + 6], x[:, 0:512]))
                ctx.op("vector", lambda E, x=xb.ap, o=mvc, b=bnc:
                       E.bn_stats(o[:, b + 6:b + 12], x[:, 512:1024]), ve)
                ctx.wait("vector", ve)
                ctx.op("vector", lambda E, o=mvc, tc=tc, b=bnc:
                       E.bn_aggr(o[:, tc * 2:tc * 2 + 2], o[:, b:b + 12]), ve)
                ctx.wait("vector", ve)
            m_ = mvc[:, 0:2 * NTC:2]
            v_ = mvc[:, 1:2 * NTC:2]
            t4_ = mvc[:, 28:28 + NTC]
            t1_ = mvc[:, 28 + NTC:28 + 2 * NTC]
            s01_ = mvc[:, 32:32 + NTC]
            rs1_ = mvc[:, 32 + NTC:32 + 2 * NTC]
            nms_ = mvc[:, 36:36 + NTC]
            nm1_ = mvc[:, 38:38 + NTC]
            ts2("vector", t4_, v_, 1.0 + EPS, EPS * EPS, ALU.mult, ALU.add)
            ts1("vector", t1_, v_, EPS, ALU.add, sem=ve)
            ctx.wait("scalar", ve)
            act(mvc[:, 28:28 + 2 * NTC], mvc[:, 28:28 + 2 * NTC], AF.Ln, sem=se)
            ctx.wait("scalar", se)
            act(mvc[:, 32:32 + 2 * NTC], mvc[:, 28:28 + 2 * NTC], AF.Exp,
                scale=-0.5, sem=se)
            ctx.wait("vector", se)
            stt("vector", nms_, m_, -1.0, s01_, ALU.mult, ALU.mult, sem=ve)
            if own:
                stt("vector", nm1_, m_, -1.0, rs1_, ALU.mult, ALU.mult, sem=ve)
            cols_ve = ve.n

            for tc in range(NTC):
                jj = blk * NTC + tc
                xb = x0b[jj % 4]
                ab = astb[jj % 2]
                ab.wait_free(ctx, "scalar")
                ctx.wait("scalar", ve, cols_ve)
                act(ab.ap[:, :], xb.ap[:, :], AF.Identity,
                    bias=nms_[:, tc:tc + 1], scale=s01_[:, tc:tc + 1], sem=se)
                ab.ready = (se, se.n)
                if own:
                    act(RES[:, jj * D:(jj + 1) * D], xb.ap[:, :], AF.Identity,
                        bias=nm1_[:, tc:tc + 1], scale=rs1_[:, tc:tc + 1], sem=se)
                xb.free = (se, se.n)
                if own:
                    def fmdst(c, jj=jj):
                        return FM[:, c * S_OWN + jj * 128:c * S_OWN + (jj + 1) * 128]
                    dst = fmdst
                else:
                    fs = afsb[jj % 2]
                    fs.wait_free(ctx, "sync")

                    def fmdst(c, fs=fs):
                        return fs.ap[:, c * 128:(c + 1) * 128]
                    dst = fmdst
                tb = banks[6 + jj % 2]
                tb.acquire(ctx)
                ab.wait_ready(ctx, "tensor")
                tb_bf = tb.ps.bitcast(BF16)
                for c in range(CH):
                    ctx.op("tensor",
                           lambda E, o=tb_bf[:, c * 128:(c + 1) * 128],
                           i=ab.ap[:, c * 128:(c + 1) * 128]:
                           E.transpose(o, i, eyeb[:, :]), pe if c == CH - 1 else None)
                tpv = pe.n
                ab.free = (pe, tpv)
                ctx.wait("vector", pe, tpv)
                if own:
                    dview = FM.rearrange("p (c t) -> p c t", c=CH)[:, :, jj * 128:(jj + 1) * 128]
                else:
                    dview = afsb[jj % 2].ap.rearrange("p (c t) -> p c t", c=CH)
                vcopy(dview, tb_bf[:, 0:1024].rearrange("p (c t) -> p c t", c=CH), ve)
                tb.release(ve, ve.n)
                tdep = (ve, ve.n)
                if not own:
                    afsb[jj % 2].ready = tdep

                ctx.wait("tensor", tdep[0], tdep[1])
                if first_scr[0]:
                    ctx.wait("tensor", dqx, prep_loads)
                    first_scr[0] = False
                for grp in range(4):
                    b = pj_banks[(tc * 4 + grp) % 3]
                    b.acquire(ctx)
                    if jj == 0 and grp == 0:
                        ctx.wait("tensor", dqw, prep_loads_w)
                    for c in range(CH):
                        mm(b.ps[:, :], dst(c),
                           WA[:, c * 2048 + grp * 512:c * 2048 + (grp + 1) * 512],
                           start=(c == 0), stop=False)
                    mm(b.ps[:, :], onesr[:, :],
                       vbkvr[0:1, grp * 512:(grp + 1) * 512],
                       start=False, stop=True, inc=True)
                    mmv = pe.n
                    ctx.wait("vector", pe, mmv)
                    if grp < 2:
                        ts1("vector", tmpA[:, :], b.ps[:, :], 0.0, ALU.min, sem=ve)
                        ctx.wait("scalar", ve)
                        ctx.wait("scalar", pe, mmv)
                        act(tmpB[:, :], b.ps[:, :], AF.Relu, sem=se)
                        act(tmpA[:, :], tmpA[:, :], AF.Exp, sem=se)
                        ctx.wait("vector", se)
                        tt("vector",
                           phik[:, tc * 1024 + grp * 512:tc * 1024 + (grp + 1) * 512],
                           tmpA[:, :], tmpB[:, :], ALU.add, sem=ve)
                    else:
                        g2_ = grp - 2
                        vcopy(vtm[:, tc * 1024 + g2_ * 512:tc * 1024 + (g2_ + 1) * 512],
                              b.ps[:, :], ve)
                    b.release(ve, ve.n)
            phiv_done = ve.n

            ctx.wait("tensor", ve, phiv_done)
            for pair in range(8):
                    b = kv_banks[pair % 3]
                    b.acquire(ctx)
                    for tc in range(NTC):
                        mm(b.ps[:, 0:128],
                           phik[:, tc * 1024 + pair * 128:tc * 1024 + (pair + 1) * 128],
                           vtm[:, tc * 1024 + pair * 128:tc * 1024 + (pair + 1) * 128],
                           start=(tc == 0), stop=False)
                        mm(b.ps[:, 128:129],
                           phik[:, tc * 1024 + pair * 128:tc * 1024 + (pair + 1) * 128],
                           ones1[:, :], start=False, stop=(tc == NTC - 1),
                           inc=(tc == NTC - 1))
                    ctx.wait("vector", pe, pe.n)
                    tt("vector", kvsb[:, pair * 132:pair * 132 + 129],
                       kvsb[:, pair * 132:pair * 132 + 129], b.ps[:, 0:129],
                       ALU.add, sem=ve)
                    b.release(ve, ve.n)
        PHASE_MARKS["A"] = len(ctx.plan["tensor"])
        phaseA_pe = pe.n
        phaseA_ve = ve.n
        phaseA_se = se.n
        phaseA_tp = dqt.n

        ctx.wait("vector", ve, phaseA_ve)
        for pair in range(8):
            vcopy(kvs[0:64, pair * 128:pair * 128 + 64],
                  kvsb[0:64, pair * 132:pair * 132 + 64])
            vcopy(kvs[64:128, pair * 128 + 64:pair * 128 + 128],
                  kvsb[64:128, pair * 132 + 64:pair * 132 + 128])
            vcopy(zmat[0:64, pair * 16 + 2 * pair:pair * 16 + 2 * pair + 1],
                  kvsb[0:64, pair * 132 + 128:pair * 132 + 129])
            vcopy(zmat[64:128, pair * 16 + 2 * pair + 1:pair * 16 + 2 * pair + 2],
                  kvsb[64:128, pair * 132 + 128:pair * 132 + 129],
                  ve if pair == 7 else None)
        kv_final = ve.n

        if upto == "A":
            _finish(ctx, locals())
            _replay(ctx, nc)
            return nc

        ctx.wait("gpsimd", ve, phaseA_ve)
        ctx.wait("gpsimd", se, phaseA_se)
        ctx.dma("gpsimd", woS.rearrange("p (c n) -> p c n", c=CH),
                wod.rearrange("(c p) n -> p c n", p=128), dqw)
        wo_ld = dqw.n

        recb = Buf(recbc)
        hsb = Buf(hst)
        recb.hs_se = 0
        hsb.hs_se = 0
        pend_hs = [None]
        c_first = [True]
        c_state = {}

        def stage_q(cb):
            tok0 = cb * 512
            for m in range(CH):
                b = banks[m % 3]
                b.acquire(ctx)
                for c in range(CH):
                    mm(b.ps[:, :], wqS[:, c * 1024 + m * 128:c * 1024 + (m + 1) * 128],
                       FM[:, c * S_OWN + tok0:c * S_OWN + tok0 + 512],
                       start=(c == 0), stop=(c == CH - 1), inc=(c == CH - 1))
                mmv = pe.n
                ctx.wait("vector", pe, mmv)
                if c_first[0]:
                    ctx.wait("vector", pe, phaseA_pe)
                    ctx.wait("scalar", pe, phaseA_pe)
                    ctx.wait("vector", dqt, phaseA_tp)
                    c_first[0] = False
                ts2("vector", ctmpA[:, :], b.ps[:, :], vbqc[:, m:m + 1], 0.0,
                    ALU.add, ALU.min, sem=ve)
                ctx.wait("scalar", ve)
                ctx.wait("scalar", pe, mmv)
                act(ctmpB[:, :], b.ps[:, :], AF.Relu, bias=vbqc[:, m:m + 1], sem=se)
                act(ctmpA[:, :], ctmpA[:, :], AF.Exp, sem=se)
                ctx.wait("vector", se)
                tt("vector", phiq[:, m * 512:(m + 1) * 512], ctmpA[:, :], ctmpB[:, :],
                   ALU.add, sem=ve)
                b.release(ve, ve.n)
                bden = banks[6]
                if m == 0:
                    bden.acquire(ctx)
                    ctx.wait("tensor", ve, kv_final)
                ctx.wait("tensor", ve, ve.n)
                mm(bden.ps[0:16, :], zmat[:, m * 16:(m + 1) * 16],
                   phiq[:, m * 512:(m + 1) * 512],
                   start=(m == 0), stop=(m == CH - 1), inc=(m == CH - 1))
            phiq_done = ve.n
            c_state[cb] = phiq_done

        def stage_attn(cb):
            phiq_done = c_state[cb]
            bden = banks[6]

            denv = pe.n
            num_banks = [banks[7], banks[0], banks[1], banks[2], banks[3],
                         banks[4], banks[5], banks[7]]
            for pair in range(7):
                bnum = num_banks[pair]
                bnum.acquire(ctx)
                mm(bnum.ps[:, :], kvs[:, pair * 128:(pair + 1) * 128],
                   phiq[:, pair * 512:(pair + 1) * 512], start=True, stop=True,
                   inc=(pair == 6))
            numv = pe.n
            ctx.wait("vector", pe, denv)
            ts1("vector", recsb[:, :], bden.ps[0:16, :], 1e-6, ALU.add, sem=ve)
            ctx.wait("vector", ve)
            recb.wait_free(ctx, "vector")
            ctx.op("vector", lambda E: E.reciprocal(recbc[0:16, :], recsb[:, :]), ve)
            ctx.wait("vector", ve)
            vcopy(rechi[:, :], recbc[0:16, :])
            tt("vector", reclo[:, :], recbc[0:16, :], rechi[:, :], ALU.subtract, sem=ve)
            recv = ve.n
            bden.release(ve, recv)
            ctx.wait("tensor", ve, recv)
            for pair in range(8):
                brc = bden
                brc.acquire(ctx)
                mm(brc.ps[:, :], onehb[:, pair * 128:(pair + 1) * 128], rechi[:, :],
                   start=True, stop=False)
                mm(brc.ps[:, :], onehb[:, pair * 128:(pair + 1) * 128], reclo[:, :],
                   start=False, stop=True, inc=True)
                rbv = pe.n
                bnum = num_banks[pair]
                if pair == 7:
                    bnum.acquire(ctx)
                    mm(bnum.ps[:, :], kvs[:, pair * 128:(pair + 1) * 128],
                       phiq[:, pair * 512:(pair + 1) * 512], start=True, stop=True,
                       inc=True)
                    ctx.wait("vector", pe, pe.n)
                else:
                    ctx.wait("vector", pe, rbv)
                recb.wait_free(ctx, "vector")
                vcopy(recbc[:, :], brc.ps[:, :], ve)
                brc.release(ve, ve.n)
                tt("vector", attnf[:, pair * 512:(pair + 1) * 512],
                   bnum.ps[:, :], recbc[:, :], ALU.mult, sem=ve)
                bnum.release(ve, ve.n)
                recb.free = (ve, ve.n)
            attn_done = ve.n
            c_state[cb] = attn_done

        def stage_wo(cb):
            attn_done = c_state[cb]
            ctx.wait("tensor", dqw, wo_ld)
            ctx.wait("tensor", ve, attn_done)
            for tc in range(4):
                jj = cb * 4 + tc
                if pend_hs[0] is not None:
                    pj, pbuf = pend_hs[0]
                    tbk = banks[2 + pj % 2]
                    tbk.acquire(ctx)
                    ctx.wait("tensor", se, pbuf.hs_se)
                    tb_bf = tbk.ps.bitcast(BF16)
                    pap = pbuf.ap.bitcast(BF16) if pbuf is recb else pbuf.ap
                    for c in range(CH):
                        ctx.op("tensor",
                               lambda E, o=tb_bf[:, c * 128:(c + 1) * 128],
                               i=pap[:, c * 128:(c + 1) * 128]:
                               E.transpose(o, i, eyeb[:, :]), pe if c == CH - 1 else None)
                    tpv = pe.n
                    pbuf.free = (pe, tpv)
                    ctx.wait("vector", pe, tpv)
                    for c in range(CH):
                        ts1("vector", FM[:, c * S_OWN + pj * 128:c * S_OWN + (pj + 1) * 128],
                            tb_bf[:, c * 128:(c + 1) * 128], g2c[:, c:c + 1], ALU.mult,
                            sem=ve if c == CH - 1 else None)
                    tbk.release(ve, ve.n)
                    pend_hs[0] = None
                for grp in range(2):
                    b = pj_banks[grp]
                    b.acquire(ctx)
                    for c in range(CH):
                        mm(b.ps[:, :],
                           attnf[:, c * 512 + tc * 128:c * 512 + (tc + 1) * 128],
                           woS[:, c * 1024 + grp * 512:c * 1024 + (grp + 1) * 512],
                           start=(c == 0), stop=(c == CH - 1), inc=(c == CH - 1))
                    mmv = pe.n
                    ctx.wait("vector", pe, mmv)
                    rsl = RES[:, jj * D + grp * 512:jj * D + (grp + 1) * 512]
                    tt("vector", rsl, b.ps[:, :], rsl, ALU.add, sem=ve)
                    b.release(ve, ve.n)
                mvc = colA if jj % 2 == 0 else colB
                rtile = RES[:, jj * D:(jj + 1) * D]
                ctx.wait("vector", ve)
                ctx.wait("vector", se)
                ctx.op("vector", lambda E, o=mvc, r=rtile: E.bn_stats(o[:, 4:10], r[:, 0:512]))
                ctx.op("vector", lambda E, o=mvc, r=rtile: E.bn_stats(o[:, 10:16], r[:, 512:1024]), ve)
                ctx.wait("vector", ve)
                ctx.op("vector", lambda E, o=mvc: E.bn_aggr(o[:, 0:2], o[:, 4:16]), ve)
                ctx.wait("vector", ve)
                ts1("vector", mvc[:, 2:3], mvc[:, 1:2], EPS, ALU.add, sem=ve)
                ctx.wait("scalar", ve)
                act(mvc[:, 2:3], mvc[:, 2:3], AF.Ln, sem=se)
                ctx.wait("scalar", se)
                act(mvc[:, 3:4], mvc[:, 2:3], AF.Exp, scale=-0.5, sem=se)
                ctx.wait("vector", se)
                stt("vector", mvc[:, 2:3], mvc[:, 0:1], -1.0, mvc[:, 3:4],
                    ALU.mult, ALU.mult, sem=ve)
                hbuf = hsb if jj % 2 == 0 else recb
                hbuf.wait_free(ctx, "scalar")
                ctx.wait("scalar", ve)
                hap = hbuf.ap.bitcast(BF16) if hbuf is recb else hbuf.ap
                act(hap[:, :], rtile, AF.Identity, bias=mvc[:, 2:3],
                    scale=mvc[:, 3:4], sem=se)
                hbuf.hs_se = se.n
                pend_hs[0] = (jj, hbuf)
            pj, pbuf = pend_hs[0]
            tbk = banks[2 + pj % 2]
            tbk.acquire(ctx)
            ctx.wait("tensor", se, pbuf.hs_se)
            tb_bf = tbk.ps.bitcast(BF16)
            pap = pbuf.ap.bitcast(BF16) if pbuf is recb else pbuf.ap
            for c in range(CH):
                ctx.op("tensor",
                       lambda E, o=tb_bf[:, c * 128:(c + 1) * 128],
                       i=pap[:, c * 128:(c + 1) * 128]:
                       E.transpose(o, i, eyeb[:, :]), pe if c == CH - 1 else None)
            tpv = pe.n
            pbuf.free = (pe, tpv)
            ctx.wait("vector", pe, tpv)
            for c in range(CH):
                ts1("vector", FM[:, c * S_OWN + pj * 128:c * S_OWN + (pj + 1) * 128],
                    tb_bf[:, c * 128:(c + 1) * 128], g2c[:, c:c + 1], ALU.mult,
                    sem=ve if c == CH - 1 else None)
            tbk.release(ve, ve.n)
            pend_hs[0] = None

        stage_q(0)
        stage_attn(0)
        for cb in range(4):
            if cb < 3:
                stage_q(cb + 1)
            stage_wo(cb)
            if cb < 3:
                stage_attn(cb + 1)

        PHASE_MARKS["C"] = len(ctx.plan["tensor"])
        hs_tp = ve.n
        hs_tp2 = 0
        phaseC_pe = pe.n
        phaseC_ve = ve.n
        phaseC_se = se.n

        if upto == "C":
            _finish(ctx, locals())
            _replay(ctx, nc)
            return nc

        fstgb = Buf(fstg)
        fstgb.free = (ve, phaseC_ve)
        vb1_banks = [banks[4], banks[5], banks[6], banks[7]]
        f_first = [True]

        def load_w1G(G):
            for c in range(CH):
                if G == 0 and c == 0:
                    ctx.wait("gpsimd", pe, phaseA_pe)
                ctx.wait("gpsimd", pe, pe.n)
                ctx.dma("gpsimd", WA[:, c * 2048:(c + 1) * 2048],
                        w1d[c * 128:(c + 1) * 128, G * 2048:(G + 1) * 2048], dqw)
                if c % 2 == 1:
                    qc = c // 2
                    if qc == 0 and G == 0:
                        ctx.wait("gpsimd", pe, phaseC_pe)
                    ctx.dma("gpsimd",
                            w2G[:, qc * 4096:(qc + 1) * 4096].rearrange("p (c n) -> p c n", c=4),
                            w2d[G * 2048 + qc * 512:G * 2048 + (qc + 1) * 512, :]
                            .rearrange("(c p) n -> p c n", p=128), dqw)
            w1v = dqw.n
            ctx.wait("tensor", dqw, w1v)
            for c in range(CH):
                for g in range(4):
                    b = vb1_banks[g]
                    if c == 0:
                        b.acquire(ctx)
                    mm(b.ps[0:1, :], b2cb[:, c:c + 1],
                       WA[:, c * 2048 + g * 512:c * 2048 + (g + 1) * 512],
                       start=(c == 0), stop=(c == CH - 1), inc=(c == CH - 1))
            ctx.wait("vector", pe)
            for g in range(4):
                if g > 0:
                    ctx.wait("vector", dqt)
                vcopy(recsb[0:1, 0:512], vb1_banks[g].ps[0:1, :], ve)
                vb1_banks[g].release(ve, ve.n)
                ctx.wait("sync", ve, ve.n)
                ctx.dma("sync",
                        vbd[0:1, 3072 + G * 2048 + g * 512:3072 + G * 2048 + (g + 1) * 512],
                        recsb[0:1, 0:512], dqt)
            ctx.wait("sync", dqt, dqt.n)
            ctx.dma("sync", vbw1c[:, G * 16:(G + 1) * 16],
                    vbd[0:1, 3072 + G * 2048:3072 + (G + 1) * 2048]
                    .rearrange("a (c p) -> (a p) c", p=128), dqx, slow=True)
            ctx.wait("vector", dqx, dqx.n)
            ts1("vector", vbw1s[:, G * 16:(G + 1) * 16], vbw1c[:, G * 16:(G + 1) * 16],
                10.0, ALU.mult, sem=ve)
            return ve.n

        ftmp = [fstg[:, 0:512], fstg[:, 512:1024]]

        x0db = [Buf(fstg2[:, 0:1024]), Buf(fstg2[:, 1024:2048])]
        for bb in x0db:
            bb.free = [(se, se.n), (ve, ve.n)]
        gw_ld = [0]
        d0_pend = []

        def d0_tile(jj):
            xb = x0db[jj % 2]
            xb.wait_free(ctx, "sync")
            ctx.wait("sync", ve, ve.n)
            ctx.dma("sync", xb.ap[:, :], xd[jj * 128:(jj + 1) * 128, :], dqx)
            xv = dqx.n
            ctx.wait("vector", dqx, xv)
            rtile = RES[:, jj * D:(jj + 1) * D]
            tt("vector", rtile, rtile, xb.ap[:, :], ALU.add, sem=ve)
            mvc = colA if jj % 2 == 0 else colB
            ctx.wait("vector", se)
            ctx.op("vector", lambda E, o=mvc, r=rtile: E.bn_stats(o[:, 4:10], r[:, 0:512]))
            ctx.op("vector", lambda E, o=mvc, r=rtile: E.bn_stats(o[:, 10:16], r[:, 512:1024]), ve)
            ctx.wait("vector", ve)
            ctx.op("vector", lambda E, o=mvc: E.bn_aggr(o[:, 0:2], o[:, 4:16]), ve)
            ctx.wait("vector", ve)
            ts1("vector", mvc[:, 2:3], mvc[:, 1:2], EPS, ALU.add, sem=ve)
            ctx.wait("scalar", ve)
            act(mvc[:, 2:3], mvc[:, 2:3], AF.Ln, sem=se)
            ctx.wait("scalar", se)
            act(mvc[:, 3:4], mvc[:, 2:3], AF.Exp, scale=-0.5, sem=se)
            ctx.wait("vector", se)
            stt("vector", mvc[:, 2:3], mvc[:, 0:1], -1.0, mvc[:, 3:4],
                ALU.mult, ALU.mult, sem=ve)
            t0ap = xb.ap.bitcast(BF16)[:, 0:1024]
            ctx.wait("scalar", ve)
            act(t0ap, rtile, AF.Identity, bias=mvc[:, 2:3], scale=mvc[:, 3:4], sem=se)
            d0_pend.append((jj, xb, t0ap, se.n))

        def d0_pe():
            jj, xb, t0ap, t0se = d0_pend.pop(0)
            tbk = banks[6 + jj % 2]
            tbk.acquire(ctx)
            ctx.wait("tensor", se, t0se)
            tb_bf = tbk.ps.bitcast(BF16)
            for c in range(CH):
                ctx.op("tensor",
                       lambda E, o=tb_bf[:, c * 128:(c + 1) * 128],
                       i=t0ap[:, c * 128:(c + 1) * 128]:
                       E.transpose(o, i, eyeb[:, :]), pe if c == CH - 1 else None)
            tpv = pe.n
            xb.free = (pe, tpv)
            ctx.wait("vector", pe, tpv)
            dview = FM.rearrange("p (c t) -> p c t", c=CH)[:, :, jj * 128:(jj + 1) * 128]
            vcopy(dview, tb_bf[:, 0:1024].rearrange("p (c t) -> p c t", c=CH), ve)
            t0fm_ve = ve.n
            tbk.release(ve, t0fm_ve)
            bg = banks[6 + jj % 2]
            bg.acquire(ctx)
            ctx.wait("tensor", ve, t0fm_ve)
            ctx.wait("tensor", dqw, gw_ld[0])
            for c in range(CH):
                mm(bg.ps[:, 0:8],
                   FM[:, c * S_OWN + jj * 128:c * S_OWN + (jj + 1) * 128],
                   gwsb[:, c * 8:(c + 1) * 8],
                   start=(c == 0), stop=(c == CH - 1), inc=(c == CH - 1))
            ctx.wait("vector", pe, pe.n)
            ctx.wait("vector", se)
            vcopy(scsb[:, 0:8], bg.ps[:, 0:8], ve)
            bg.release(ve, ve.n)
            ctx.wait("vector", ve)
            vred(sccol[:, 0:1], scsb[:, 0:8], ALU.max, sem=ve)
            ctx.wait("vector", ve)
            ts1("vector", sccol[:, 1:2], sccol[:, 0:1], -1.0, ALU.mult, sem=ve)
            ctx.wait("scalar", ve)
            act(scsb[:, 8:16], scsb[:, 0:8], AF.Exp, bias=sccol[:, 1:2], sem=se)
            ctx.wait("vector", se)
            vred(sccol[:, 2:3], scsb[:, 8:16], ALU.add, sem=ve)
            ctx.wait("vector", ve)
            ctx.op("vector", lambda E: E.reciprocal(sccol[:, 3:4], sccol[:, 2:3]), ve)
            ctx.wait("vector", ve)
            ts1("vector", scsb[:, 0:8], scsb[:, 8:16], sccol[:, 3:4], ALU.mult, sem=ve)
            ctx.wait("vector", ve)
            vred(sccol[:, 4:5], scsb[:, 0:8], ALU.max, sem=ve)
            ctx.wait("vector", ve)
            ts1("vector", scsb[:, 8:16], scsb[:, 0:8], sccol[:, 4:5], ALU.is_ge, sem=ve)
            ctx.wait("vector", ve)
            stt("vector", scsb[:, 8:16], scsb[:, 8:16], -1e9, scsb[:, 0:8],
                ALU.mult, ALU.add, sem=ve)
            ctx.wait("vector", ve)
            vred(sccol[:, 5:6], scsb[:, 8:16], ALU.max, sem=ve)
            ctx.wait("vector", ve)
            ts1("vector", scsb[:, 8:16], scsb[:, 0:8], sccol[:, 5:6], ALU.is_ge, sem=ve)
            ctx.wait("vector", ve)
            tt("vector", wdsb[:, jj * 8:(jj + 1) * 8], scsb[:, 8:16], scsb[:, 0:8],
               ALU.mult, sem=ve)

        for G in range(2):
            vbv = load_w1G(G)
            w2v = dqw.n
            if G == 1:
                ctx.dma("gpsimd", gwsb[:, :].rearrange("p (c n) -> p c n", c=CH),
                        gwd.rearrange("(c p) n -> p c n", p=128), dqw)
                gw_ld[0] = dqw.n
            for fb in range(4):
                tok0 = fb * 512
                w1_banks = [banks[0], banks[1], banks[4]]
                for cc in range(16):
                    b = w1_banks[cc % 3]
                    b.acquire(ctx)
                    ctx.wait("tensor", ve, vbv)
                    if G == 0 and fb == 0 and cc == 0:
                        ctx.wait("tensor", ve, hs_tp)
                    for c in range(CH):
                        mm(b.ps[:, :],
                           WA[:, c * 2048 + cc * 128:c * 2048 + (cc + 1) * 128],
                           FM[:, c * S_OWN + tok0:c * S_OWN + tok0 + 512],
                           start=(c == 0), stop=(c == CH - 1), inc=(c == CH - 1))
                    mmv = pe.n
                    tmp = ftmp[cc % 2]
                    ctx.wait("scalar", pe, mmv)
                    ctx.wait("scalar", ve)
                    act(tmp, b.ps[:, :], AF.Identity, scale=10.0,
                        bias=vbw1s[:, G * 16 + cc:G * 16 + cc + 1], sem=se)
                    b.release(se, se.n)
                    ctx.wait("vector", se)
                    ts2("vector", tmp, tmp, RC, -RC, ALU.add, ALU.add, sem=ve)
                    ctx.wait("vector", ve)
                    ts2("vector", tmp, tmp, 127.0, -128.0, ALU.min, ALU.max, sem=ve)
                    ctx.wait("scalar", ve)
                    act(g1G[:, cc * 512:(cc + 1) * 512], tmp,
                        AF.Identity if DBG_FFN_IDENT[0] else AF.Gelu_apprx_tanh,
                        scale=0.1, sem=se)
                g1_done = se.n
                ctx.wait("tensor", se, g1_done)
                ctx.wait("tensor", dqw, w2v)
                for tc in range(4):
                    jj = fb * 4 + tc
                    for grp in range(2):
                        b = [banks[2], banks[3], banks[5]][(tc * 2 + grp) % 3]
                        b.acquire(ctx)
                        for cc in range(16):
                            mm(b.ps[:, :],
                               g1G[:, cc * 512 + tc * 128:cc * 512 + (tc + 1) * 128],
                               w2G[:, cc * 1024 + grp * 512:cc * 1024 + (grp + 1) * 512],
                               start=(cc == 0), stop=(cc == 15), inc=(cc == 15))
                        ctx.wait("vector", pe, pe.n)
                        rsl = RES[:, jj * D + grp * 512:jj * D + (grp + 1) * 512]
                        tt("vector", rsl, b.ps[:, :], rsl, ALU.add, sem=ve)
                        b.release(ve, ve.n)
                    if len(d0_pend) >= 2:
                        d0_pe()
                    if G == 1:
                        d0_tile(jj)
        ffn_pe = pe.n
        ffn_ve = ve.n

        PHASE_MARKS["F"] = len(ctx.plan["tensor"])
        while d0_pend:
            d0_pe()
        wd_ve = ve.n
        x1_ve = ve.n

        if upto == "F":
            _finish(ctx, locals())
            _replay(ctx, nc)
            return nc

        u_tms = [WB[:, 12288:13056], WB[:, 13824:14592]]
        u_fms = [WB[:, 13056:13824], WB[:, 14592:15360]]
        silu_bf = WB[:, 15360:16128]
        W1b = [Buf(W1xp_aps[0]), Buf(W1xp_aps[1])]
        W2b = [Buf(W2xp_aps[0]), Buf(W2xp_aps[1])]
        W1b[0].free = (pe, ffn_pe)
        W1b[1].free = [(ve, x1_ve), (pe, ffn_pe)]
        W2b[0].free = (ve, wd_ve)
        W2b[1].free = (ve, wd_ve)
        u_tmbs = [Buf(u_tms[0]), Buf(u_tms[1])]
        u_fmbs = [Buf(u_fms[0]), Buf(u_fms[1])]
        for bb in u_tmbs + u_fmbs:
            bb.free = [(ve, x1_ve), (pe, ffn_pe)]
        h_banks = [[banks[0], banks[1], banks[2]], [banks[3], banks[4], banks[5]]]
        eo_banks = [banks[6], banks[7]]

        for e0, wbx, dram, nch in ((0, W1b[0], W1xd, CH), (0, W2b[0], W2xd, 6)):
            wbx.wait_free(ctx, "gpsimd")
            ctx.dma("gpsimd", wbx.ap.rearrange("p (c n) -> p c n", c=nch),
                    dram[0].rearrange("(c p) n -> p c n", p=128), dqw)
            wbx.ready = (dqw, dqw.n)

        def prefetch_chunk(e1, which, c):
            if e1 >= NEXP:
                return
            wbx = (W1b if which == 1 else W2b)[e1 % 2]
            dram = W1xd if which == 1 else W2xd
            ncols = 1536 if which == 1 else 1024
            if c == 0:
                wbx.wait_free(ctx, "gpsimd")
            ctx.wait("gpsimd", pe, pe.n)
            ctx.dma("gpsimd", wbx.ap[:, c * ncols:(c + 1) * ncols],
                    dram[e1, c * 128:(c + 1) * 128, :], dqw)
            nmax = CH if which == 1 else 6
            if c == nmax - 1:
                wbx.ready = (dqw, dqw.n)

        for e in range(NEXP):
            wb1 = W1b[e % 2]
            wb2 = W2b[e % 2]
            wb1.wait_ready(ctx, "tensor")
            ctx.wait("tensor", ve, wd_ve)
            last_h = [None]
            last_eo = [None]
            pend = []

            def emit_eo(tca, ui):
                fb_ = u_fmbs[ui]
                fb_.wait_ready(ctx, "tensor")
                wb2.wait_ready(ctx, "tensor")
                for grp in range(2):
                    b = eo_banks[grp]
                    b.acquire(ctx)
                    for cc in range(6):
                        mm(b.ps[:, :], u_fms[ui][:, cc * 128:(cc + 1) * 128],
                           wb2.ap[:, cc * 1024 + grp * 512:cc * 1024 + (grp + 1) * 512],
                           start=(cc == 0), stop=(cc == 5), inc=(cc == 5))
                    mmv2 = pe.n
                    last_eo[0] = mmv2
                    ctx.wait("vector", pe, mmv2)
                    rsl = RES[:, tca * D + grp * 512:tca * D + (grp + 1) * 512]
                    tt("vector", rsl, b.ps[:, :], rsl, ALU.add, sem=ve)
                    b.release(ve, ve.n)
                fb_.free = (pe, last_eo[0])

            for tca in range(16):
                upi = tca % 2
                hb = h_banks[tca % 2]
                trio = trios[tca % 2]
                gate_pe = None
                for ng in (1, 2, 0):
                    b = hb[ng]
                    b.acquire(ctx)
                    for c in range(CH):
                        mm(b.ps[:, :],
                           FM[:, c * S_OWN + tca * 128:c * S_OWN + (tca + 1) * 128],
                           wb1.ap[:, c * 1536 + ng * 512:c * 1536 + (ng + 1) * 512],
                           start=(c == 0), stop=(c == CH - 1), inc=(c == CH - 1))
                    if ng == 2:
                        gate_pe = pe.n
                mmv = pe.n
                last_h[0] = mmv
                ctx.wait("scalar", pe, gate_pe)
                ctx.wait("scalar", ve)
                act(silu_bf[:, 0:768], trio[:, 768:1536], AF.Silu, sem=se)
                sv = se.n
                ub = u_tmbs[upi]
                ub.wait_free(ctx, "vector")
                ctx.wait("vector", se, sv)
                ctx.wait("vector", pe, mmv)
                ut = u_tms[upi]
                if e < 8:
                    wdcol = wdsb[:, tca * 8 + e:tca * 8 + e + 1]
                    stt("vector", ut[:, 0:768], trio[:, 0:768], wdcol,
                        silu_bf[:, 0:768], ALU.mult, ALU.mult, sem=ve)
                else:
                    tt("vector", ut[:, 0:768], trio[:, 0:768],
                       silu_bf[:, 0:768], ALU.mult, sem=ve)
                uv = ve.n
                ub.ready = (ve, uv)
                hb[1].release(ve, uv)
                hb[2].release(ve, uv)
                if len(pend) >= 2:
                    tca_, ui_ = pend.pop(0)
                    emit_eo(tca_, ui_)
                fb_ = u_fmbs[upi]
                ctx.wait("tensor", ve, uv)
                tb_bf = hb[0].ps.bitcast(BF16)
                for c in range(6):
                    ctx.op("tensor",
                           lambda E, o=tb_bf[:, c * 128:(c + 1) * 128],
                           i=ut[:, c * 128:(c + 1) * 128]:
                           E.transpose(o, i, eyeb[:, :]), pe if c == 5 else None)
                tpv = pe.n
                ub.free = (pe, tpv)
                fb_.wait_free(ctx, "scalar")
                ctx.wait("scalar", pe, tpv)
                act(u_fms[upi][:, 0:768], tb_bf[:, 0:768], AF.Identity, sem=se)
                hb[0].release(se, se.n)
                fb_.ready = (se, se.n)
                pend.append((tca, upi))
                if tca % 2 == 1:
                    prefetch_chunk(e + 1, 1, tca // 2)
                elif 2 <= tca <= 12:
                    prefetch_chunk(e + 1, 2, tca // 2 - 1)
            while pend:
                tca_, ui_ = pend.pop(0)
                emit_eo(tca_, ui_)
            wb1.free = (pe, last_h[0])
            wb2.free = (pe, last_eo[0])
        _finish(ctx, locals())
        _replay(ctx, nc)
        return nc


def _finish(ctx, env):
    pe, ve, se = env["pe"], env["ve"], env["se"]
    dqo, dqt, dqx, dqw = env["dqo"], env["dqt"], env["dqx"], env["dqw"]
    RES, outd = env["RES"], env["outd"]
    for s in (pe, ve, se, dqt, dqx, dqw):
        ctx.wait("scalar", s)
    tap_aps = {
        "as_fm": env["FM"], "hsfm": env["FM"], "t0fm": env["FM"],
        "kvsb": env["kvsb"], "res": env["RES"], "phik": env["phik"],
        "vtm": env["vtm"], "phiq": env["phiq"], "attn": env["attnf"],
        "rec": env["recsb"], "wd": env["wdsb"], "vbq": env["vbqc"],
        "vbkv": env["VBKV"], "wkvs": env["WA"], "wqs": env["wqS"],
        "g1g": env["g1G"], "zmat": env["zmat"], "kvsq": env["kvs"],
    }
    nc = ctx.nc
    for t in env["taps"]:
        src = tap_aps[t]
        shp = [src.shape[0], src.shape[1]]
        dt = src.dtype
        td = nc.dram_tensor("tap_" + t, shp, dt, kind="ExternalOutput").ap()
        ctx.dma("scalar", td[:, :], src, dqo)
    for jj in range(16):
        ctx.dma("scalar", outd[jj * 128:(jj + 1) * 128, :],
                RES[:, jj * 1024:(jj + 1) * 1024], dqo)
    ctx.wait("scalar", dqo)
    ctx.wait("sync", dqo)


def _replay(ctx, nc):
    with nc.Block() as block:
        block.sync(lambda E: [f(E) for f in ctx.plan["sync"]])
        block.scalar(lambda E: [f(E) for f in ctx.plan["scalar"]])
        block.vector(lambda E: [f(E) for f in ctx.plan["vector"]])
        block.tensor(lambda E: [f(E) for f in ctx.plan["tensor"]])
        block.gpsimd(lambda E: [f(E) for f in ctx.plan["gpsimd"]])


def _host_layout(inputs):
    x = np.asarray(inputs["L0_attn_x"], np.float32)
    wq = np.ascontiguousarray(np.asarray(inputs["L0_attn_wq"], np.float32))
    wk = np.asarray(inputs["L0_attn_wk"], np.float32)
    wv = np.asarray(inputs["L0_attn_wv"], np.float32)
    wkv = np.ascontiguousarray(np.concatenate([wk, wv], axis=1))
    wo = np.ascontiguousarray(np.asarray(inputs["L0_attn_wo"], np.float32))
    w1 = np.ascontiguousarray(np.asarray(inputs["L0_attn_w1"], np.float32))
    w2 = np.ascontiguousarray(np.asarray(inputs["L0_attn_w2"], np.float32))
    gw = np.ascontiguousarray(np.asarray(inputs["L0_ffn_gate_w"], np.float32).T)
    W1 = np.asarray(inputs["L0_ffn_W1"], np.float32)
    W2 = np.asarray(inputs["L0_ffn_W2"], np.float32)
    Ws1 = np.asarray(inputs["L0_ffn_Ws1"], np.float32)
    Ws2 = np.asarray(inputs["L0_ffn_Ws2"], np.float32)
    W1t = W1.transpose(0, 2, 1)
    W2t = W2.transpose(0, 2, 1)
    Ws1t = Ws1.T
    Ws2t = Ws2.T
    W1x = np.zeros((NEXP, D, 1536), np.float32)
    W2x = np.zeros((NEXP, 768, D), np.float32)
    W1x[:8] = W1t
    W2x[:8] = W2t
    W1x[8] = np.concatenate([Ws1t[:, 0:768], Ws1t[:, 1536:2304]], axis=1)
    W1x[9] = np.concatenate([Ws1t[:, 768:1536], Ws1t[:, 2304:3072]], axis=1)
    W2x[8] = Ws2t[0:768]
    W2x[9] = Ws2t[768:1536]
    oneh = np.zeros((16, 1024), np.float32)
    for p_ in range(8):
        oneh[2 * p_, p_ * 128:p_ * 128 + 64] = 1.0
        oneh[2 * p_ + 1, p_ * 128 + 64:p_ * 128 + 128] = 1.0
    shared = dict(
        eye=np.eye(128, dtype=np.float32),
        oneh=oneh,
        wq=wq, wkv=wkv, wo=wo, w1=w1, w2=w2, gw=gw,
        W1x=np.ascontiguousarray(W1x), W2x=np.ascontiguousarray(W2x),
        g1=np.ascontiguousarray(np.asarray(inputs["L0_attn_ln1_g"], np.float32)),
        b1=np.ascontiguousarray(np.asarray(inputs["L0_attn_ln1_b"], np.float32)),
        g2=np.ascontiguousarray(np.asarray(inputs["L0_attn_ln2_g"], np.float32)),
        b2=np.ascontiguousarray(np.asarray(inputs["L0_attn_ln2_b"], np.float32)),
    )
    in_maps = []
    for c in range(8):
        b, h = c // 2, c % 2
        own = x[b, h * 2048:(h + 1) * 2048]
        oth = x[b, (1 - h) * 2048:(1 - h) * 2048 + 2048]
        m = dict(shared)
        m["x"] = np.ascontiguousarray(np.concatenate([own, oth], axis=0))
        in_maps.append(m)
    return in_maps


def kernel(**inputs):
    in_maps = _host_layout(inputs)
    nc = build_program()
    res = run_bass_kernel_spmd(nc, in_maps, core_ids=list(range(8)))
    out = np.zeros((4, 4096, 1024), np.float32)
    for c in range(8):
        b, h = c // 2, c % 2
        out[b, h * 2048:(h + 1) * 2048] = res.results[c]["out"]
    return out


# version 18
# speedup vs baseline: 1.3678x; 1.0440x over previous
import contextlib
import numpy as np

import concourse.bass as bass
import concourse.mybir as mybir
from concourse.bass_utils import run_bass_kernel_spmd

F32 = mybir.dt.float32
BF16 = mybir.dt.bfloat16
AF = mybir.ActivationFunctionType
ALU = mybir.AluOpType
AX = mybir.AxisListType

D = 1024
DBG_FFN_IDENT = [False]
PHASE_MARKS = {}
DBG_EO_INLINE = [False]
DBG_D1_TP_SYNC = [False]
DBG_D1_NOPING = [False]
DBG_D0_TP_SYNC = [False]
CH = 8
S_ALL = 4096
S_OWN = 2048
EPS = 1e-5
RC = 12582912.0
NEXP = 10

ENGS = ("sync", "scalar", "vector", "tensor", "gpsimd")


class SemT:
    __slots__ = ("h", "n")

    def __init__(self, h):
        self.h = h
        self.n = 0


class Ctx:
    def __init__(self, nc, es):
        self.nc = nc
        self.es = es
        self.plan = {e: [] for e in ENGS}
        self.lastwait = {}

    def sem(self, name):
        return SemT(self.es.enter_context(self.nc.semaphore(name)))

    def emit(self, eng, fn):
        self.plan[eng].append(fn)

    def wait(self, eng, sem, val=None):
        v = sem.n if val is None else val
        if v <= 0:
            return
        key = (eng, id(sem))
        if self.lastwait.get(key, -1) >= v:
            return
        self.lastwait[key] = v
        self.emit(eng, lambda E, s=sem.h, v=v: E.wait_ge(s, v))

    def dma(self, eng, out, in_, sem, transpose=False, slow=False):
        sem.n += 16

        def f(E, out=out, in_=in_, t=transpose, s=sem.h, sl=slow):
            E.dma_start(out=out, in_=in_, transpose=t,
                        allow_slow_non_contiguous=sl).then_inc(s, 16)

        self.emit(eng, f)
        return sem.n

    def op(self, eng, builder, sem=None):
        if sem is not None:
            sem.n += 1
        sh = sem.h if sem is not None else None

        def f(E, b=builder, sh=sh):
            i = b(E)
            if sh is not None:
                i.then_inc(sh, 1)

        self.emit(eng, f)
        return sem.n if sem is not None else None


class Bank:
    def __init__(self, ps):
        self.ps = ps
        self.free = None

    def acquire(self, ctx):
        if self.free is not None:
            ctx.wait("tensor", self.free[0], self.free[1])

    def release(self, sem, val):
        self.free = (sem, val)


class Buf:
    def __init__(self, ap):
        self.ap = ap
        self.free = None
        self.ready = None

    @staticmethod
    def _w(ctx, eng, deps):
        if deps is None:
            return
        if isinstance(deps, list):
            for s, v in deps:
                ctx.wait(eng, s, v)
        else:
            ctx.wait(eng, deps[0], deps[1])

    def wait_free(self, ctx, eng):
        self._w(ctx, eng, self.free)

    def wait_ready(self, ctx, eng):
        self._w(ctx, eng, self.ready)


def build_program(upto="E", taps=()):
    nc = bass.Bass("TRN2", target_bir_lowering=False)
    taps = list(taps)

    xd = nc.dram_tensor("x", [S_ALL, D], F32, kind="ExternalInput").ap()
    wqd = nc.dram_tensor("wq", [D, D], F32, kind="ExternalInput").ap()
    wkvd = nc.dram_tensor("wkv", [D, 2 * D], F32, kind="ExternalInput").ap()
    wod = nc.dram_tensor("wo", [D, D], F32, kind="ExternalInput").ap()
    w1d = nc.dram_tensor("w1", [D, 4096], F32, kind="ExternalInput").ap()
    w2d = nc.dram_tensor("w2", [4096, D], F32, kind="ExternalInput").ap()
    gwd = nc.dram_tensor("gw", [D, 8], F32, kind="ExternalInput").ap()
    W1xd = nc.dram_tensor("W1x", [NEXP, D, 1536], F32, kind="ExternalInput").ap()
    W2xd = nc.dram_tensor("W2x", [NEXP, 768, D], F32, kind="ExternalInput").ap()
    eyed = nc.dram_tensor("eye", [128, 128], F32, kind="ExternalInput").ap()
    onehd = nc.dram_tensor("oneh", [16, 1024], F32, kind="ExternalInput").ap()
    g1d = nc.dram_tensor("g1", [D], F32, kind="ExternalInput").ap()
    b1d = nc.dram_tensor("b1", [D], F32, kind="ExternalInput").ap()
    g2d = nc.dram_tensor("g2", [D], F32, kind="ExternalInput").ap()
    b2d = nc.dram_tensor("b2", [D], F32, kind="ExternalInput").ap()
    outd = nc.dram_tensor("out", [S_OWN, D], F32, kind="ExternalOutput").ap()
    vbd = nc.dram_tensor("vbd", [1, 8192], F32).ap()
    recd = nc.dram_tensor("recd", [16, 512], F32).ap()

    es = contextlib.ExitStack()
    with es:
        ctx = Ctx(nc, es)
        sb = lambda name, shape, dt: es.enter_context(
            nc.sbuf_tensor(name, shape, dt)).ap()

        dqx = ctx.sem("dqx")
        dqt = ctx.sem("dqt")
        dqt2 = ctx.sem("dqt2")
        dqw = ctx.sem("dqw")
        pe = ctx.sem("pe")
        ve = ctx.sem("ve")
        se = ctx.sem("se")
        ge = ctx.sem("ge")
        dqo = ctx.sem("dqo")

        RES = sb("RES", [128, 16 * D], F32)
        FM = sb("FM", [128, CH * S_OWN], BF16)
        g1c = sb("g1c", [128, 8], F32)
        b1c = sb("b1c", [128, 8], F32)
        g2c = sb("g2c", [128, 8], F32)
        b2c = sb("b2c", [128, 8], F32)
        b1cb = sb("b1cb", [128, 8], BF16)
        b2cb = sb("b2cb", [128, 8], BF16)
        vbqc = sb("vbqc", [128, 8], F32)
        vbw1c = sb("vbw1c", [128, 32], F32)
        vbw1s = sb("vbw1s", [128, 32], F32)
        ones1 = sb("ones1", [128, 1], BF16)
        eyeb = sb("eyeb", [128, 128], BF16)
        onehb = sb("onehb", [16, 1024], BF16)
        rechi = sb("rechi", [16, 512], BF16)
        reclo = sb("reclo", [16, 512], BF16)
        fstg2 = sb("fstg2", [128, 2048], F32)
        gwsb = sb("gwsb", [128, 64], BF16)
        kvsb = sb("kvsb", [128, 8 * 132], F32)
        kvs = sb("kvs", [128, 8 * 128], BF16)
        zmat = sb("zmat", [128, 8 * 16], BF16)
        wdsb = sb("wdsb", [128, 16 * 8], F32)
        colA = sb("colA", [128, 48], F32)
        colB = sb("colB", [128, 48], F32)
        recsb = sb("recsb", [16, 512], F32)
        scsb = sb("scsb", [128, 16], F32)
        sccol = sb("sccol", [128, 8], F32)

        WA = sb("WA", [128, 16384], BF16)
        WB = sb("WB", [128, 16384], BF16)
        SCR = sb("SCR", [128, 12288], BF16)

        wqS = WB[:, 0:8192]
        vbkvr = WB[0:1, 8192:10240]
        onesr = WB[0:1, 14336:14464]
        VBKV = WB[:, 8192:12288].bitcast(F32)
        tmpA = WB[:, 12288:13312].bitcast(F32)
        tmpB = WB[:, 13312:14336].bitcast(F32)
        woS = WB[:, 8192:16384]
        w2G = WB
        W1xp_aps = [WA[:, 0:12288], WB[:, 0:12288]]
        u_tm = WB[:, 12288:13824]
        u_tm = WB[:, 12288:13056]
        u_fm = WB[:, 13056:13824]
        silu_sb = WB[:, 13824:15360].bitcast(F32)

        x0p = [SCR[:, 0:2048].bitcast(F32), SCR[:, 2048:4096].bitcast(F32)]
        astp = [SCR[:, 4096:5120], SCR[:, 5120:6144]]
        afsp = [SCR[:, 6144:7168], SCR[:, 7168:8192]]
        phik = SCR[:, 8192:10240]
        vtm = SCR[:, 10240:12288]
        stg = [SCR[:, 0:4096].bitcast(F32), SCR[:, 4096:8192].bitcast(F32)]
        plainb = SCR[:, 8192:10240]
        phiq = SCR[:, 0:4096]
        attnf = SCR[:, 4096:8192]
        recbc = SCR[:, 8192:9216].bitcast(F32)
        ctmpA = SCR[:, 9216:10240].bitcast(F32)
        ctmpB = SCR[:, 10240:11264].bitcast(F32)
        hst = SCR[:, 11264:12288]
        g1G = SCR[:, 0:8192]
        fstg = SCR[:, 8192:12288].bitcast(F32)
        t0p = [SCR[:, 0:1024], SCR[:, 1024:2048]]
        gws = SCR[:, 2048:2112]
        W2xp_aps = [SCR[:, 0:6144], SCR[:, 6144:12288]]

        pt0 = es.enter_context(nc.psum_tensor("pt0", [128, 1536], F32)).ap()
        pt1 = es.enter_context(nc.psum_tensor("pt1", [128, 1536], F32)).ap()
        pb6 = es.enter_context(nc.psum_tensor("pb6", [128, 512], F32)).ap()
        pb7 = es.enter_context(nc.psum_tensor("pb7", [128, 512], F32)).ap()
        banks = [Bank(pt0[:, 0:512]), Bank(pt0[:, 512:1024]), Bank(pt0[:, 1024:1536]),
                 Bank(pt1[:, 0:512]), Bank(pt1[:, 512:1024]), Bank(pt1[:, 1024:1536]),
                 Bank(pb6), Bank(pb7)]
        trios = [pt0, pt1]

        def mm(out_ap, lhsT, rhs, start, stop, inc=False):
            return ctx.op("tensor",
                          lambda E, o=out_ap, l=lhsT, r=rhs, st=start, sp=stop:
                          E.matmul(o, l, r, start=st, stop=sp),
                          pe if inc else None)

        def ts1(eng, out, in0, s1, op0, sem=None):
            return ctx.op(eng, lambda E, o=out, i=in0, a=s1, p=op0:
                          E.tensor_single_scalar(o, i, a, p), sem)

        def ts2(eng, out, in0, s1, s2, op0, op1, sem=None):
            return ctx.op(eng, lambda E, o=out, i=in0, a=s1, b=s2, p=op0, q=op1:
                          E.tensor_scalar(o, i, a, b, p, q), sem)

        def tt(eng, out, in0, in1, op, sem=None):
            return ctx.op(eng, lambda E, o=out, a=in0, b=in1, p=op:
                          E.tensor_tensor(o, a, b, p), sem)

        def stt(eng, out, in0, scal, in1, op0, op1, sem=None):
            return ctx.op(eng, lambda E, o=out, a=in0, s=scal, b=in1, p=op0, q=op1:
                          E.scalar_tensor_tensor(o, a, s, b, p, q), sem)

        def act(out, in_, func, bias=0.0, scale=1.0, sem=None):
            return ctx.op("scalar", lambda E, o=out, i=in_, f=func, b=bias, s=scale:
                          E.activation(o, i, f, bias=b, scale=s), sem)

        def vcopy(out, in_, sem=None):
            return ctx.op("vector", lambda E, o=out, i=in_: E.tensor_copy(o, i), sem)

        def vred(out, in_, op, sem=None):
            return ctx.op("vector", lambda E, o=out, i=in_, p=op:
                          E.tensor_reduce(o, i, AX.X, p), sem)

        def tp_split(pairs, split=False):
            for i, (dst, srcp) in enumerate(pairs):
                if split and i % 2 == 1:
                    ctx.dma("scalar", dst, srcp, dqt2, transpose=True)
                else:
                    ctx.dma("sync", dst, srcp, dqt, transpose=True)
            return [(dqt, dqt.n), (dqt2, dqt2.n)]

        for dst, src in ((g1c, g1d), (b1c, b1d), (g2c, g2d), (b2c, b2d)):
            ctx.dma("sync", dst[:, :], src.rearrange("(c p) -> p c", p=128), dqx, slow=True)
        ctx.wait("vector", dqx, dqx.n)
        vcopy(b1cb[:, :], b1c[:, :])
        vcopy(b2cb[:, :], b2c[:, :])
        ctx.op("vector", lambda E: E.memset(ones1[:, :], 1.0))
        ctx.op("vector", lambda E: E.memset(onesr[:, :], 1.0))
        ctx.op("vector", lambda E: E.memset(kvsb[:, :], 0.0))
        ctx.op("vector", lambda E: E.memset(kvs[:, :], 0.0))
        ctx.op("vector", lambda E: E.memset(zmat[:, :], 0.0), ve)

        ctx.dma("gpsimd", eyeb[:, :], eyed[:, :], dqw)
        ctx.dma("gpsimd", onehb[:, :], onehd[:, :], dqw)
        stgb = [Buf(stg[0]), Buf(stg[1])]
        vb_banks_q = [banks[0], banks[1]]
        vb_banks_kv = [banks[2], banks[3], banks[4], banks[5]]

        def prep_fold(wsrc, ncols, dst_ap, gcol, vb_banks, bvec):
            for c in range(CH):
                s = stgb[c % 2]
                s.wait_free(ctx, "gpsimd")
                ctx.dma("gpsimd", s.ap[:, 0:ncols], wsrc[c * 128:(c + 1) * 128, :], dqw)
                ctx.wait("vector", dqw, dqw.n)
                ctx.wait("vector", pe)
                ts1("vector", dst_ap[:, c * ncols:(c + 1) * ncols], s.ap[:, 0:ncols],
                    gcol[:, c:c + 1], ALU.mult)
                vcopy(plainb[:, 0:ncols], s.ap[:, 0:ncols], ve)
                s.free = (ve, ve.n)
                ctx.wait("tensor", ve, ve.n)
                for g in range(ncols // 512):
                    b = vb_banks[g]
                    if c == 0:
                        b.acquire(ctx)
                    mm(b.ps[0:1, :], bvec[:, c:c + 1], plainb[:, g * 512:(g + 1) * 512],
                       start=(c == 0), stop=(c == CH - 1), inc=(c == CH - 1))

        prep_fold(wqd, D, wqS, g1c, vb_banks_q, b1cb)
        prep_fold(wkvd, 2 * D, WA, g1c, vb_banks_kv, b1cb)

        ctx.wait("vector", pe)
        for i, b in enumerate(vb_banks_q + vb_banks_kv):
            if i > 0:
                ctx.wait("vector", dqt)
            vcopy(tmpA[0:1, 0:512], b.ps[0:1, :], ve)
            b.release(ve, ve.n)
            ctx.wait("sync", ve, ve.n)
            ctx.dma("sync", vbd[0:1, i * 512:(i + 1) * 512], tmpA[0:1, 0:512], dqt)
        vb_evac = ve.n
        ctx.wait("sync", dqt, dqt.n)
        ctx.dma("sync", vbqc[:, :],
                vbd[0:1, 0:1024].rearrange("a (c p) -> (a p) c", p=128), dqx, slow=True)
        ctx.wait("gpsimd", dqt, dqt.n)
        ctx.dma("gpsimd", vbkvr[:, :], vbd[0:1, 1024:3072], dqw)
        prep_loads = dqx.n
        prep_loads_w = dqw.n
        PHASE_MARKS["prep"] = len(ctx.plan["tensor"])
        prep_pe = pe.n
        prep_ve = ve.n

        NTC = 2
        ABLK = 16
        x0b = [Buf(x0p[0]), Buf(x0p[1]),
               Buf(fstg2[:, 0:1024]), Buf(fstg2[:, 1024:2048])]
        astb = [Buf(astp[0]), Buf(astp[1])]
        afsb = [Buf(afsp[0]), Buf(afsp[1])]
        for bb in x0b + astb + afsb:
            bb.free = (pe, prep_pe)
        kv_banks = [banks[3], banks[4], banks[5]]
        pj_banks = [banks[0], banks[1], banks[2]]
        first_scr = [True]

        for blk in range(ABLK):
            own = blk < 8
            mvc = colA if blk % 2 == 0 else colB
            for tc in range(NTC):
                jj = blk * NTC + tc
                xb = x0b[jj % 4]
                xb.wait_free(ctx, "sync")
                if first_scr[0]:
                    ctx.wait("sync", ve, prep_ve)
                ctx.dma("sync", xb.ap[:, :], xd[jj * 128:(jj + 1) * 128, :], dqx)
                xb.ready = (dqx, dqx.n)
                xb.wait_ready(ctx, "vector")
                if blk >= 2:
                    ctx.wait("vector", se)
                bnc = 4 + tc * 12
                ctx.op("vector", lambda E, x=xb.ap, o=mvc, b=bnc:
                       E.bn_stats(o[:, b:b + 6], x[:, 0:512]))
                ctx.op("vector", lambda E, x=xb.ap, o=mvc, b=bnc:
                       E.bn_stats(o[:, b + 6:b + 12], x[:, 512:1024]), ve)
                ctx.wait("vector", ve)
                ctx.op("vector", lambda E, o=mvc, tc=tc, b=bnc:
                       E.bn_aggr(o[:, tc * 2:tc * 2 + 2], o[:, b:b + 12]), ve)
                ctx.wait("vector", ve)
            m_ = mvc[:, 0:2 * NTC:2]
            v_ = mvc[:, 1:2 * NTC:2]
            t4_ = mvc[:, 28:28 + NTC]
            t1_ = mvc[:, 28 + NTC:28 + 2 * NTC]
            s01_ = mvc[:, 32:32 + NTC]
            rs1_ = mvc[:, 32 + NTC:32 + 2 * NTC]
            nms_ = mvc[:, 36:36 + NTC]
            nm1_ = mvc[:, 38:38 + NTC]
            ts2("vector", t4_, v_, 1.0 + EPS, EPS * EPS, ALU.mult, ALU.add)
            ts1("vector", t1_, v_, EPS, ALU.add, sem=ve)
            ctx.wait("scalar", ve)
            act(mvc[:, 28:28 + 2 * NTC], mvc[:, 28:28 + 2 * NTC], AF.Ln, sem=se)
            ctx.wait("scalar", se)
            act(mvc[:, 32:32 + 2 * NTC], mvc[:, 28:28 + 2 * NTC], AF.Exp,
                scale=-0.5, sem=se)
            ctx.wait("vector", se)
            stt("vector", nms_, m_, -1.0, s01_, ALU.mult, ALU.mult, sem=ve)
            if own:
                stt("vector", nm1_, m_, -1.0, rs1_, ALU.mult, ALU.mult, sem=ve)
            cols_ve = ve.n

            for tc in range(NTC):
                jj = blk * NTC + tc
                xb = x0b[jj % 4]
                ab = astb[jj % 2]
                ab.wait_free(ctx, "scalar")
                ctx.wait("scalar", ve, cols_ve)
                act(ab.ap[:, :], xb.ap[:, :], AF.Identity,
                    bias=nms_[:, tc:tc + 1], scale=s01_[:, tc:tc + 1], sem=se)
                ab.ready = (se, se.n)
                if own:
                    act(RES[:, jj * D:(jj + 1) * D], xb.ap[:, :], AF.Identity,
                        bias=nm1_[:, tc:tc + 1], scale=rs1_[:, tc:tc + 1], sem=se)
                xb.free = (se, se.n)
                if own:
                    def fmdst(c, jj=jj):
                        return FM[:, c * S_OWN + jj * 128:c * S_OWN + (jj + 1) * 128]
                    dst = fmdst
                else:
                    fs = afsb[jj % 2]
                    fs.wait_free(ctx, "sync")

                    def fmdst(c, fs=fs):
                        return fs.ap[:, c * 128:(c + 1) * 128]
                    dst = fmdst
                tb = banks[6 + jj % 2]
                tb.acquire(ctx)
                ab.wait_ready(ctx, "tensor")
                tb_bf = tb.ps.bitcast(BF16)
                for c in range(CH):
                    ctx.op("tensor",
                           lambda E, o=tb_bf[:, c * 128:(c + 1) * 128],
                           i=ab.ap[:, c * 128:(c + 1) * 128]:
                           E.transpose(o, i, eyeb[:, :]), pe if c == CH - 1 else None)
                tpv = pe.n
                ab.free = (pe, tpv)
                ctx.wait("vector", pe, tpv)
                if own:
                    dview = FM.rearrange("p (c t) -> p c t", c=CH)[:, :, jj * 128:(jj + 1) * 128]
                else:
                    dview = afsb[jj % 2].ap.rearrange("p (c t) -> p c t", c=CH)
                vcopy(dview, tb_bf[:, 0:1024].rearrange("p (c t) -> p c t", c=CH), ve)
                tb.release(ve, ve.n)
                tdep = (ve, ve.n)
                if not own:
                    afsb[jj % 2].ready = tdep

                ctx.wait("tensor", tdep[0], tdep[1])
                if first_scr[0]:
                    ctx.wait("tensor", dqx, prep_loads)
                    first_scr[0] = False
                for grp in range(4):
                    b = pj_banks[(tc * 4 + grp) % 3]
                    b.acquire(ctx)
                    if jj == 0 and grp == 0:
                        ctx.wait("tensor", dqw, prep_loads_w)
                    for c in range(CH):
                        mm(b.ps[:, :], dst(c),
                           WA[:, c * 2048 + grp * 512:c * 2048 + (grp + 1) * 512],
                           start=(c == 0), stop=False)
                    mm(b.ps[:, :], onesr[:, :],
                       vbkvr[0:1, grp * 512:(grp + 1) * 512],
                       start=False, stop=True, inc=True)
                    mmv = pe.n
                    ctx.wait("vector", pe, mmv)
                    if grp < 2:
                        ts1("vector", tmpA[:, :], b.ps[:, :], 0.0, ALU.min, sem=ve)
                        ctx.wait("scalar", ve)
                        ctx.wait("scalar", pe, mmv)
                        act(tmpB[:, :], b.ps[:, :], AF.Relu, sem=se)
                        act(tmpA[:, :], tmpA[:, :], AF.Exp, sem=se)
                        ctx.wait("vector", se)
                        tt("vector",
                           phik[:, tc * 1024 + grp * 512:tc * 1024 + (grp + 1) * 512],
                           tmpA[:, :], tmpB[:, :], ALU.add, sem=ve)
                    else:
                        g2_ = grp - 2
                        vcopy(vtm[:, tc * 1024 + g2_ * 512:tc * 1024 + (g2_ + 1) * 512],
                              b.ps[:, :], ve)
                    b.release(ve, ve.n)
            phiv_done = ve.n

            ctx.wait("tensor", ve, phiv_done)
            for pair in range(8):
                    b = kv_banks[pair % 3]
                    b.acquire(ctx)
                    for tc in range(NTC):
                        mm(b.ps[:, 0:128],
                           phik[:, tc * 1024 + pair * 128:tc * 1024 + (pair + 1) * 128],
                           vtm[:, tc * 1024 + pair * 128:tc * 1024 + (pair + 1) * 128],
                           start=(tc == 0), stop=False)
                        mm(b.ps[:, 128:129],
                           phik[:, tc * 1024 + pair * 128:tc * 1024 + (pair + 1) * 128],
                           ones1[:, :], start=False, stop=(tc == NTC - 1),
                           inc=(tc == NTC - 1))
                    ctx.wait("vector", pe, pe.n)
                    tt("vector", kvsb[:, pair * 132:pair * 132 + 129],
                       kvsb[:, pair * 132:pair * 132 + 129], b.ps[:, 0:129],
                       ALU.add, sem=ve)
                    b.release(ve, ve.n)
        PHASE_MARKS["A"] = len(ctx.plan["tensor"])
        phaseA_pe = pe.n
        phaseA_ve = ve.n
        phaseA_se = se.n
        phaseA_tp = dqt.n

        ctx.wait("vector", ve, phaseA_ve)
        for pair in range(8):
            vcopy(kvs[0:64, pair * 128:pair * 128 + 64],
                  kvsb[0:64, pair * 132:pair * 132 + 64])
            vcopy(kvs[64:128, pair * 128 + 64:pair * 128 + 128],
                  kvsb[64:128, pair * 132 + 64:pair * 132 + 128])
            vcopy(zmat[0:64, pair * 16 + 2 * pair:pair * 16 + 2 * pair + 1],
                  kvsb[0:64, pair * 132 + 128:pair * 132 + 129])
            vcopy(zmat[64:128, pair * 16 + 2 * pair + 1:pair * 16 + 2 * pair + 2],
                  kvsb[64:128, pair * 132 + 128:pair * 132 + 129],
                  ve if pair == 7 else None)
        kv_final = ve.n

        if upto == "A":
            _finish(ctx, locals())
            _replay(ctx, nc)
            return nc

        ctx.wait("gpsimd", ve, phaseA_ve)
        ctx.wait("gpsimd", se, phaseA_se)
        ctx.dma("gpsimd", woS.rearrange("p (c n) -> p c n", c=CH),
                wod.rearrange("(c p) n -> p c n", p=128), dqw)
        wo_ld = dqw.n

        recb = Buf(recbc)
        hsb = Buf(hst)
        recb.hs_se = 0
        hsb.hs_se = 0
        pend_hs = [None]
        c_first = [True]
        c_state = {}

        def stage_q(cb):
            tok0 = cb * 512
            for m in range(CH):
                b = banks[m % 3]
                b.acquire(ctx)
                for c in range(CH):
                    mm(b.ps[:, :], wqS[:, c * 1024 + m * 128:c * 1024 + (m + 1) * 128],
                       FM[:, c * S_OWN + tok0:c * S_OWN + tok0 + 512],
                       start=(c == 0), stop=(c == CH - 1), inc=(c == CH - 1))
                mmv = pe.n
                ctx.wait("vector", pe, mmv)
                if c_first[0]:
                    ctx.wait("vector", pe, phaseA_pe)
                    ctx.wait("scalar", pe, phaseA_pe)
                    ctx.wait("vector", dqt, phaseA_tp)
                    c_first[0] = False
                ts2("vector", ctmpA[:, :], b.ps[:, :], vbqc[:, m:m + 1], 0.0,
                    ALU.add, ALU.min, sem=ve)
                ctx.wait("scalar", ve)
                ctx.wait("scalar", pe, mmv)
                act(ctmpB[:, :], b.ps[:, :], AF.Relu, bias=vbqc[:, m:m + 1], sem=se)
                act(ctmpA[:, :], ctmpA[:, :], AF.Exp, sem=se)
                ctx.wait("vector", se)
                tt("vector", phiq[:, m * 512:(m + 1) * 512], ctmpA[:, :], ctmpB[:, :],
                   ALU.add, sem=ve)
                b.release(ve, ve.n)
                bden = banks[6]
                if m == 0:
                    bden.acquire(ctx)
                    ctx.wait("tensor", ve, kv_final)
                ctx.wait("tensor", ve, ve.n)
                mm(bden.ps[0:16, :], zmat[:, m * 16:(m + 1) * 16],
                   phiq[:, m * 512:(m + 1) * 512],
                   start=(m == 0), stop=(m == CH - 1), inc=(m == CH - 1))
            phiq_done = ve.n
            c_state[cb] = phiq_done

        def stage_attn(cb):
            phiq_done = c_state[cb]
            bden = banks[6]

            denv = pe.n
            num_banks = [banks[7], banks[0], banks[1], banks[2], banks[3],
                         banks[4], banks[5], banks[7]]
            for pair in range(7):
                bnum = num_banks[pair]
                bnum.acquire(ctx)
                mm(bnum.ps[:, :], kvs[:, pair * 128:(pair + 1) * 128],
                   phiq[:, pair * 512:(pair + 1) * 512], start=True, stop=True,
                   inc=(pair == 6))
            numv = pe.n
            ctx.wait("vector", pe, denv)
            ts1("vector", recsb[:, :], bden.ps[0:16, :], 1e-6, ALU.add, sem=ve)
            ctx.wait("vector", ve)
            recb.wait_free(ctx, "vector")
            ctx.op("vector", lambda E: E.reciprocal(recbc[0:16, :], recsb[:, :]), ve)
            ctx.wait("vector", ve)
            vcopy(rechi[:, :], recbc[0:16, :])
            tt("vector", reclo[:, :], recbc[0:16, :], rechi[:, :], ALU.subtract, sem=ve)
            recv = ve.n
            bden.release(ve, recv)
            ctx.wait("tensor", ve, recv)
            for pair in range(8):
                brc = bden
                brc.acquire(ctx)
                mm(brc.ps[:, :], onehb[:, pair * 128:(pair + 1) * 128], rechi[:, :],
                   start=True, stop=False)
                mm(brc.ps[:, :], onehb[:, pair * 128:(pair + 1) * 128], reclo[:, :],
                   start=False, stop=True, inc=True)
                rbv = pe.n
                bnum = num_banks[pair]
                if pair == 7:
                    bnum.acquire(ctx)
                    mm(bnum.ps[:, :], kvs[:, pair * 128:(pair + 1) * 128],
                       phiq[:, pair * 512:(pair + 1) * 512], start=True, stop=True,
                       inc=True)
                    ctx.wait("vector", pe, pe.n)
                else:
                    ctx.wait("vector", pe, rbv)
                recb.wait_free(ctx, "vector")
                vcopy(recbc[:, :], brc.ps[:, :], ve)
                brc.release(ve, ve.n)
                tt("vector", attnf[:, pair * 512:(pair + 1) * 512],
                   bnum.ps[:, :], recbc[:, :], ALU.mult, sem=ve)
                bnum.release(ve, ve.n)
                recb.free = (ve, ve.n)
            attn_done = ve.n
            c_state[cb] = attn_done

        def stage_wo(cb):
            attn_done = c_state[cb]
            ctx.wait("tensor", dqw, wo_ld)
            ctx.wait("tensor", ve, attn_done)
            for tc in range(4):
                jj = cb * 4 + tc
                if pend_hs[0] is not None:
                    pj, pbuf = pend_hs[0]
                    tbk = banks[2 + pj % 2]
                    tbk.acquire(ctx)
                    ctx.wait("tensor", se, pbuf.hs_se)
                    tb_bf = tbk.ps.bitcast(BF16)
                    pap = pbuf.ap.bitcast(BF16) if pbuf is recb else pbuf.ap
                    for c in range(CH):
                        ctx.op("tensor",
                               lambda E, o=tb_bf[:, c * 128:(c + 1) * 128],
                               i=pap[:, c * 128:(c + 1) * 128]:
                               E.transpose(o, i, eyeb[:, :]), pe if c == CH - 1 else None)
                    tpv = pe.n
                    pbuf.free = (pe, tpv)
                    ctx.wait("vector", pe, tpv)
                    for c in range(CH):
                        ts1("vector", FM[:, c * S_OWN + pj * 128:c * S_OWN + (pj + 1) * 128],
                            tb_bf[:, c * 128:(c + 1) * 128], g2c[:, c:c + 1], ALU.mult,
                            sem=ve if c == CH - 1 else None)
                    tbk.release(ve, ve.n)
                    pend_hs[0] = None
                for grp in range(2):
                    b = pj_banks[grp]
                    b.acquire(ctx)
                    for c in range(CH):
                        mm(b.ps[:, :],
                           attnf[:, c * 512 + tc * 128:c * 512 + (tc + 1) * 128],
                           woS[:, c * 1024 + grp * 512:c * 1024 + (grp + 1) * 512],
                           start=(c == 0), stop=(c == CH - 1), inc=(c == CH - 1))
                    mmv = pe.n
                    ctx.wait("vector", pe, mmv)
                    rsl = RES[:, jj * D + grp * 512:jj * D + (grp + 1) * 512]
                    tt("vector", rsl, b.ps[:, :], rsl, ALU.add, sem=ve)
                    b.release(ve, ve.n)
                mvc = colA if jj % 2 == 0 else colB
                rtile = RES[:, jj * D:(jj + 1) * D]
                ctx.wait("vector", ve)
                ctx.wait("vector", se)
                ctx.op("vector", lambda E, o=mvc, r=rtile: E.bn_stats(o[:, 4:10], r[:, 0:512]))
                ctx.op("vector", lambda E, o=mvc, r=rtile: E.bn_stats(o[:, 10:16], r[:, 512:1024]), ve)
                ctx.wait("vector", ve)
                ctx.op("vector", lambda E, o=mvc: E.bn_aggr(o[:, 0:2], o[:, 4:16]), ve)
                ctx.wait("vector", ve)
                ts1("vector", mvc[:, 2:3], mvc[:, 1:2], EPS, ALU.add, sem=ve)
                ctx.wait("scalar", ve)
                act(mvc[:, 2:3], mvc[:, 2:3], AF.Ln, sem=se)
                ctx.wait("scalar", se)
                act(mvc[:, 3:4], mvc[:, 2:3], AF.Exp, scale=-0.5, sem=se)
                ctx.wait("vector", se)
                stt("vector", mvc[:, 2:3], mvc[:, 0:1], -1.0, mvc[:, 3:4],
                    ALU.mult, ALU.mult, sem=ve)
                hbuf = hsb if jj % 2 == 0 else recb
                hbuf.wait_free(ctx, "scalar")
                ctx.wait("scalar", ve)
                hap = hbuf.ap.bitcast(BF16) if hbuf is recb else hbuf.ap
                act(hap[:, :], rtile, AF.Identity, bias=mvc[:, 2:3],
                    scale=mvc[:, 3:4], sem=se)
                hbuf.hs_se = se.n
                pend_hs[0] = (jj, hbuf)
            pj, pbuf = pend_hs[0]
            tbk = banks[2 + pj % 2]
            tbk.acquire(ctx)
            ctx.wait("tensor", se, pbuf.hs_se)
            tb_bf = tbk.ps.bitcast(BF16)
            pap = pbuf.ap.bitcast(BF16) if pbuf is recb else pbuf.ap
            for c in range(CH):
                ctx.op("tensor",
                       lambda E, o=tb_bf[:, c * 128:(c + 1) * 128],
                       i=pap[:, c * 128:(c + 1) * 128]:
                       E.transpose(o, i, eyeb[:, :]), pe if c == CH - 1 else None)
            tpv = pe.n
            pbuf.free = (pe, tpv)
            ctx.wait("vector", pe, tpv)
            for c in range(CH):
                ts1("vector", FM[:, c * S_OWN + pj * 128:c * S_OWN + (pj + 1) * 128],
                    tb_bf[:, c * 128:(c + 1) * 128], g2c[:, c:c + 1], ALU.mult,
                    sem=ve if c == CH - 1 else None)
            tbk.release(ve, ve.n)
            pend_hs[0] = None

        w1g0_v = [0]
        w1g0_next = [0]

        def w1g0_chunk():
            c = w1g0_next[0]
            if c >= CH:
                return
            w1g0_next[0] += 1
            ctx.wait("gpsimd", pe, pe.n)
            ctx.dma("gpsimd", WA[:, c * 2048:(c + 1) * 2048],
                    w1d[c * 128:(c + 1) * 128, 0:2048], dqw)
            if c == CH - 1:
                w1g0_v[0] = dqw.n

        def load_w2_quarter(G, qc):
            ctx.wait("gpsimd", pe, pe.n)
            ctx.dma("gpsimd",
                    w2G[:, qc * 4096:(qc + 1) * 4096].rearrange("p (c n) -> p c n", c=4),
                    w2d[G * 2048 + qc * 512:G * 2048 + (qc + 1) * 512, :]
                    .rearrange("(c p) n -> p c n", p=128), dqw)
            return dqw.n

        stage_q(0)
        w1g0_chunk()
        stage_attn(0)
        w1g0_chunk()
        for cb in range(4):
            if cb < 3:
                stage_q(cb + 1)
                w1g0_chunk()
                if cb == 2:
                    load_w2_quarter(0, 0)
                    load_w2_quarter(0, 1)
            stage_wo(cb)
            w1g0_chunk()
            if cb < 3:
                stage_attn(cb + 1)
                w1g0_chunk()
        w2v_g0 = max(load_w2_quarter(0, 2),
                     load_w2_quarter(0, 3))

        PHASE_MARKS["C"] = len(ctx.plan["tensor"])
        hs_tp = ve.n
        hs_tp2 = 0
        phaseC_pe = pe.n
        phaseC_ve = ve.n
        phaseC_se = se.n

        if upto == "C":
            _finish(ctx, locals())
            _replay(ctx, nc)
            return nc

        fstgb = Buf(fstg)
        fstgb.free = (ve, phaseC_ve)
        vb1_banks = [banks[4], banks[5], banks[6], banks[7]]
        f_first = [True]

        def vb_w1_mms(G, w1v):
            ctx.wait("tensor", dqw, w1v)
            for c in range(CH):
                for g in range(4):
                    b = vb1_banks[g]
                    if c == 0:
                        b.acquire(ctx)
                    mm(b.ps[0:1, :], b2cb[:, c:c + 1],
                       WA[:, c * 2048 + g * 512:c * 2048 + (g + 1) * 512],
                       start=(c == 0), stop=(c == CH - 1), inc=(c == CH - 1))
            ctx.wait("vector", pe)
            for g in range(4):
                if g > 0:
                    ctx.wait("vector", dqt)
                vcopy(recsb[0:1, 0:512], vb1_banks[g].ps[0:1, :], ve)
                vb1_banks[g].release(ve, ve.n)
                ctx.wait("sync", ve, ve.n)
                ctx.dma("sync",
                        vbd[0:1, 3072 + G * 2048 + g * 512:3072 + G * 2048 + (g + 1) * 512],
                        recsb[0:1, 0:512], dqt)
            ctx.wait("sync", dqt, dqt.n)
            ctx.dma("sync", vbw1c[:, G * 16:(G + 1) * 16],
                    vbd[0:1, 3072 + G * 2048:3072 + (G + 1) * 2048]
                    .rearrange("a (c p) -> (a p) c", p=128), dqx, slow=True)
            ctx.wait("vector", dqx, dqx.n)
            ts1("vector", vbw1s[:, G * 16:(G + 1) * 16], vbw1c[:, G * 16:(G + 1) * 16],
                10.0, ALU.mult, sem=ve)
            return ve.n

        ftmp = [fstg[:, 0:512], fstg[:, 512:1024]]

        x0db = [Buf(fstg2[:, 0:1024]), Buf(fstg2[:, 1024:2048])]
        for bb in x0db:
            bb.free = [(se, se.n), (ve, ve.n)]
        gw_ld = [0]
        d0_pend = []

        def d0_tile(jj):
            xb = x0db[jj % 2]
            xb.wait_free(ctx, "sync")
            ctx.wait("sync", ve, ve.n)
            ctx.dma("sync", xb.ap[:, :], xd[jj * 128:(jj + 1) * 128, :], dqx)
            xv = dqx.n
            ctx.wait("vector", dqx, xv)
            rtile = RES[:, jj * D:(jj + 1) * D]
            tt("vector", rtile, rtile, xb.ap[:, :], ALU.add, sem=ve)
            mvc = colA if jj % 2 == 0 else colB
            ctx.wait("vector", se)
            ctx.op("vector", lambda E, o=mvc, r=rtile: E.bn_stats(o[:, 4:10], r[:, 0:512]))
            ctx.op("vector", lambda E, o=mvc, r=rtile: E.bn_stats(o[:, 10:16], r[:, 512:1024]), ve)
            ctx.wait("vector", ve)
            ctx.op("vector", lambda E, o=mvc: E.bn_aggr(o[:, 0:2], o[:, 4:16]), ve)
            ctx.wait("vector", ve)
            ts1("vector", mvc[:, 2:3], mvc[:, 1:2], EPS, ALU.add, sem=ve)
            ctx.wait("scalar", ve)
            act(mvc[:, 2:3], mvc[:, 2:3], AF.Ln, sem=se)
            ctx.wait("scalar", se)
            act(mvc[:, 3:4], mvc[:, 2:3], AF.Exp, scale=-0.5, sem=se)
            ctx.wait("vector", se)
            stt("vector", mvc[:, 2:3], mvc[:, 0:1], -1.0, mvc[:, 3:4],
                ALU.mult, ALU.mult, sem=ve)
            t0ap = xb.ap.bitcast(BF16)[:, 0:1024]
            ctx.wait("scalar", ve)
            act(t0ap, rtile, AF.Identity, bias=mvc[:, 2:3], scale=mvc[:, 3:4], sem=se)
            d0_pend.append((jj, xb, t0ap, se.n))

        def d0_pe():
            jj, xb, t0ap, t0se = d0_pend.pop(0)
            tbk = banks[6 + jj % 2]
            tbk.acquire(ctx)
            ctx.wait("tensor", se, t0se)
            tb_bf = tbk.ps.bitcast(BF16)
            for c in range(CH):
                ctx.op("tensor",
                       lambda E, o=tb_bf[:, c * 128:(c + 1) * 128],
                       i=t0ap[:, c * 128:(c + 1) * 128]:
                       E.transpose(o, i, eyeb[:, :]), pe if c == CH - 1 else None)
            tpv = pe.n
            xb.free = (pe, tpv)
            ctx.wait("vector", pe, tpv)
            dview = FM.rearrange("p (c t) -> p c t", c=CH)[:, :, jj * 128:(jj + 1) * 128]
            vcopy(dview, tb_bf[:, 0:1024].rearrange("p (c t) -> p c t", c=CH), ve)
            t0fm_ve = ve.n
            tbk.release(ve, t0fm_ve)
            bg = banks[6 + jj % 2]
            bg.acquire(ctx)
            ctx.wait("tensor", ve, t0fm_ve)
            ctx.wait("tensor", dqw, gw_ld[0])
            for c in range(CH):
                mm(bg.ps[:, 0:8],
                   FM[:, c * S_OWN + jj * 128:c * S_OWN + (jj + 1) * 128],
                   gwsb[:, c * 8:(c + 1) * 8],
                   start=(c == 0), stop=(c == CH - 1), inc=(c == CH - 1))
            ctx.wait("vector", pe, pe.n)
            ctx.wait("vector", se)
            vcopy(scsb[:, 0:8], bg.ps[:, 0:8], ve)
            bg.release(ve, ve.n)
            ctx.wait("vector", ve)
            vred(sccol[:, 0:1], scsb[:, 0:8], ALU.max, sem=ve)
            ctx.wait("vector", ve)
            ts1("vector", sccol[:, 1:2], sccol[:, 0:1], -1.0, ALU.mult, sem=ve)
            ctx.wait("scalar", ve)
            act(scsb[:, 8:16], scsb[:, 0:8], AF.Exp, bias=sccol[:, 1:2], sem=se)
            ctx.wait("vector", se)
            vred(sccol[:, 2:3], scsb[:, 8:16], ALU.add, sem=ve)
            ctx.wait("vector", ve)
            ctx.op("vector", lambda E: E.reciprocal(sccol[:, 3:4], sccol[:, 2:3]), ve)
            ctx.wait("vector", ve)
            ts1("vector", scsb[:, 0:8], scsb[:, 8:16], sccol[:, 3:4], ALU.mult, sem=ve)
            ctx.wait("vector", ve)
            vred(sccol[:, 4:5], scsb[:, 0:8], ALU.max, sem=ve)
            ctx.wait("vector", ve)
            ts1("vector", scsb[:, 8:16], scsb[:, 0:8], sccol[:, 4:5], ALU.is_ge, sem=ve)
            ctx.wait("vector", ve)
            stt("vector", scsb[:, 8:16], scsb[:, 8:16], -1e9, scsb[:, 0:8],
                ALU.mult, ALU.add, sem=ve)
            ctx.wait("vector", ve)
            vred(sccol[:, 5:6], scsb[:, 8:16], ALU.max, sem=ve)
            ctx.wait("vector", ve)
            ts1("vector", scsb[:, 8:16], scsb[:, 0:8], sccol[:, 5:6], ALU.is_ge, sem=ve)
            ctx.wait("vector", ve)
            tt("vector", wdsb[:, jj * 8:(jj + 1) * 8], scsb[:, 8:16], scsb[:, 0:8],
               ALU.mult, sem=ve)

        w1g1_v = [0]
        w1g1_next = [0]

        def w1g1_chunk():
            c = w1g1_next[0]
            if c >= CH:
                return
            w1g1_next[0] += 1
            ctx.wait("gpsimd", pe, pe.n)
            ctx.dma("gpsimd", WA[:, c * 2048:(c + 1) * 2048],
                    w1d[c * 128:(c + 1) * 128, 2048:4096], dqw)
            if c == CH - 1:
                w1g1_v[0] = dqw.n

        for G in range(2):
            if G == 0:
                w2v = w2v_g0
                vbv = vb_w1_mms(0, w1g0_v[0])
            else:
                for qc in range(4):
                    w2v = load_w2_quarter(1, qc)
                vbv = vb_w1_mms(1, w1g1_v[0])
                ctx.dma("gpsimd", gwsb[:, :].rearrange("p (c n) -> p c n", c=CH),
                        gwd.rearrange("(c p) n -> p c n", p=128), dqw)
                gw_ld[0] = dqw.n
            for fb in range(4):
                tok0 = fb * 512
                w1_banks = [banks[0], banks[1], banks[4]]
                g1_se = []
                for cc in range(16):
                    b = w1_banks[cc % 3]
                    b.acquire(ctx)
                    if G == 0 and fb == 0 and cc == 0:
                        ctx.wait("tensor", ve, hs_tp)
                    for c in range(CH):
                        mm(b.ps[:, :],
                           WA[:, c * 2048 + cc * 128:c * 2048 + (cc + 1) * 128],
                           FM[:, c * S_OWN + tok0:c * S_OWN + tok0 + 512],
                           start=(c == 0), stop=(c == CH - 1), inc=(c == CH - 1))
                    mmv = pe.n
                    tmp = ftmp[cc % 2]
                    ctx.wait("scalar", pe, mmv)
                    ctx.wait("scalar", ve, vbv)
                    ctx.wait("scalar", ve)
                    act(tmp, b.ps[:, :], AF.Identity, scale=10.0,
                        bias=vbw1s[:, G * 16 + cc:G * 16 + cc + 1], sem=se)
                    b.release(se, se.n)
                    ctx.wait("vector", se)
                    ts2("vector", tmp, tmp, RC, -RC, ALU.add, ALU.add, sem=ve)
                    ctx.wait("vector", ve)
                    ts2("vector", tmp, tmp, 127.0, -128.0, ALU.min, ALU.max, sem=ve)
                    ctx.wait("scalar", ve)
                    act(g1G[:, cc * 512:(cc + 1) * 512], tmp,
                        AF.Identity if DBG_FFN_IDENT[0] else AF.Gelu_apprx_tanh,
                        scale=0.1, sem=se)
                    g1_se.append(se.n)
                ctx.wait("tensor", dqw, w2v)
                for tc in range(4):
                    jj = fb * 4 + tc
                    for grp in range(2):
                        b = [banks[2], banks[3], banks[5]][(tc * 2 + grp) % 3]
                        b.acquire(ctx)
                        for cc in range(16):
                            ctx.wait("tensor", se, g1_se[cc])
                            mm(b.ps[:, :],
                               g1G[:, cc * 512 + tc * 128:cc * 512 + (tc + 1) * 128],
                               w2G[:, cc * 1024 + grp * 512:cc * 1024 + (grp + 1) * 512],
                               start=(cc == 0), stop=(cc == 15), inc=(cc == 15))
                        ctx.wait("vector", pe, pe.n)
                        rsl = RES[:, jj * D + grp * 512:jj * D + (grp + 1) * 512]
                        tt("vector", rsl, b.ps[:, :], rsl, ALU.add, sem=ve)
                        b.release(ve, ve.n)
                    if G == 0 and fb == 3:
                        w1g1_chunk()
                        w1g1_chunk()
                    if len(d0_pend) >= 2:
                        d0_pe()
                    if G == 1:
                        d0_tile(jj)
        ffn_pe = pe.n
        ffn_ve = ve.n

        PHASE_MARKS["F"] = len(ctx.plan["tensor"])
        while d0_pend:
            d0_pe()
        wd_ve = ve.n
        x1_ve = ve.n

        if upto == "F":
            _finish(ctx, locals())
            _replay(ctx, nc)
            return nc

        u_tms = [WB[:, 12288:13056], WB[:, 13824:14592]]
        u_fms = [WB[:, 13056:13824], WB[:, 14592:15360]]
        silu_bf = WB[:, 15360:16128]
        W1b = [Buf(W1xp_aps[0]), Buf(W1xp_aps[1])]
        W2b = [Buf(W2xp_aps[0]), Buf(W2xp_aps[1])]
        W1b[0].free = (pe, ffn_pe)
        W1b[1].free = [(ve, x1_ve), (pe, ffn_pe)]
        W2b[0].free = (ve, wd_ve)
        W2b[1].free = (ve, wd_ve)
        u_tmbs = [Buf(u_tms[0]), Buf(u_tms[1])]
        u_fmbs = [Buf(u_fms[0]), Buf(u_fms[1])]
        for bb in u_tmbs + u_fmbs:
            bb.free = [(ve, x1_ve), (pe, ffn_pe)]
        h_banks = [[banks[0], banks[1], banks[2]], [banks[3], banks[4], banks[5]]]
        eo_banks = [banks[6], banks[7]]

        for e0, wbx, dram, nch in ((0, W1b[0], W1xd, CH), (0, W2b[0], W2xd, 6)):
            wbx.wait_free(ctx, "gpsimd")
            ctx.dma("gpsimd", wbx.ap.rearrange("p (c n) -> p c n", c=nch),
                    dram[0].rearrange("(c p) n -> p c n", p=128), dqw)
            wbx.ready = (dqw, dqw.n)

        def prefetch_chunk(e1, which, c):
            if e1 >= NEXP:
                return
            wbx = (W1b if which == 1 else W2b)[e1 % 2]
            dram = W1xd if which == 1 else W2xd
            ncols = 1536 if which == 1 else 1024
            if c == 0:
                wbx.wait_free(ctx, "gpsimd")
            ctx.wait("gpsimd", pe, pe.n)
            ctx.dma("gpsimd", wbx.ap[:, c * ncols:(c + 1) * ncols],
                    dram[e1, c * 128:(c + 1) * 128, :], dqw)
            nmax = CH if which == 1 else 6
            if c == nmax - 1:
                wbx.ready = (dqw, dqw.n)

        for e in range(NEXP):
            wb1 = W1b[e % 2]
            wb2 = W2b[e % 2]
            wb1.wait_ready(ctx, "tensor")
            ctx.wait("tensor", ve, wd_ve)
            last_h = [None]
            last_eo = [None]
            pend = []

            def emit_eo(tca, ui):
                fb_ = u_fmbs[ui]
                fb_.wait_ready(ctx, "tensor")
                wb2.wait_ready(ctx, "tensor")
                for grp in range(2):
                    b = eo_banks[grp]
                    b.acquire(ctx)
                    for cc in range(6):
                        mm(b.ps[:, :], u_fms[ui][:, cc * 128:(cc + 1) * 128],
                           wb2.ap[:, cc * 1024 + grp * 512:cc * 1024 + (grp + 1) * 512],
                           start=(cc == 0), stop=(cc == 5), inc=(cc == 5))
                    mmv2 = pe.n
                    last_eo[0] = mmv2
                    ctx.wait("vector", pe, mmv2)
                    rsl = RES[:, tca * D + grp * 512:tca * D + (grp + 1) * 512]
                    tt("vector", rsl, b.ps[:, :], rsl, ALU.add, sem=ve)
                    b.release(ve, ve.n)
                fb_.free = (pe, last_eo[0])
                if e == NEXP - 1:
                    ctx.wait("sync", ve, ve.n)
                    ctx.dma("sync", outd[tca * 128:(tca + 1) * 128, :],
                            RES[:, tca * D:(tca + 1) * D], dqo)

            for tca in range(16):
                upi = tca % 2
                hb = h_banks[tca % 2]
                trio = trios[tca % 2]
                gate_pe = None
                for ng in (1, 2, 0):
                    b = hb[ng]
                    b.acquire(ctx)
                    for c in range(CH):
                        mm(b.ps[:, :],
                           FM[:, c * S_OWN + tca * 128:c * S_OWN + (tca + 1) * 128],
                           wb1.ap[:, c * 1536 + ng * 512:c * 1536 + (ng + 1) * 512],
                           start=(c == 0), stop=(c == CH - 1), inc=(c == CH - 1))
                    if ng == 2:
                        gate_pe = pe.n
                mmv = pe.n
                last_h[0] = mmv
                ctx.wait("scalar", pe, gate_pe)
                ctx.wait("scalar", ve)
                act(silu_bf[:, 0:768], trio[:, 768:1536], AF.Silu, sem=se)
                sv = se.n
                ub = u_tmbs[upi]
                ub.wait_free(ctx, "vector")
                ctx.wait("vector", se, sv)
                ctx.wait("vector", pe, mmv)
                ut = u_tms[upi]
                if e < 8:
                    wdcol = wdsb[:, tca * 8 + e:tca * 8 + e + 1]
                    stt("vector", ut[:, 0:768], trio[:, 0:768], wdcol,
                        silu_bf[:, 0:768], ALU.mult, ALU.mult, sem=ve)
                else:
                    tt("vector", ut[:, 0:768], trio[:, 0:768],
                       silu_bf[:, 0:768], ALU.mult, sem=ve)
                uv = ve.n
                ub.ready = (ve, uv)
                hb[1].release(ve, uv)
                hb[2].release(ve, uv)
                if len(pend) >= 2:
                    tca_, ui_ = pend.pop(0)
                    emit_eo(tca_, ui_)
                fb_ = u_fmbs[upi]
                ctx.wait("tensor", ve, uv)
                tb_bf = hb[0].ps.bitcast(BF16)
                for c in range(6):
                    ctx.op("tensor",
                           lambda E, o=tb_bf[:, c * 128:(c + 1) * 128],
                           i=ut[:, c * 128:(c + 1) * 128]:
                           E.transpose(o, i, eyeb[:, :]), pe if c == 5 else None)
                tpv = pe.n
                ub.free = (pe, tpv)
                fb_.wait_free(ctx, "scalar")
                ctx.wait("scalar", pe, tpv)
                act(u_fms[upi][:, 0:768], tb_bf[:, 0:768], AF.Identity, sem=se)
                hb[0].release(se, se.n)
                fb_.ready = (se, se.n)
                pend.append((tca, upi))
                if tca % 2 == 1:
                    prefetch_chunk(e + 1, 1, tca // 2)
                elif 2 <= tca <= 12:
                    prefetch_chunk(e + 1, 2, tca // 2 - 1)
            while pend:
                tca_, ui_ = pend.pop(0)
                emit_eo(tca_, ui_)
            wb1.free = (pe, last_h[0])
            wb2.free = (pe, last_eo[0])
        streamed_out = True
        _finish(ctx, locals())
        _replay(ctx, nc)
        return nc


def _finish(ctx, env):
    pe, ve, se = env["pe"], env["ve"], env["se"]
    dqo, dqt, dqx, dqw = env["dqo"], env["dqt"], env["dqx"], env["dqw"]
    RES, outd = env["RES"], env["outd"]
    for s in (pe, ve, se, dqt, dqx, dqw):
        ctx.wait("scalar", s)
    tap_aps = {
        "as_fm": env["FM"], "hsfm": env["FM"], "t0fm": env["FM"],
        "kvsb": env["kvsb"], "res": env["RES"], "phik": env["phik"],
        "vtm": env["vtm"], "phiq": env["phiq"], "attn": env["attnf"],
        "rec": env["recsb"], "wd": env["wdsb"], "vbq": env["vbqc"],
        "vbkv": env["VBKV"], "wkvs": env["WA"], "wqs": env["wqS"],
        "g1g": env["g1G"], "zmat": env["zmat"], "kvsq": env["kvs"],
    }
    nc = ctx.nc
    for t in env["taps"]:
        src = tap_aps[t]
        shp = [src.shape[0], src.shape[1]]
        dt = src.dtype
        td = nc.dram_tensor("tap_" + t, shp, dt, kind="ExternalOutput").ap()
        ctx.dma("scalar", td[:, :], src, dqo)
    if not env.get("streamed_out"):
        for jj in range(16):
            ctx.dma("scalar", outd[jj * 128:(jj + 1) * 128, :],
                    RES[:, jj * 1024:(jj + 1) * 1024], dqo)
    ctx.wait("scalar", dqo)
    ctx.wait("sync", dqo)


def _replay(ctx, nc):
    with nc.Block() as block:
        block.sync(lambda E: [f(E) for f in ctx.plan["sync"]])
        block.scalar(lambda E: [f(E) for f in ctx.plan["scalar"]])
        block.vector(lambda E: [f(E) for f in ctx.plan["vector"]])
        block.tensor(lambda E: [f(E) for f in ctx.plan["tensor"]])
        block.gpsimd(lambda E: [f(E) for f in ctx.plan["gpsimd"]])


def _host_layout(inputs):
    x = np.asarray(inputs["L0_attn_x"], np.float32)
    wq = np.ascontiguousarray(np.asarray(inputs["L0_attn_wq"], np.float32))
    wk = np.asarray(inputs["L0_attn_wk"], np.float32)
    wv = np.asarray(inputs["L0_attn_wv"], np.float32)
    wkv = np.ascontiguousarray(np.concatenate([wk, wv], axis=1))
    wo = np.ascontiguousarray(np.asarray(inputs["L0_attn_wo"], np.float32))
    w1 = np.ascontiguousarray(np.asarray(inputs["L0_attn_w1"], np.float32))
    w2 = np.ascontiguousarray(np.asarray(inputs["L0_attn_w2"], np.float32))
    gw = np.ascontiguousarray(np.asarray(inputs["L0_ffn_gate_w"], np.float32).T)
    W1 = np.asarray(inputs["L0_ffn_W1"], np.float32)
    W2 = np.asarray(inputs["L0_ffn_W2"], np.float32)
    Ws1 = np.asarray(inputs["L0_ffn_Ws1"], np.float32)
    Ws2 = np.asarray(inputs["L0_ffn_Ws2"], np.float32)
    W1t = W1.transpose(0, 2, 1)
    W2t = W2.transpose(0, 2, 1)
    Ws1t = Ws1.T
    Ws2t = Ws2.T
    W1x = np.zeros((NEXP, D, 1536), np.float32)
    W2x = np.zeros((NEXP, 768, D), np.float32)
    W1x[:8] = W1t
    W2x[:8] = W2t
    W1x[8] = np.concatenate([Ws1t[:, 0:768], Ws1t[:, 1536:2304]], axis=1)
    W1x[9] = np.concatenate([Ws1t[:, 768:1536], Ws1t[:, 2304:3072]], axis=1)
    W2x[8] = Ws2t[0:768]
    W2x[9] = Ws2t[768:1536]
    oneh = np.zeros((16, 1024), np.float32)
    for p_ in range(8):
        oneh[2 * p_, p_ * 128:p_ * 128 + 64] = 1.0
        oneh[2 * p_ + 1, p_ * 128 + 64:p_ * 128 + 128] = 1.0
    shared = dict(
        eye=np.eye(128, dtype=np.float32),
        oneh=oneh,
        wq=wq, wkv=wkv, wo=wo, w1=w1, w2=w2, gw=gw,
        W1x=np.ascontiguousarray(W1x), W2x=np.ascontiguousarray(W2x),
        g1=np.ascontiguousarray(np.asarray(inputs["L0_attn_ln1_g"], np.float32)),
        b1=np.ascontiguousarray(np.asarray(inputs["L0_attn_ln1_b"], np.float32)),
        g2=np.ascontiguousarray(np.asarray(inputs["L0_attn_ln2_g"], np.float32)),
        b2=np.ascontiguousarray(np.asarray(inputs["L0_attn_ln2_b"], np.float32)),
    )
    in_maps = []
    for c in range(8):
        b, h = c // 2, c % 2
        own = x[b, h * 2048:(h + 1) * 2048]
        oth = x[b, (1 - h) * 2048:(1 - h) * 2048 + 2048]
        m = dict(shared)
        m["x"] = np.ascontiguousarray(np.concatenate([own, oth], axis=0))
        in_maps.append(m)
    return in_maps


def kernel(**inputs):
    in_maps = _host_layout(inputs)
    nc = build_program()
    res = run_bass_kernel_spmd(nc, in_maps, core_ids=list(range(8)))
    out = np.zeros((4, 4096, 1024), np.float32)
    for c in range(8):
        b, h = c // 2, c % 2
        out[b, h * 2048:(h + 1) * 2048] = res.results[c]["out"]
    return out
